# version 21
# baseline (speedup 1.0000x reference)
import numpy as np
import contextlib
import concourse.bass as bass
import concourse.mybir as mybir
from concourse.bass_utils import run_bass_kernel_spmd

F32 = mybir.dt.float32
BF16 = mybir.dt.bfloat16
I32 = mybir.dt.int32
AF = mybir.ActivationFunctionType
ALU = mybir.AluOpType

D = 2048
SEQ = 2048
NH = 16
HD = 128
INTER = 5504
KC = D // 128
IT = INTER // 128
TS = 512
NSLAB = SEQ // TS
CH = 128
NCH = TS // CH
EPS = 1e-5

ENGS = ("sync", "gpsimd", "vector", "scalar", "tensor")
DQUEUES = ("sync", "scalar")

DEBUG_EDGES = False


def build():
    nc = bass.Bass("TRN2", target_bir_lowering=False,
                   detect_race_conditions=DEBUG_EDGES)

    x_in = nc.dram_tensor("x_in", [D, SEQ], F32, kind="ExternalInput").ap()
    wq_d = nc.dram_tensor("wq", [D, D], F32, kind="ExternalInput").ap()
    wk_d = nc.dram_tensor("wk", [D, D], F32, kind="ExternalInput").ap()
    wv_d = nc.dram_tensor("wv", [D, D], F32, kind="ExternalInput").ap()
    wb_d = nc.dram_tensor("wb", [D, D], F32, kind="ExternalInput").ap()
    wo_d = nc.dram_tensor("wo", [D, D], F32, kind="ExternalInput").ap()
    lut_d = nc.dram_tensor("lut", [1, 16], F32, kind="ExternalInput").ap()
    y_out = nc.dram_tensor("y", [D, SEQ], F32, kind="ExternalOutput").ap()

    myidx_d = nc.dram_tensor("myidx", [128, 33024], I32, kind="ExternalInput").ap()
    sc_part = nc.dram_tensor("sc_part", [128, 33024], BF16, kind="Internal").ap()
    sc_all = nc.dram_tensor("sc_all", [8, 128, 33024], BF16, kind="Internal",
                            addr_space="Shared").ap()
    sc_q = nc.dram_tensor("sc_q", [NH, 128, D], BF16, kind="Internal").ap()
    sc_k = nc.dram_tensor("sc_k", [NH, 128, D], BF16, kind="Internal").ap()
    sc_o = nc.dram_tensor("sc_o", [KC, 128, D], BF16, kind="Internal").ap()
    sc_v = nc.dram_tensor("sc_v", [8, 128, KC * 256], BF16, kind="Internal").ap()
    sc_b = nc.dram_tensor("sc_b", [8, 128, KC * 256], BF16, kind="Internal").ap()

    es = contextlib.ExitStack()
    with es:
        sbuf = lambda name, shp, dt: es.enter_context(nc.sbuf_tensor(name, shp, dt))
        psal = lambda name, shp, dt: es.enter_context(nc.psum_tensor(name, shp, dt))

        x_sb = sbuf("x_sb", [128, KC * TS], F32)
        xln = sbuf("xln", [128, KC * TS], BF16)
        pool = sbuf("pool", [128, 24576], BF16)
        h_sl = pool[:, 0:IT * TS]
        q_fm = pool[:, 0:8192]
        k_fm = pool[:, 8192:16384]
        u_tm = pool[:, 16384:24576]
        o_fm = sbuf("o_fm", [128, 8192], BF16)
        S_f32 = sbuf("S_f32", [128, NH * HD], F32)
        S_bf = sbuf("S_bf", [128, NH * HD], BF16)
        wpool = sbuf("wpool", [128, 20480], BF16)
        wg = [wpool[:, 0:2048], wpool[:, 2048:4096]]
        wu = [wpool[:, 4096:6144], wpool[:, 6144:8192]]
        wd = [wpool[:, 8192:8192 + INTER], wpool[:, 8192 + INTER:8192 + 2 * INTER]]
        qkw = [wpool[:, 0:2048], wpool[:, 2048:4096]]
        vbw = [wpool[:, 4096:12288], wpool[:, 12288:20480]]
        sg = sbuf("sg", [128, 2 * TS], BF16)
        sqb = sbuf("sqb", [128, 4 * TS], F32)
        lntmp = sbuf("lntmp", [128, TS], F32)
        ktm = sbuf("ktm", [128, 256], BF16)
        attm = sbuf("attm", [128, 256], BF16)
        stat = sbuf("stat", [1, 6 * TS], F32)
        m_sb = stat[0:1, 0:TS]
        ms_sb = stat[0:1, TS:2 * TS]
        m2_sb = stat[0:1, 2 * TS:3 * TS]
        var_sb = stat[0:1, 3 * TS:4 * TS]
        ln_sb = stat[0:1, 4 * TS:5 * TS]
        rstd_sb = stat[0:1, 5 * TS:6 * TS]
        lut_sb = sbuf("lut_sb", [1, 16], F32)
        lutb = sbuf("lutb", [128, 16], F32)
        ones_c = sbuf("ones_c", [1, 128], F32)
        onesk = sbuf("onesk", [128, 1], F32)
        maskc = sbuf("maskc", [128, 128], BF16)
        identb = sbuf("identb", [128, 128], BF16)
        rawi = sbuf("rawi", [128, 4096], I32)
        bfb = pool[:, 0:4096]
        accb = pool[:, 4096:8192]
        tmpb = pool[:, 8192:10240]
        wtb = pool[:, 10240:14336]
        cwt = pool[:, 14336:18432]
        rawf = x_sb[:, 0:4096]

        ps = [psal(f"ps{i}", [128, 512], F32) for i in range(6)]
        pst = [psal("pst0", [128, 1024], BF16), psal("pst1", [128, 1024], BF16)]

        csem = {e: es.enter_context(nc.semaphore("c_" + e)) for e in ENGS}
        NP = 1 if DEBUG_EDGES else 8
        dpool = {e: [es.enter_context(nc.semaphore(f"d_{e}{i}")) for i in range(NP)]
                 for e in DQUEUES}
        dpc = {e: [0] * NP for e in DQUEUES}
        dseq = {e: 0 for e in DQUEUES}
        cnt = {e: 0 for e in ENGS}
        q = {e: [] for e in ENGS}
        lastw = {e: {} for e in ENGS}

        def ins(e, f, inc=False):
            if DEBUG_EDGES:
                prev = cnt[e]
                h = csem[e]
                if prev:
                    q[e].append(lambda E, h=h, v=prev: E.wait_ge(h, v))
                cnt[e] += 1
                v = cnt[e]
                q[e].append(lambda E, f=f, h=h: f(E).then_inc(h, 1))
                return v
            if inc:
                cnt[e] += 1
                v = cnt[e]
                h = csem[e]
                q[e].append(lambda E, f=f, h=h: f(E).then_inc(h, 1))
                return v
            q[e].append(f)
            return None

        def dma(e, out_ap, in_ap):
            i = dseq[e] % NP
            dseq[e] += 1
            h = dpool[e][i]
            if dpc[e][i]:
                q[e].append(lambda E, h=h, v=dpc[e][i]: E.wait_ge(h, v))
            dpc[e][i] += 16
            v = dpc[e][i]
            q[e].append(lambda E, o=out_ap, i_=in_ap, h=h:
                        E.dma_start(out=o, in_=i_).then_inc(h, 16))
            return (e, i, v)

        def wt(e, kind, oe, v):
            if not v or v <= 0:
                return
            if kind == "c" and oe == e:
                return
            key = (kind, oe)
            if lastw[e].get(key, 0) >= v:
                return
            lastw[e][key] = v
            h = csem[oe]
            q[e].append(lambda E, h=h, v=v: E.wait_ge(h, v))

        def wts(e, v):
            if not v:
                return
            key = ("self",)
            if lastw[e].get(key, 0) >= v:
                return
            lastw[e][key] = v
            q[e].append(lambda E, h=csem[e], v=v: E.wait_ge(h, v))

        def wt_d(e, tokens):
            if tokens is None:
                return
            if isinstance(tokens, tuple):
                tokens = [tokens]
            for (oe, i, v) in tokens:
                key = ("d", oe, i)
                if lastw[e].get(key, 0) >= v:
                    continue
                lastw[e][key] = v
                h = dpool[oe][i]
                q[e].append(lambda E, h=h, v=v: E.wait_ge(h, v))

        def wt_m(e, m):
            if m[0] == "c":
                wt(e, "c", m[1], m[2])
            else:
                wt_d(e, m[1])

        pbank = {i: {"w": ("tensor", 0), "r": {}}
                 for i in list(range(6)) + ["t0", "t1"]}

        def mm_first(bank):
            for oe, v in pbank[bank]["r"].items():
                wt("tensor", "c", oe, v)
            pbank[bank]["r"] = {}

        def mm_last(bank, v):
            pbank[bank]["w"] = ("tensor", v)

        def prd(bank, e, f):
            we, wv = pbank[bank]["w"]
            wt(e, "c", we, wv)
            v = ins(e, f, inc=True)
            pbank[bank]["r"][e] = max(pbank[bank]["r"].get(e, 0), v)
            return v

        ins("gpsimd", lambda E: E.memset(ones_c[:, :], 1.0))
        ins("gpsimd", lambda E: E.memset(onesk[:, :], 1.0 / D))
        g1 = ins("gpsimd", lambda E: E.memset(identb[:, :], 0.0), inc=True)
        q["gpsimd"].append(lambda E, h=csem["gpsimd"], v=g1: E.wait_ge(h, v))
        ins("gpsimd", lambda E: E.affine_select(
            out=identb[:, :], in_=identb[:, :], compare_op=ALU.not_equal,
            fill=1.0, base=0, pattern=[[-1, 128]], channel_multiplier=1))
        g2 = ins("gpsimd", lambda E: E.memset(maskc[:, :], 1.0), inc=True)
        q["gpsimd"].append(lambda E, h=csem["gpsimd"], v=g2: E.wait_ge(h, v))
        ins("gpsimd", lambda E: E.affine_select(
            out=maskc[:, :], in_=maskc[:, :], compare_op=ALU.is_ge,
            fill=0.0, base=0, pattern=[[1, 128]], channel_multiplier=-1))
        ins("gpsimd", lambda E: E.memset(S_f32[:, :], 0.0))
        v_init = ins("gpsimd", lambda E: E.memset(S_bf[:, :], 0.0), inc=True)

        d_lut = dma("sync", lut_sb[:, :], lut_d)
        wt_d("tensor", d_lut)
        wt("tensor", "c", "gpsimd", v_init)
        mm_first(5)
        t_lut = ins("tensor", lambda E: E.matmul(
            ps[5][:, 0:16], ones_c[:, :], lut_sb[:, :], start=True, stop=True), inc=True)
        mm_last(5, t_lut)
        v_lut = prd(5, "vector", lambda E: E.tensor_copy(out=lutb[:, :], in_=ps[5][:, 0:16]))
        q["vector"].append(lambda E, h=csem["vector"], v=v_lut: E.wait_ge(h, v))
        wt("vector", "c", "gpsimd", v_init)

        mat = [(c, 2048) for c in range(16)] + [(16, 256)]
        NMAT = len(mat)

        casts = [(wn, wd_, kc) for wn, wd_ in
                 (("q", wq_d), ("k", wk_d), ("v", wv_d), ("b", wb_d), ("o", wo_d))
                 for kc in range(KC)]
        NCAST = len(casts)

        v_cast = {}
        t_tp = {}
        a_ev = {}
        g_wr = {}
        a_cc = {}
        c_wr = {}
        nbs = [w // 128 for (_, w) in mat]

        idx_ld = {}

        def emit_idx_load(k):
            c, w = mat[k]
            b = k % 2
            if k >= 2:
                wt("sync", "c", "scalar", v_cast[k - 2])
            idx_ld[k] = dma("sync", rawi[:, b * 2048:b * 2048 + w],
                            myidx_d[:, c * 2048:c * 2048 + w])

        emit_idx_load(0)
        emit_idx_load(1)

        def emit_mat(k):
            c, w = mat[k]
            b = k % 2
            nb = nbs[k]
            ld = idx_ld[k]
            wt_d("scalar", ld)
            v_cast[k] = ins("scalar", lambda E, b=b, w=w: E.copy(
                out=bfb[:, b * 2048:b * 2048 + w],
                in_=rawi[:, b * 2048:b * 2048 + w]), inc=True)
            wt("vector", "c", "scalar", v_cast[k])
            if k + 2 < NMAT:
                emit_idx_load(k + 2)
            if k >= 2:
                wt("vector", "c", "tensor", t_tp[(k - 2, nbs[k - 2] - 1)])
            for j16 in range(16):
                sc_ap = lutb[:, j16:j16 + 1]
                if j16 == 0:
                    ins("vector", lambda E, b=b, w=w, jj=j16, s=sc_ap: E.tensor_scalar(
                        out=accb[:, b * 2048:b * 2048 + w],
                        in0=bfb[:, b * 2048:b * 2048 + w],
                        scalar1=float(jj), scalar2=s,
                        op0=ALU.is_equal, op1=ALU.mult))
                else:
                    ins("vector", lambda E, b=b, w=w, jj=j16, s=sc_ap: E.tensor_scalar(
                        out=tmpb[:, 0:w],
                        in0=bfb[:, b * 2048:b * 2048 + w],
                        scalar1=float(jj), scalar2=s,
                        op0=ALU.is_equal, op1=ALU.mult))
                    ins("vector", lambda E, b=b, w=w: E.tensor_add(
                        accb[:, b * 2048:b * 2048 + w],
                        accb[:, b * 2048:b * 2048 + w],
                        tmpb[:, 0:w]))
            v_sel = ins("vector", lambda E: E.engine_nop(), inc=True)
            wt("tensor", "c", "vector", v_sel)
            if k >= 1:
                wt("tensor", "c", "scalar", a_ev[(k - 1, nbs[k - 1] - 1)])
            if k >= 2:
                wt_d("scalar", g_wr[k - 2])
            for j in range(nb):
                slot = j % 2
                if j >= 2:
                    wt("tensor", "c", "scalar", a_ev[(k, j - 2)])
                t_tp[(k, j)] = ins("tensor", lambda E, b=b, j=j, slot=slot: E.transpose(
                    pst[slot][:, 0:128],
                    accb[:, b * 2048 + j * 128:b * 2048 + (j + 1) * 128],
                    identb[:, :]), inc=True)
                wt("scalar", "c", "tensor", t_tp[(k, j)])
                a_ev[(k, j)] = ins("scalar", lambda E, b=b, j=j, slot=slot: E.copy(
                    out=wtb[:, b * 2048 + j * 128:b * 2048 + (j + 1) * 128],
                    in_=pst[slot][:, 0:128]), inc=True)
            wts("scalar", a_ev[(k, nb - 1)])
            g_wr[k] = dma("scalar", sc_part[:, c * 2048:c * 2048 + w],
                          wtb[:, b * 2048:b * 2048 + w])

        def emit_cast(ci):
            wname, wdram, kc = casts[ci]
            b = ci % 2
            if ci >= 2:
                wt("sync", "c", "scalar", a_cc[ci - 2])
            ld = dma("sync", rawf[:, b * 2048:(b + 1) * 2048],
                     wdram[kc * 128:(kc + 1) * 128, :])
            wt_d("scalar", ld)
            if ci >= 2:
                wt_d("scalar", c_wr[ci - 2])
            a_cc[ci] = ins("scalar", lambda E, b=b: E.copy(
                out=cwt[:, b * 2048:(b + 1) * 2048],
                in_=rawf[:, b * 2048:(b + 1) * 2048]), inc=True)
            wts("scalar", a_cc[ci])
            if wname in ("q", "k", "o"):
                scd = {"q": sc_q, "k": sc_k, "o": sc_o}[wname]
                c_wr[ci] = dma(
                    "scalar",
                    scd[:, :, kc * 128:(kc + 1) * 128].rearrange("h p c -> p h c"),
                    cwt[:, b * 2048:(b + 1) * 2048].rearrange("p (h c) -> p h c", c=128))
            else:
                scd = sc_v if wname == "v" else sc_b
                for sl8 in range(8):
                    c_wr[ci] = dma("scalar", scd[sl8, :, kc * 256:(kc + 1) * 256],
                                   cwt[:, b * 2048 + sl8 * 256:b * 2048 + (sl8 + 1) * 256])

        ci_next = 0
        for k in range(NMAT):
            emit_mat(k)
            want = (k + 1) * NCAST // NMAT
            while ci_next < want:
                emit_cast(ci_next)
                ci_next += 1
        while ci_next < NCAST:
            emit_cast(ci_next)
            ci_next += 1

        castB_done = a_cc[NCAST - 1]
        wt_d("gpsimd", [g_wr[k] for k in range(NMAT)])
        v_cc = ins("gpsimd", lambda E: E.collective_compute(
            "AllGather", ALU.bypass, replica_groups=[list(range(8))],
            ins=[sc_part.opt()], outs=[sc_all.opt()]), inc=True)
        for i in range(NP):
            if dpc["scalar"][i]:
                q["scalar"].append(lambda E, h=dpool["scalar"][i], v=dpc["scalar"][i]:
                                   E.wait_ge(h, v))
        cast_all_done = ins("scalar", lambda E: E.nop(), inc=True)

        PARTC = 33024
        GOFF = 0
        UOFF = IT * 2048
        DOFF = 2 * IT * 2048

        def load_image(dst_ap_fn, col0, w):
            toks = []
            off = 0
            while w > 0:
                r = col0 // PARTC
                lc = col0 % PARTC
                take = min(w, PARTC - lc)
                toks.append(dma("sync", dst_ap_fn(off, take),
                                sc_all[r, :, lc:lc + take]))
                col0 += take
                off += take
                w -= take
            return toks
        cast_all_done = ins("scalar", lambda E: E.nop(), inc=True)

        t_o = {}
        t_sup = {}
        a_oev = {}
        a_ktm = {}
        v_att = {}
        v_scast = {}
        v_dn = {}
        v_wo = {}
        t_gu = {}
        t_dn = {}
        ffn_i = [0]
        d_x = {}
        st_x = {}

        def ln_phase(marker):
            mm_first(4)
            vm = None
            for dc in range(KC):
                wt_m("tensor", marker(dc))
                last = dc == KC - 1
                vm = ins("tensor", lambda E, dc=dc, first=(dc == 0), last=last: E.matmul(
                    ps[4][0:1, 0:TS], onesk[:, :], x_sb[:, dc * TS:(dc + 1) * TS],
                    start=first, stop=last), inc=last)
            mm_last(4, vm)
            mm_first(5)
            a_sq = {}
            t_sqm = {}
            for dc in range(KC):
                wt_m("scalar", marker(dc))
                if dc >= 4:
                    wt("scalar", "c", "tensor", t_sqm[dc - 4])
                a_sq[dc] = ins("scalar", lambda E, dc=dc: E.activation(
                    out=sqb[:, (dc % 4) * TS:(dc % 4 + 1) * TS],
                    in_=x_sb[:, dc * TS:(dc + 1) * TS], func=AF.Square), inc=True)
                wt("tensor", "c", "scalar", a_sq[dc])
                last = dc == KC - 1
                t_sqm[dc] = ins("tensor", lambda E, dc=dc, first=(dc == 0), last=last: E.matmul(
                    ps[5][0:1, 0:TS], onesk[:, :],
                    sqb[:, (dc % 4) * TS:(dc % 4 + 1) * TS],
                    start=first, stop=last), inc=True)
            mm_last(5, t_sqm[KC - 1])
            prd(4, "vector", lambda E: E.tensor_copy(out=m_sb, in_=ps[4][0:1, 0:TS]))
            prd(5, "vector", lambda E: E.tensor_copy(out=ms_sb, in_=ps[5][0:1, 0:TS]))
            ins("vector", lambda E: E.tensor_mul(m2_sb, m_sb, m_sb))
            ins("vector", lambda E: E.tensor_sub(var_sb, ms_sb, m2_sb))
            v_var = ins("vector", lambda E: E.tensor_scalar_add(var_sb, var_sb, EPS), inc=True)
            wt("scalar", "c", "vector", v_var)
            a_ln = ins("scalar", lambda E: E.activation(out=ln_sb, in_=var_sb, func=AF.Ln), inc=True)
            wt("vector", "c", "scalar", a_ln)
            v_l2 = ins("vector", lambda E: E.tensor_scalar_mul(ln_sb, ln_sb, -0.5), inc=True)
            wt("scalar", "c", "vector", v_l2)
            a_rstd = ins("scalar", lambda E: E.activation(
                out=rstd_sb, in_=ln_sb, func=AF.Exp), inc=True)
            wt("tensor", "c", "scalar", a_rstd)
            mm_first(4)
            t_bm = ins("tensor", lambda E: E.matmul(
                ps[4][:, 0:TS], ones_c[:, :], m_sb, start=True, stop=True), inc=True)
            mm_last(4, t_bm)
            mm_first(5)
            t_br = ins("tensor", lambda E: E.matmul(
                ps[5][:, 0:TS], ones_c[:, :], rstd_sb, start=True, stop=True), inc=True)
            mm_last(5, t_br)
            wt("vector", "c", "tensor", t_br)
            v_app = []
            for dc in range(KC):
                wt_m("vector", marker(dc))
                ins("vector", lambda E, dc=dc: E.tensor_sub(
                    lntmp[:, :], x_sb[:, dc * TS:(dc + 1) * TS], ps[4][:, 0:TS]))
                v_app.append(ins("vector", lambda E, dc=dc: E.tensor_mul(
                    xln[:, dc * TS:(dc + 1) * TS], lntmp[:, :], ps[5][:, 0:TS]), inc=True))
            pbank[4]["r"]["vector"] = v_app[-1]
            pbank[5]["r"]["vector"] = v_app[-1]
            return v_app

        a_silu = {}
        v_h = {}

        def ffn_phase(v_xln):
            fi = ffn_i[0]
            ffn_i[0] += 1
            t_enter = cnt["tensor"]
            v_enter = cnt["vector"]
            d_g = {}
            d_u = {}

            def emit_gu_load(it):
                b = it % 2
                if it >= 2:
                    wt("sync", "c", "tensor", t_gu[(fi, it - 2)])
                else:
                    wt("sync", "c", "tensor", t_enter)
                wt("sync", "c", "gpsimd", v_cc)
                d_g[it] = load_image(
                    lambda off, take, b=b: wg[b][:, off:off + take],
                    GOFF + it * 2048, 2048)
                d_u[it] = load_image(
                    lambda off, take, b=b: wu[b][:, off:off + take],
                    UOFF + it * 2048, 2048)

            emit_gu_load(0)
            emit_gu_load(1)
            for it in range(IT):
                b = it % 2
                bg, bu = 2 * b, 2 * b + 1
                wt_d("tensor", d_g[it] + d_u[it])
                mm_first(bg)
                tg = None
                for kc in range(KC):
                    wt("tensor", "c", "vector", v_xln[kc])
                    last = kc == KC - 1
                    tg = ins("tensor", lambda E, kc=kc, b=b, bg=bg,
                             first=(kc == 0), last=last: E.matmul(
                        ps[bg][:, 0:TS], wg[b][:, kc * 128:(kc + 1) * 128],
                        xln[:, kc * TS:(kc + 1) * TS],
                        start=first, stop=last), inc=last)
                mm_last(bg, tg)
                mm_first(bu)
                tu = None
                for kc in range(KC):
                    last = kc == KC - 1
                    tu = ins("tensor", lambda E, kc=kc, b=b, bu=bu,
                             first=(kc == 0), last=last: E.matmul(
                        ps[bu][:, 0:TS], wu[b][:, kc * 128:(kc + 1) * 128],
                        xln[:, kc * TS:(kc + 1) * TS],
                        start=first, stop=last), inc=last)
                mm_last(bu, tu)
                t_gu[(fi, it)] = tu
                if it >= 2:
                    wt("scalar", "c", "vector", v_h[(fi, it - 2)])
                else:
                    wt("scalar", "c", "vector", v_enter)
                a_silu[(fi, it)] = prd(bg, "scalar", lambda E, b=b, bg=bg: E.activation(
                    out=sg[:, b * TS:(b + 1) * TS], in_=ps[bg][:, 0:TS], func=AF.Silu))
                wt("vector", "c", "scalar", a_silu[(fi, it)])
                if fi == 0:
                    wt("vector", "c", "scalar", cast_all_done)
                v_h[(fi, it)] = prd(bu, "vector", lambda E, it=it, b=b, bu=bu: E.tensor_mul(
                    h_sl[:, it * TS:(it + 1) * TS], sg[:, b * TS:(b + 1) * TS],
                    ps[bu][:, 0:TS]))
                if it + 2 < IT:
                    emit_gu_load(it + 2)
            d_d = {}

            def emit_d_load(dt):
                b = dt % 2
                if dt >= 2:
                    wt("sync", "c", "tensor", t_dn[(fi, dt - 2)])
                else:
                    wt("sync", "c", "tensor", t_enter)
                wt("sync", "c", "gpsimd", v_cc)
                d_d[dt] = load_image(
                    lambda off, take, b=b: wd[b][:, off:off + take],
                    DOFF + dt * INTER, INTER)

            emit_d_load(0)
            emit_d_load(1)
            for dt in range(KC):
                b = dt % 2
                bank = dt % 2
                wt_d("tensor", d_d[dt])
                wt("tensor", "c", "vector", v_h[(fi, IT - 1)])
                mm_first(bank)
                td = None
                for ic in range(IT):
                    last = ic == IT - 1
                    td = ins("tensor", lambda E, ic=ic, b=b, bank=bank,
                             first=(ic == 0), last=last: E.matmul(
                        ps[bank][:, 0:TS], wd[b][:, ic * 128:(ic + 1) * 128],
                        h_sl[:, ic * TS:(ic + 1) * TS],
                        start=first, stop=last), inc=last)
                mm_last(bank, td)
                t_dn[(fi, dt)] = td
                v_dn[(fi, dt)] = prd(bank, "vector", lambda E, dt=dt, bank=bank:
                                     E.scalar_tensor_tensor(
                    out=x_sb[:, dt * TS:(dt + 1) * TS],
                    in0=ps[bank][:, 0:TS], scalar=0.5,
                    in1=x_sb[:, dt * TS:(dt + 1) * TS],
                    op0=ALU.mult, op1=ALU.add))
                if dt + 2 < KC:
                    emit_d_load(dt + 2)
            return fi

        def attn_phase(s, v_xln):
            t_enter = cnt["tensor"]
            v_enter = cnt["vector"]
            t_qk = {}
            a_qk = {}
            d_qk = {}
            qk_srcs = [(sc_q, q_fm, hh) for hh in range(NH)] + \
                      [(sc_k, k_fm, hh) for hh in range(NH)]

            def emit_qk_load(i):
                scd, dstv, hh = qk_srcs[i]
                b = i % 2
                if i >= 2:
                    wt("sync", "c", "tensor", t_qk[i - 2])
                else:
                    wt("sync", "c", "tensor", t_enter)
                wt("sync", "c", "scalar", cast_all_done)
                d_qk[i] = dma("sync", qkw[b][:, :], scd[hh, :, :])

            emit_qk_load(0)
            emit_qk_load(1)
            for i in range(2 * NH):
                scd, dstv, hh = qk_srcs[i]
                b = i % 2
                bank = i % 2
                wt_d("tensor", d_qk[i])
                mm_first(bank)
                tl = None
                for kc in range(KC):
                    wt("tensor", "c", "vector", v_xln[kc])
                    last = kc == KC - 1
                    tl = ins("tensor", lambda E, kc=kc, b=b, bank=bank,
                             first=(kc == 0), last=last: E.matmul(
                        ps[bank][:, 0:TS], qkw[b][:, kc * 128:(kc + 1) * 128],
                        xln[:, kc * TS:(kc + 1) * TS],
                        start=first, stop=last), inc=last)
                mm_last(bank, tl)
                t_qk[i] = tl
                a_qk[i] = prd(bank, "scalar", lambda E, hh=hh, bank=bank, dstv=dstv:
                              E.copy(out=dstv[:, hh * TS:(hh + 1) * TS],
                                     in_=ps[bank][:, 0:TS]))
                if i + 2 < 2 * NH:
                    emit_qk_load(i + 2)
            v_u_last = 0
            t_vb_last = {}
            a_sig_first = [True]
            d_vb = {}
            vb4 = [vbw[0][:, 0:4096], vbw[0][:, 4096:8192],
                   vbw[1][:, 0:4096], vbw[1][:, 4096:8192]]

            def emit_vb_load(sl8):
                if sl8 >= 2:
                    wt("sync", "c", "tensor", t_vb_last[sl8 - 2])
                elif sl8 == 0:
                    wt("sync", "c", "tensor", t_qk[2 * NH - 1])
                bq = (sl8 % 2) * 2
                d_vb[sl8] = [dma("sync", vb4[bq], sc_v[sl8]),
                             dma("sync", vb4[bq + 1], sc_b[sl8])]

            emit_vb_load(0)
            emit_vb_load(1)
            for sl8 in range(8):
                bq = (sl8 % 2) * 2
                for tt in range(NCH):
                    bv, bb = 2 * (tt % 2), 2 * (tt % 2) + 1
                    wt_d("tensor", d_vb[sl8])
                    mm_first(bv)
                    tv = None
                    for kc in range(KC):
                        last = kc == KC - 1
                        tv = ins("tensor", lambda E, kc=kc, tt=tt, bv=bv, bq=bq,
                                 first=(kc == 0), last=last: E.matmul(
                            ps[bv][:, 0:256],
                            xln[:, kc * TS + tt * 128:kc * TS + (tt + 1) * 128],
                            vb4[bq][:, kc * 256:(kc + 1) * 256],
                            start=first, stop=last), inc=last)
                    mm_last(bv, tv)
                    mm_first(bb)
                    tb = None
                    for kc in range(KC):
                        last = kc == KC - 1
                        tb = ins("tensor", lambda E, kc=kc, tt=tt, bb=bb, bq=bq,
                                 first=(kc == 0), last=last: E.matmul(
                            ps[bb][:, 0:256],
                            xln[:, kc * TS + tt * 128:kc * TS + (tt + 1) * 128],
                            vb4[bq + 1][:, kc * 256:(kc + 1) * 256],
                            start=first, stop=last), inc=last)
                    mm_last(bb, tb)
                    t_vb_last[sl8] = tb
                    if a_sig_first[0]:
                        wt("scalar", "c", "vector", v_enter)
                        a_sig_first[0] = False
                    else:
                        wt("scalar", "c", "vector", v_u_last)
                    a_sig = prd(bb, "scalar", lambda E, bb=bb: E.activation(
                        out=sg[:, 0:256], in_=ps[bb][:, 0:256], func=AF.Sigmoid))
                    wt("vector", "c", "scalar", a_sig)
                    v_u_last = prd(bv, "vector", lambda E, sl8=sl8, tt=tt, bv=bv:
                                   E.tensor_mul(
                        u_tm[:, tt * D + sl8 * 256:tt * D + (sl8 + 1) * 256],
                        sg[:, 0:256], ps[bv][:, 0:256]))
                if sl8 + 2 < 8:
                    emit_vb_load(sl8 + 2)
            for bk in (0, 1, 2, 3, 4, 5, "t0", "t1"):
                for oe, v in pbank[bk]["r"].items():
                    wt("tensor", "c", oe, v)
                pbank[bk]["r"] = {}

            def head_tail(gc, c, hh):
                bo = 2 if hh % 2 == 0 else 5
                wt("tensor", "c", "vector", v_att[(gc, hh)])
                if (gc, hh - 2) in a_oev:
                    wt("tensor", "c", "scalar", a_oev[(gc, hh - 2)])
                elif (gc - 1, NH - 2 + (hh % 2)) in a_oev:
                    wt("tensor", "c", "scalar", a_oev[(gc - 1, NH - 2 + (hh % 2))])
                if (gc - 1, hh) in v_scast:
                    wt("tensor", "c", "vector", v_scast[(gc - 1, hh)])
                ins("tensor", lambda E, c=c, hh=hh, bo=bo: E.matmul(
                    ps[bo][:, 0:128],
                    u_tm[:, c * D + hh * HD:c * D + (hh + 1) * HD],
                    attm[:, (hh % 2) * 128:(hh % 2 + 1) * 128],
                    start=True, stop=False))
                t_o[(gc, hh)] = ins("tensor", lambda E, c=c, hh=hh, bo=bo: E.matmul(
                    ps[bo][:, 0:128],
                    S_bf[:, hh * HD:(hh + 1) * HD],
                    q_fm[:, hh * TS + c * 128:hh * TS + (c + 1) * 128],
                    start=False, stop=True), inc=True)
                bs = 3 + (hh % 2)
                wt("tensor", "c", "scalar", a_ktm[(gc, hh)])
                if (gc, hh - 2) in v_scast:
                    wt("tensor", "c", "vector", v_scast[(gc, hh - 2)])
                elif c > 0:
                    wt("tensor", "c", "vector", v_scast[(gc - 1, NH - 2 + (hh % 2))])
                t_sup[(gc, hh)] = ins("tensor", lambda E, c=c, hh=hh, bs=bs: E.matmul(
                    ps[bs][:, 0:128],
                    ktm[:, (hh % 2) * 128:(hh % 2 + 1) * 128],
                    u_tm[:, c * D + hh * HD:c * D + (hh + 1) * HD],
                    start=True, stop=True), inc=True)
                wt("scalar", "c", "tensor", t_o[(gc, hh)])
                a_oev[(gc, hh)] = ins("scalar", lambda E, c=c, hh=hh, bo=bo: E.copy(
                    out=o_fm[:, hh * TS + c * 128:hh * TS + (c + 1) * 128],
                    in_=ps[bo][:, 0:128]), inc=True)
                wt("vector", "c", "tensor", t_sup[(gc, hh)])
                ins("vector", lambda E, hh=hh, bs=bs: E.tensor_add(
                    S_f32[:, hh * HD:(hh + 1) * HD],
                    S_f32[:, hh * HD:(hh + 1) * HD],
                    ps[bs][:, 0:128]))
                v_scast[(gc, hh)] = ins("vector", lambda E, hh=hh: E.tensor_copy(
                    out=S_bf[:, hh * HD:(hh + 1) * HD],
                    in_=S_f32[:, hh * HD:(hh + 1) * HD]), inc=True)

            for c in range(NCH):
                gc = s * NCH + c
                for hh in range(NH):
                    slot = hh % 2
                    ba = hh % 2
                    if (gc, hh - 2) in a_ktm:
                        wt("tensor", "c", "scalar", a_ktm[(gc, hh - 2)])
                    elif (gc - 1, NH - 2 + slot) in a_ktm:
                        wt("tensor", "c", "scalar", a_ktm[(gc - 1, NH - 2 + slot)])
                    wt("tensor", "c", "scalar", a_qk[NH + hh])
                    wt("tensor", "c", "vector", v_u_last)
                    t_ktp = ins("tensor", lambda E, c=c, hh=hh, slot=slot: E.transpose(
                        pst[slot][:, 0:128],
                        k_fm[:, hh * TS + c * 128:hh * TS + (c + 1) * 128],
                        identb[:, :]), inc=True)
                    if (gc, hh - 2) in v_att:
                        wt("tensor", "c", "vector", v_att[(gc, hh - 2)])
                    elif c > 0:
                        wt("tensor", "c", "vector", v_att[(gc - 1, NH - 2 + ba)])
                    wt("tensor", "c", "scalar", a_qk[hh])
                    t_att = ins("tensor", lambda E, c=c, hh=hh, ba=ba: E.matmul(
                        ps[ba][:, 0:128],
                        k_fm[:, hh * TS + c * 128:hh * TS + (c + 1) * 128],
                        q_fm[:, hh * TS + c * 128:hh * TS + (c + 1) * 128],
                        start=True, stop=True), inc=True)
                    wt("scalar", "c", "tensor", t_ktp)
                    if (gc, hh - 2) in t_sup:
                        wt("scalar", "c", "tensor", t_sup[(gc, hh - 2)])
                    elif (gc - 1, NH - 2 + slot) in t_sup:
                        wt("scalar", "c", "tensor", t_sup[(gc - 1, NH - 2 + slot)])
                    a_ktm[(gc, hh)] = ins("scalar", lambda E, slot=slot: E.copy(
                        out=ktm[:, slot * 128:(slot + 1) * 128],
                        in_=pst[slot][:, 0:128]), inc=True)
                    wt("vector", "c", "tensor", t_att)
                    if (gc, hh - 2) in t_o:
                        wt("vector", "c", "tensor", t_o[(gc, hh - 2)])
                    elif (gc - 1, NH - 2 + ba) in t_o:
                        wt("vector", "c", "tensor", t_o[(gc - 1, NH - 2 + ba)])
                    v_att[(gc, hh)] = ins("vector", lambda E, ba=ba, slot=slot: E.tensor_mul(
                        attm[:, slot * 128:(slot + 1) * 128],
                        ps[ba][:, 0:128], maskc[:, :]), inc=True)
                    if hh > 0:
                        head_tail(gc, c, hh - 1)
                head_tail(gc, c, NH - 1)
            for bk in (0, 1, 2, 3, 4, 5):
                pbank[bk]["w"] = ("tensor", cnt["tensor"])
            pbank[0]["r"]["vector"] = cnt["vector"]
            pbank[1]["r"]["vector"] = cnt["vector"]
            pbank[2]["r"]["scalar"] = cnt["scalar"]
            pbank[5]["r"]["scalar"] = cnt["scalar"]
            pbank[3]["r"]["vector"] = cnt["vector"]
            pbank[4]["r"]["vector"] = cnt["vector"]
            pbank["t0"]["r"]["scalar"] = cnt["scalar"]
            pbank["t1"]["r"]["scalar"] = cnt["scalar"]

            gcl = s * NCH + NCH - 1
            t_wo = {}
            d_wo = {}

            def emit_wo_load(dt):
                b = dt % 2
                if dt >= 2:
                    wt("sync", "c", "tensor", t_wo[dt - 2])
                else:
                    wt("sync", "c", "tensor", t_qk[2 * NH - 2 + b])
                d_wo[dt] = dma("sync", qkw[b][:, :], sc_o[dt, :, :])

            emit_wo_load(0)
            emit_wo_load(1)
            for dt in range(KC):
                b = dt % 2
                bank = dt % 2
                wt_d("tensor", d_wo[dt])
                wt("tensor", "c", "scalar", a_oev[(gcl, NH - 1)])
                mm_first(bank)
                tw = None
                for kc in range(KC):
                    last = kc == KC - 1
                    tw = ins("tensor", lambda E, kc=kc, b=b, bank=bank,
                             first=(kc == 0), last=last: E.matmul(
                        ps[bank][:, 0:TS], qkw[b][:, kc * 128:(kc + 1) * 128],
                        o_fm[:, kc * TS:(kc + 1) * TS],
                        start=first, stop=last), inc=last)
                mm_last(bank, tw)
                t_wo[dt] = tw
                v_wo[(s, dt)] = prd(bank, "vector", lambda E, dt=dt, bank=bank:
                                    E.tensor_add(
                    x_sb[:, dt * TS:(dt + 1) * TS],
                    x_sb[:, dt * TS:(dt + 1) * TS],
                    ps[bank][:, 0:TS]))
                if dt + 2 < KC:
                    emit_wo_load(dt + 2)

        for s in range(NSLAB):
            for dc in range(KC):
                if s == 0:
                    wt("sync", "c", "scalar", castB_done)
                else:
                    wt_d("sync", st_x[(s - 1, dc)])
                d_x[(s, dc)] = dma("sync", x_sb[:, dc * TS:(dc + 1) * TS],
                                   x_in[dc * 128:(dc + 1) * 128, s * TS:(s + 1) * TS])

            v_xln1 = ln_phase(lambda dc, s=s: ("d", d_x[(s, dc)]))
            fi1 = ffn_phase(v_xln1)
            v_xln2 = ln_phase(lambda dc, fi=fi1: ("c", "vector", v_dn[(fi, dc)]))
            attn_phase(s, v_xln2)
            v_xln3 = ln_phase(lambda dc, s=s: ("c", "vector", v_wo[(s, dc)]))
            fi2 = ffn_phase(v_xln3)

            for dc in range(KC):
                wt("sync", "c", "vector", v_dn[(fi2, dc)])
                st_x[(s, dc)] = dma("sync",
                                    y_out[dc * 128:(dc + 1) * 128, s * TS:(s + 1) * TS],
                                    x_sb[:, dc * TS:(dc + 1) * TS])

        for e in DQUEUES:
            for i in range(NP):
                if dpc[e][i]:
                    q[e].append(lambda E, h=dpool[e][i], v=dpc[e][i]: E.wait_ge(h, v))

        block = es.enter_context(nc.Block())

        @block.sync
        def _(E):
            for f in q["sync"]:
                f(E)

        @block.gpsimd
        def _(E):
            for f in q["gpsimd"]:
                f(E)

        @block.vector
        def _(E):
            for f in q["vector"]:
                f(E)

        @block.scalar
        def _(E):
            for f in q["scalar"]:
                f(E)

        @block.tensor
        def _(E):
            for f in q["tensor"]:
                f(E)

    return nc


_CACHE = {}


def kernel(L0_attn_hidden_states, L0_attn_Wq, L0_attn_Wk, L0_attn_Wv,
           L0_attn_Wb, L0_attn_Wo, L0_ffn_lut, L0_ffn_gate_idx,
           L0_ffn_up_idx, L0_ffn_down_idx):
    x = np.asarray(L0_attn_hidden_states, dtype=np.float32)
    gidx = np.asarray(L0_ffn_gate_idx, dtype=np.int32)
    uidx = np.asarray(L0_ffn_up_idx, dtype=np.int32)
    didx = np.asarray(L0_ffn_down_idx, dtype=np.int32)
    gb = gidx.reshape(IT, 128, KC, 128).transpose(0, 2, 1, 3).reshape(-1, 128, 128)
    ub = uidx.reshape(IT, 128, KC, 128).transpose(0, 2, 1, 3).reshape(-1, 128, 128)
    db = didx.reshape(KC, 128, IT, 128).transpose(0, 2, 1, 3).reshape(-1, 128, 128)
    blocks = np.concatenate([gb, ub, db], axis=0)
    base = {
        "wq": np.ascontiguousarray(np.asarray(L0_attn_Wq, dtype=np.float32)),
        "wk": np.ascontiguousarray(np.asarray(L0_attn_Wk, dtype=np.float32)),
        "wv": np.ascontiguousarray(np.asarray(L0_attn_Wv, dtype=np.float32)),
        "wb": np.ascontiguousarray(np.asarray(L0_attn_Wb, dtype=np.float32)),
        "wo": np.ascontiguousarray(np.asarray(L0_attn_Wo, dtype=np.float32)),
        "lut": np.ascontiguousarray(np.asarray(L0_ffn_lut, dtype=np.float32).reshape(1, 16)),
    }
    if "nc" not in _CACHE:
        _CACHE["nc"] = build()
    nc = _CACHE["nc"]
    in_maps = []
    for c in range(8):
        m = dict(base)
        m["x_in"] = np.ascontiguousarray(x[c].T)
        m["myidx"] = np.ascontiguousarray(
            blocks[c * 258:(c + 1) * 258].transpose(1, 0, 2).reshape(128, 33024))
        in_maps.append(m)
    res = run_bass_kernel_spmd(nc, in_maps, core_ids=list(range(8)))
    out = np.stack([np.asarray(res.results[c]["y"]).T for c in range(8)], axis=0)
    return np.ascontiguousarray(out.astype(np.float32))


# version 22
# speedup vs baseline: 1.0599x; 1.0599x over previous
import numpy as np
import contextlib
import concourse.bass as bass
import concourse.mybir as mybir
from concourse.bass_utils import run_bass_kernel_spmd

F32 = mybir.dt.float32
BF16 = mybir.dt.bfloat16
I32 = mybir.dt.int32
AF = mybir.ActivationFunctionType
ALU = mybir.AluOpType

D = 2048
SEQ = 2048
NH = 16
HD = 128
INTER = 5504
KC = D // 128
IT = INTER // 128
TS = 512
NSLAB = SEQ // TS
CH = 128
NCH = TS // CH
EPS = 1e-5

ENGS = ("sync", "gpsimd", "vector", "scalar", "tensor")
DQUEUES = ("sync", "scalar")

DEBUG_EDGES = False


def build():
    nc = bass.Bass("TRN2", target_bir_lowering=False,
                   detect_race_conditions=DEBUG_EDGES)

    x_in = nc.dram_tensor("x_in", [D, SEQ], F32, kind="ExternalInput").ap()
    wq_d = nc.dram_tensor("wq", [D, D], F32, kind="ExternalInput").ap()
    wk_d = nc.dram_tensor("wk", [D, D], F32, kind="ExternalInput").ap()
    wv_d = nc.dram_tensor("wv", [D, D], F32, kind="ExternalInput").ap()
    wb_d = nc.dram_tensor("wb", [D, D], F32, kind="ExternalInput").ap()
    wo_d = nc.dram_tensor("wo", [D, D], F32, kind="ExternalInput").ap()
    lut_d = nc.dram_tensor("lut", [1, 16], F32, kind="ExternalInput").ap()
    y_out = nc.dram_tensor("y", [D, SEQ], F32, kind="ExternalOutput").ap()

    myidx_d = nc.dram_tensor("myidx", [128, 33024], I32, kind="ExternalInput").ap()
    sc_part = nc.dram_tensor("sc_part", [128, 33024], BF16, kind="Internal").ap()
    sc_all = nc.dram_tensor("sc_all", [8, 128, 33024], BF16, kind="Internal",
                            addr_space="Shared").ap()
    sc_q = nc.dram_tensor("sc_q", [NH, 128, D], BF16, kind="Internal").ap()
    sc_k = nc.dram_tensor("sc_k", [NH, 128, D], BF16, kind="Internal").ap()
    sc_o = nc.dram_tensor("sc_o", [KC, 128, D], BF16, kind="Internal").ap()
    sc_v = nc.dram_tensor("sc_v", [8, 128, KC * 256], BF16, kind="Internal").ap()
    sc_b = nc.dram_tensor("sc_b", [8, 128, KC * 256], BF16, kind="Internal").ap()

    es = contextlib.ExitStack()
    with es:
        sbuf = lambda name, shp, dt: es.enter_context(nc.sbuf_tensor(name, shp, dt))
        psal = lambda name, shp, dt: es.enter_context(nc.psum_tensor(name, shp, dt))

        x_sb = sbuf("x_sb", [128, KC * TS], F32)
        xln = sbuf("xln", [128, KC * TS], BF16)
        pool = sbuf("pool", [128, 24576], BF16)
        h_sl = pool[:, 0:IT * TS]
        q_fm = pool[:, 0:8192]
        k_fm = pool[:, 8192:16384]
        u_tm = pool[:, 16384:24576]
        o_fm = sbuf("o_fm", [128, 8192], BF16)
        S_f32 = sbuf("S_f32", [128, NH * HD], F32)
        S_bf = sbuf("S_bf", [128, NH * HD], BF16)
        wpool = sbuf("wpool", [128, 20480], BF16)
        wg = [wpool[:, 0:2048], wpool[:, 2048:4096], wpool[:, 16384:18432]]
        wu = [wpool[:, 4096:6144], wpool[:, 6144:8192], wpool[:, 18432:20480]]
        wd = [wpool[:, 8192:8192 + INTER], wpool[:, 8192 + INTER:8192 + 2 * INTER]]
        qkw = [wpool[:, 0:2048], wpool[:, 2048:4096]]
        vbw = [wpool[:, 4096:12288], wpool[:, 12288:20480]]
        sg = sbuf("sg", [128, 2 * TS], BF16)
        sqb = sbuf("sqb", [128, 4 * TS], F32)
        lntmp = sbuf("lntmp", [128, TS], F32)
        ktm = sbuf("ktm", [128, 256], BF16)
        attm = sbuf("attm", [128, 256], BF16)
        stat = sbuf("stat", [1, 6 * TS], F32)
        m_sb = stat[0:1, 0:TS]
        ms_sb = stat[0:1, TS:2 * TS]
        m2_sb = stat[0:1, 2 * TS:3 * TS]
        var_sb = stat[0:1, 3 * TS:4 * TS]
        ln_sb = stat[0:1, 4 * TS:5 * TS]
        rstd_sb = stat[0:1, 5 * TS:6 * TS]
        lut_sb = sbuf("lut_sb", [1, 16], F32)
        lutb = sbuf("lutb", [128, 16], F32)
        ones_c = sbuf("ones_c", [1, 128], F32)
        onesk = sbuf("onesk", [128, 1], F32)
        maskc = sbuf("maskc", [128, 128], BF16)
        identb = sbuf("identb", [128, 128], BF16)
        rawi = sbuf("rawi", [128, 4096], I32)
        bfb = pool[:, 0:4096]
        accb = pool[:, 4096:8192]
        tmpb = pool[:, 8192:10240]
        wtb = pool[:, 10240:14336]
        cwt = pool[:, 14336:18432]
        rawf = x_sb[:, 0:4096]

        ps = [psal(f"ps{i}", [128, 512], F32) for i in range(6)]
        pst = [psal("pst0", [128, 1024], BF16), psal("pst1", [128, 1024], BF16)]

        csem = {e: es.enter_context(nc.semaphore("c_" + e)) for e in ENGS}
        NP = 1 if DEBUG_EDGES else 8
        dpool = {e: [es.enter_context(nc.semaphore(f"d_{e}{i}")) for i in range(NP)]
                 for e in DQUEUES}
        dpc = {e: [0] * NP for e in DQUEUES}
        dseq = {e: 0 for e in DQUEUES}
        cnt = {e: 0 for e in ENGS}
        q = {e: [] for e in ENGS}
        lastw = {e: {} for e in ENGS}

        def ins(e, f, inc=False):
            if DEBUG_EDGES:
                prev = cnt[e]
                h = csem[e]
                if prev:
                    q[e].append(lambda E, h=h, v=prev: E.wait_ge(h, v))
                cnt[e] += 1
                v = cnt[e]
                q[e].append(lambda E, f=f, h=h: f(E).then_inc(h, 1))
                return v
            if inc:
                cnt[e] += 1
                v = cnt[e]
                h = csem[e]
                q[e].append(lambda E, f=f, h=h: f(E).then_inc(h, 1))
                return v
            q[e].append(f)
            return None

        def dma(e, out_ap, in_ap):
            i = dseq[e] % NP
            dseq[e] += 1
            h = dpool[e][i]
            if dpc[e][i]:
                q[e].append(lambda E, h=h, v=dpc[e][i]: E.wait_ge(h, v))
            dpc[e][i] += 16
            v = dpc[e][i]
            q[e].append(lambda E, o=out_ap, i_=in_ap, h=h:
                        E.dma_start(out=o, in_=i_).then_inc(h, 16))
            return (e, i, v)

        def wt(e, kind, oe, v):
            if not v or v <= 0:
                return
            if kind == "c" and oe == e:
                return
            key = (kind, oe)
            if lastw[e].get(key, 0) >= v:
                return
            lastw[e][key] = v
            h = csem[oe]
            q[e].append(lambda E, h=h, v=v: E.wait_ge(h, v))

        def wts(e, v):
            if not v:
                return
            key = ("self",)
            if lastw[e].get(key, 0) >= v:
                return
            lastw[e][key] = v
            q[e].append(lambda E, h=csem[e], v=v: E.wait_ge(h, v))

        def wt_d(e, tokens):
            if tokens is None:
                return
            if isinstance(tokens, tuple):
                tokens = [tokens]
            for (oe, i, v) in tokens:
                key = ("d", oe, i)
                if lastw[e].get(key, 0) >= v:
                    continue
                lastw[e][key] = v
                h = dpool[oe][i]
                q[e].append(lambda E, h=h, v=v: E.wait_ge(h, v))

        def wt_m(e, m):
            if m[0] == "c":
                wt(e, "c", m[1], m[2])
            else:
                wt_d(e, m[1])

        pbank = {i: {"w": ("tensor", 0), "r": {}}
                 for i in list(range(6)) + ["t0", "t1"]}

        def mm_first(bank):
            for oe, v in pbank[bank]["r"].items():
                wt("tensor", "c", oe, v)
            pbank[bank]["r"] = {}

        def mm_last(bank, v):
            pbank[bank]["w"] = ("tensor", v)

        def prd(bank, e, f):
            we, wv = pbank[bank]["w"]
            wt(e, "c", we, wv)
            v = ins(e, f, inc=True)
            pbank[bank]["r"][e] = max(pbank[bank]["r"].get(e, 0), v)
            return v

        ins("gpsimd", lambda E: E.memset(ones_c[:, :], 1.0))
        ins("gpsimd", lambda E: E.memset(onesk[:, :], 1.0 / D))
        g1 = ins("gpsimd", lambda E: E.memset(identb[:, :], 0.0), inc=True)
        q["gpsimd"].append(lambda E, h=csem["gpsimd"], v=g1: E.wait_ge(h, v))
        ins("gpsimd", lambda E: E.affine_select(
            out=identb[:, :], in_=identb[:, :], compare_op=ALU.not_equal,
            fill=1.0, base=0, pattern=[[-1, 128]], channel_multiplier=1))
        g2 = ins("gpsimd", lambda E: E.memset(maskc[:, :], 1.0), inc=True)
        q["gpsimd"].append(lambda E, h=csem["gpsimd"], v=g2: E.wait_ge(h, v))
        ins("gpsimd", lambda E: E.affine_select(
            out=maskc[:, :], in_=maskc[:, :], compare_op=ALU.is_ge,
            fill=0.0, base=0, pattern=[[1, 128]], channel_multiplier=-1))
        ins("gpsimd", lambda E: E.memset(S_f32[:, :], 0.0))
        v_init = ins("gpsimd", lambda E: E.memset(S_bf[:, :], 0.0), inc=True)

        d_lut = dma("sync", lut_sb[:, :], lut_d)
        wt_d("tensor", d_lut)
        wt("tensor", "c", "gpsimd", v_init)
        mm_first(5)
        t_lut = ins("tensor", lambda E: E.matmul(
            ps[5][:, 0:16], ones_c[:, :], lut_sb[:, :], start=True, stop=True), inc=True)
        mm_last(5, t_lut)
        v_lut = prd(5, "vector", lambda E: E.tensor_copy(out=lutb[:, :], in_=ps[5][:, 0:16]))
        q["vector"].append(lambda E, h=csem["vector"], v=v_lut: E.wait_ge(h, v))
        wt("vector", "c", "gpsimd", v_init)

        mat = [(c, 2048) for c in range(16)] + [(16, 256)]
        NMAT = len(mat)

        casts = [(wn, wd_, kc) for wn, wd_ in
                 (("q", wq_d), ("k", wk_d), ("v", wv_d), ("b", wb_d), ("o", wo_d))
                 for kc in range(KC)]
        NCAST = len(casts)

        v_cast = {}
        t_tp = {}
        a_ev = {}
        g_wr = {}
        a_cc = {}
        c_wr = {}
        nbs = [w // 128 for (_, w) in mat]

        idx_ld = {}

        def emit_idx_load(k):
            c, w = mat[k]
            b = k % 2
            if k >= 2:
                wt("sync", "c", "vector", v_cast[k - 2])
            idx_ld[k] = dma("sync", rawi[:, b * 2048:b * 2048 + w],
                            myidx_d[:, c * 2048:c * 2048 + w])

        emit_idx_load(0)
        emit_idx_load(1)

        def emit_mat(k):
            c, w = mat[k]
            b = k % 2
            nb = nbs[k]
            ld = idx_ld[k]
            wt_d("vector", ld)
            v_cast[k] = ins("vector", lambda E, b=b, w=w: E.tensor_copy(
                out=bfb[:, b * 2048:b * 2048 + w],
                in_=rawi[:, b * 2048:b * 2048 + w]), inc=True)
            if k + 2 < NMAT:
                emit_idx_load(k + 2)
            if k >= 2:
                wt("vector", "c", "tensor", t_tp[(k - 2, nbs[k - 2] - 1)])
            for j16 in range(16):
                sc_ap = lutb[:, j16:j16 + 1]
                if j16 == 0:
                    ins("vector", lambda E, b=b, w=w, jj=j16, s=sc_ap: E.tensor_scalar(
                        out=accb[:, b * 2048:b * 2048 + w],
                        in0=bfb[:, b * 2048:b * 2048 + w],
                        scalar1=float(jj), scalar2=s,
                        op0=ALU.is_equal, op1=ALU.mult))
                else:
                    ins("vector", lambda E, b=b, w=w, jj=j16, s=sc_ap: E.tensor_scalar(
                        out=tmpb[:, 0:w],
                        in0=bfb[:, b * 2048:b * 2048 + w],
                        scalar1=float(jj), scalar2=s,
                        op0=ALU.is_equal, op1=ALU.mult))
                    ins("vector", lambda E, b=b, w=w: E.tensor_add(
                        accb[:, b * 2048:b * 2048 + w],
                        accb[:, b * 2048:b * 2048 + w],
                        tmpb[:, 0:w]))
            v_sel = ins("vector", lambda E: E.engine_nop(), inc=True)
            wt("tensor", "c", "vector", v_sel)
            if k >= 1:
                wt("tensor", "c", "scalar", a_ev[(k - 1, nbs[k - 1] - 1)])
            if k >= 2:
                wt_d("scalar", g_wr[k - 2])
            for j in range(nb):
                slot = j % 2
                if j >= 2:
                    wt("tensor", "c", "scalar", a_ev[(k, j - 2)])
                t_tp[(k, j)] = ins("tensor", lambda E, b=b, j=j, slot=slot: E.transpose(
                    pst[slot][:, 0:128],
                    accb[:, b * 2048 + j * 128:b * 2048 + (j + 1) * 128],
                    identb[:, :]), inc=True)
                wt("scalar", "c", "tensor", t_tp[(k, j)])
                a_ev[(k, j)] = ins("scalar", lambda E, b=b, j=j, slot=slot: E.copy(
                    out=wtb[:, b * 2048 + j * 128:b * 2048 + (j + 1) * 128],
                    in_=pst[slot][:, 0:128]), inc=True)
            wts("scalar", a_ev[(k, nb - 1)])
            g_wr[k] = dma("scalar", sc_part[:, c * 2048:c * 2048 + w],
                          wtb[:, b * 2048:b * 2048 + w])

        def emit_cast(ci):
            wname, wdram, kc = casts[ci]
            b = ci % 2
            if ci >= 2:
                wt("sync", "c", "scalar", a_cc[ci - 2])
            ld = dma("sync", rawf[:, b * 2048:(b + 1) * 2048],
                     wdram[kc * 128:(kc + 1) * 128, :])
            wt_d("scalar", ld)
            if ci >= 2:
                wt_d("scalar", c_wr[ci - 2])
            a_cc[ci] = ins("scalar", lambda E, b=b: E.copy(
                out=cwt[:, b * 2048:(b + 1) * 2048],
                in_=rawf[:, b * 2048:(b + 1) * 2048]), inc=True)
            wts("scalar", a_cc[ci])
            if wname in ("q", "k", "o"):
                scd = {"q": sc_q, "k": sc_k, "o": sc_o}[wname]
                c_wr[ci] = dma(
                    "scalar",
                    scd[:, :, kc * 128:(kc + 1) * 128].rearrange("h p c -> p h c"),
                    cwt[:, b * 2048:(b + 1) * 2048].rearrange("p (h c) -> p h c", c=128))
            else:
                scd = sc_v if wname == "v" else sc_b
                for sl8 in range(8):
                    c_wr[ci] = dma("scalar", scd[sl8, :, kc * 256:(kc + 1) * 256],
                                   cwt[:, b * 2048 + sl8 * 256:b * 2048 + (sl8 + 1) * 256])

        ci_next = 0
        for k in range(NMAT):
            emit_mat(k)
            want = (k + 1) * NCAST // NMAT
            while ci_next < want:
                emit_cast(ci_next)
                ci_next += 1
        while ci_next < NCAST:
            emit_cast(ci_next)
            ci_next += 1

        castB_done = a_cc[NCAST - 1]
        wt_d("gpsimd", [g_wr[k] for k in range(NMAT)])
        v_cc = ins("gpsimd", lambda E: E.collective_compute(
            "AllGather", ALU.bypass, replica_groups=[list(range(8))],
            ins=[sc_part.opt()], outs=[sc_all.opt()]), inc=True)
        for i in range(NP):
            if dpc["scalar"][i]:
                q["scalar"].append(lambda E, h=dpool["scalar"][i], v=dpc["scalar"][i]:
                                   E.wait_ge(h, v))
        cast_all_done = ins("scalar", lambda E: E.nop(), inc=True)

        PARTC = 33024
        GOFF = 0
        UOFF = IT * 2048
        DOFF = 2 * IT * 2048

        def load_image(dst_ap_fn, col0, w):
            toks = []
            off = 0
            while w > 0:
                r = col0 // PARTC
                lc = col0 % PARTC
                take = min(w, PARTC - lc)
                toks.append(dma("sync", dst_ap_fn(off, take),
                                sc_all[r, :, lc:lc + take]))
                col0 += take
                off += take
                w -= take
            return toks
        cast_all_done = ins("scalar", lambda E: E.nop(), inc=True)

        t_o = {}
        t_sup = {}
        a_oev = {}
        a_ktm = {}
        v_att = {}
        v_scast = {}
        v_dn = {}
        v_wo = {}
        t_gu = {}
        t_dn = {}
        ffn_i = [0]
        d_x = {}
        st_x = {}

        def ln_phase(marker):
            mm_first(4)
            vm = None
            for dc in range(KC):
                wt_m("tensor", marker(dc))
                last = dc == KC - 1
                vm = ins("tensor", lambda E, dc=dc, first=(dc == 0), last=last: E.matmul(
                    ps[4][0:1, 0:TS], onesk[:, :], x_sb[:, dc * TS:(dc + 1) * TS],
                    start=first, stop=last), inc=last)
            mm_last(4, vm)
            mm_first(5)
            a_sq = {}
            t_sqm = {}
            for dc in range(KC):
                wt_m("scalar", marker(dc))
                if dc >= 4:
                    wt("scalar", "c", "tensor", t_sqm[dc - 4])
                a_sq[dc] = ins("scalar", lambda E, dc=dc: E.activation(
                    out=sqb[:, (dc % 4) * TS:(dc % 4 + 1) * TS],
                    in_=x_sb[:, dc * TS:(dc + 1) * TS], func=AF.Square), inc=True)
                wt("tensor", "c", "scalar", a_sq[dc])
                last = dc == KC - 1
                t_sqm[dc] = ins("tensor", lambda E, dc=dc, first=(dc == 0), last=last: E.matmul(
                    ps[5][0:1, 0:TS], onesk[:, :],
                    sqb[:, (dc % 4) * TS:(dc % 4 + 1) * TS],
                    start=first, stop=last), inc=True)
            mm_last(5, t_sqm[KC - 1])
            prd(4, "vector", lambda E: E.tensor_copy(out=m_sb, in_=ps[4][0:1, 0:TS]))
            mm_first(4)
            t_bm = ins("tensor", lambda E: E.matmul(
                ps[4][:, 0:TS], ones_c[:, :], m_sb, start=True, stop=True), inc=True)
            mm_last(4, t_bm)
            prd(5, "vector", lambda E: E.tensor_copy(out=ms_sb, in_=ps[5][0:1, 0:TS]))
            ins("vector", lambda E: E.tensor_mul(m2_sb, m_sb, m_sb))
            ins("vector", lambda E: E.tensor_sub(var_sb, ms_sb, m2_sb))
            v_var = ins("vector", lambda E: E.tensor_scalar_add(var_sb, var_sb, EPS), inc=True)
            wt("scalar", "c", "vector", v_var)
            a_ln = ins("scalar", lambda E: E.activation(out=ln_sb, in_=var_sb, func=AF.Ln), inc=True)
            wt("vector", "c", "scalar", a_ln)
            v_l2 = ins("vector", lambda E: E.tensor_scalar_mul(ln_sb, ln_sb, -0.5), inc=True)
            wt("scalar", "c", "vector", v_l2)
            a_rstd = ins("scalar", lambda E: E.activation(
                out=rstd_sb, in_=ln_sb, func=AF.Exp), inc=True)
            wt("tensor", "c", "scalar", a_rstd)
            mm_first(5)
            t_br = ins("tensor", lambda E: E.matmul(
                ps[5][:, 0:TS], ones_c[:, :], rstd_sb, start=True, stop=True), inc=True)
            mm_last(5, t_br)
            wt("vector", "c", "tensor", t_br)
            v_app = []
            for dc in range(KC):
                wt_m("vector", marker(dc))
                ins("vector", lambda E, dc=dc: E.tensor_sub(
                    lntmp[:, :], x_sb[:, dc * TS:(dc + 1) * TS], ps[4][:, 0:TS]))
                v_app.append(ins("vector", lambda E, dc=dc: E.tensor_mul(
                    xln[:, dc * TS:(dc + 1) * TS], lntmp[:, :], ps[5][:, 0:TS]), inc=True))
            pbank[4]["r"]["vector"] = v_app[-1]
            pbank[5]["r"]["vector"] = v_app[-1]
            return v_app

        a_silu = {}
        v_h = {}

        def ffn_phase(v_xln):
            fi = ffn_i[0]
            ffn_i[0] += 1
            t_enter = cnt["tensor"]
            v_enter = cnt["vector"]
            d_g = {}
            d_u = {}

            def emit_gu_load(it):
                b = it % 3
                if it >= 3:
                    wt("sync", "c", "tensor", t_gu[(fi, it - 3)])
                else:
                    wt("sync", "c", "tensor", t_enter)
                wt("sync", "c", "gpsimd", v_cc)
                d_g[it] = load_image(
                    lambda off, take, b=b: wg[b][:, off:off + take],
                    GOFF + it * 2048, 2048)
                d_u[it] = load_image(
                    lambda off, take, b=b: wu[b][:, off:off + take],
                    UOFF + it * 2048, 2048)

            d_d = {}

            def emit_d_load(dt):
                b = dt % 2
                if dt >= 2:
                    wt("sync", "c", "tensor", t_dn[(fi, dt - 2)])
                elif dt == 1:
                    wt("sync", "c", "tensor", t_gu[(fi, IT - 2)])
                else:
                    wt("sync", "c", "tensor", t_enter)
                wt("sync", "c", "gpsimd", v_cc)
                d_d[dt] = load_image(
                    lambda off, take, b=b: wd[b][:, off:off + take],
                    DOFF + dt * INTER, INTER)

            emit_gu_load(0)
            emit_gu_load(1)
            emit_gu_load(2)
            for it in range(IT):
                b = it % 3
                b2 = it % 2
                bg, bu = 2 * b2, 2 * b2 + 1
                wt_d("tensor", d_g[it] + d_u[it])
                mm_first(bg)
                tg = None
                for kc in range(KC):
                    wt("tensor", "c", "vector", v_xln[kc])
                    last = kc == KC - 1
                    tg = ins("tensor", lambda E, kc=kc, b=b, bg=bg,
                             first=(kc == 0), last=last: E.matmul(
                        ps[bg][:, 0:TS], wg[b][:, kc * 128:(kc + 1) * 128],
                        xln[:, kc * TS:(kc + 1) * TS],
                        start=first, stop=last), inc=last)
                mm_last(bg, tg)
                mm_first(bu)
                tu = None
                for kc in range(KC):
                    last = kc == KC - 1
                    tu = ins("tensor", lambda E, kc=kc, b=b, bu=bu,
                             first=(kc == 0), last=last: E.matmul(
                        ps[bu][:, 0:TS], wu[b][:, kc * 128:(kc + 1) * 128],
                        xln[:, kc * TS:(kc + 1) * TS],
                        start=first, stop=last), inc=last)
                mm_last(bu, tu)
                t_gu[(fi, it)] = tu
                if it >= 2:
                    wt("scalar", "c", "vector", v_h[(fi, it - 2)])
                else:
                    wt("scalar", "c", "vector", v_enter)
                a_silu[(fi, it)] = prd(bg, "scalar", lambda E, b2=b2, bg=bg: E.activation(
                    out=sg[:, b2 * TS:(b2 + 1) * TS], in_=ps[bg][:, 0:TS], func=AF.Silu))
                wt("vector", "c", "scalar", a_silu[(fi, it)])
                if fi == 0:
                    wt("vector", "c", "scalar", cast_all_done)
                v_h[(fi, it)] = prd(bu, "vector", lambda E, it=it, b2=b2, bu=bu: E.tensor_mul(
                    h_sl[:, it * TS:(it + 1) * TS], sg[:, b2 * TS:(b2 + 1) * TS],
                    ps[bu][:, 0:TS]))
                if it + 3 < IT:
                    emit_gu_load(it + 3)
                if it == 6:
                    emit_d_load(0)
            emit_d_load(1)
            for dt in range(KC):
                b = dt % 2
                bank = dt % 2
                wt_d("tensor", d_d[dt])
                wt("tensor", "c", "vector", v_h[(fi, IT - 1)])
                mm_first(bank)
                td = None
                for ic in range(IT):
                    last = ic == IT - 1
                    td = ins("tensor", lambda E, ic=ic, b=b, bank=bank,
                             first=(ic == 0), last=last: E.matmul(
                        ps[bank][:, 0:TS], wd[b][:, ic * 128:(ic + 1) * 128],
                        h_sl[:, ic * TS:(ic + 1) * TS],
                        start=first, stop=last), inc=last)
                mm_last(bank, td)
                t_dn[(fi, dt)] = td
                v_dn[(fi, dt)] = prd(bank, "vector", lambda E, dt=dt, bank=bank:
                                     E.scalar_tensor_tensor(
                    out=x_sb[:, dt * TS:(dt + 1) * TS],
                    in0=ps[bank][:, 0:TS], scalar=0.5,
                    in1=x_sb[:, dt * TS:(dt + 1) * TS],
                    op0=ALU.mult, op1=ALU.add))
                if dt + 2 < KC:
                    emit_d_load(dt + 2)
            return fi

        def attn_phase(s, v_xln):
            t_enter = cnt["tensor"]
            v_enter = cnt["vector"]
            t_qk = {}
            a_qk = {}
            d_qk = {}
            qk_srcs = [(sc_q, q_fm, hh) for hh in range(NH)] + \
                      [(sc_k, k_fm, hh) for hh in range(NH)]

            def emit_qk_load(i):
                scd, dstv, hh = qk_srcs[i]
                b = i % 2
                if i >= 2:
                    wt("sync", "c", "tensor", t_qk[i - 2])
                else:
                    wt("sync", "c", "tensor", t_enter)
                wt("sync", "c", "scalar", cast_all_done)
                d_qk[i] = dma("sync", qkw[b][:, :], scd[hh, :, :])

            v_u_last = 0
            t_vb_last = {}
            a_sig_first = [True]
            d_vb = {}
            vb4 = [vbw[0][:, 0:4096], vbw[0][:, 4096:8192],
                   vbw[1][:, 0:4096], vbw[1][:, 4096:8192]]

            def emit_vb_load(sl8):
                if sl8 >= 2:
                    wt("sync", "c", "tensor", t_vb_last[sl8 - 2])
                else:
                    wt("sync", "c", "tensor", t_enter)
                bq = (sl8 % 2) * 2
                d_vb[sl8] = [dma("sync", vb4[bq], sc_v[sl8]),
                             dma("sync", vb4[bq + 1], sc_b[sl8])]

            emit_qk_load(0)
            emit_qk_load(1)
            emit_vb_load(0)
            emit_vb_load(1)
            for i in range(2 * NH):
                scd, dstv, hh = qk_srcs[i]
                b = i % 2
                bank = i % 2
                wt_d("tensor", d_qk[i])
                mm_first(bank)
                tl = None
                for kc in range(KC):
                    wt("tensor", "c", "vector", v_xln[kc])
                    last = kc == KC - 1
                    tl = ins("tensor", lambda E, kc=kc, b=b, bank=bank,
                             first=(kc == 0), last=last: E.matmul(
                        ps[bank][:, 0:TS], qkw[b][:, kc * 128:(kc + 1) * 128],
                        xln[:, kc * TS:(kc + 1) * TS],
                        start=first, stop=last), inc=last)
                mm_last(bank, tl)
                t_qk[i] = tl
                a_qk[i] = prd(bank, "scalar", lambda E, hh=hh, bank=bank, dstv=dstv:
                              E.copy(out=dstv[:, hh * TS:(hh + 1) * TS],
                                     in_=ps[bank][:, 0:TS]))
                if i + 2 < 2 * NH:
                    emit_qk_load(i + 2)
            for sl8 in range(8):
                bq = (sl8 % 2) * 2
                for tt in range(NCH):
                    bv, bb = 2 * (tt % 2), 2 * (tt % 2) + 1
                    wt_d("tensor", d_vb[sl8])
                    mm_first(bv)
                    tv = None
                    for kc in range(KC):
                        last = kc == KC - 1
                        tv = ins("tensor", lambda E, kc=kc, tt=tt, bv=bv, bq=bq,
                                 first=(kc == 0), last=last: E.matmul(
                            ps[bv][:, 0:256],
                            xln[:, kc * TS + tt * 128:kc * TS + (tt + 1) * 128],
                            vb4[bq][:, kc * 256:(kc + 1) * 256],
                            start=first, stop=last), inc=last)
                    mm_last(bv, tv)
                    mm_first(bb)
                    tb = None
                    for kc in range(KC):
                        last = kc == KC - 1
                        tb = ins("tensor", lambda E, kc=kc, tt=tt, bb=bb, bq=bq,
                                 first=(kc == 0), last=last: E.matmul(
                            ps[bb][:, 0:256],
                            xln[:, kc * TS + tt * 128:kc * TS + (tt + 1) * 128],
                            vb4[bq + 1][:, kc * 256:(kc + 1) * 256],
                            start=first, stop=last), inc=last)
                    mm_last(bb, tb)
                    t_vb_last[sl8] = tb
                    if a_sig_first[0]:
                        wt("scalar", "c", "vector", v_enter)
                        a_sig_first[0] = False
                    else:
                        wt("scalar", "c", "vector", v_u_last)
                    a_sig = prd(bb, "scalar", lambda E, bb=bb: E.activation(
                        out=sg[:, 0:256], in_=ps[bb][:, 0:256], func=AF.Sigmoid))
                    wt("vector", "c", "scalar", a_sig)
                    v_u_last = prd(bv, "vector", lambda E, sl8=sl8, tt=tt, bv=bv:
                                   E.tensor_mul(
                        u_tm[:, tt * D + sl8 * 256:tt * D + (sl8 + 1) * 256],
                        sg[:, 0:256], ps[bv][:, 0:256]))
                if sl8 + 2 < 8:
                    emit_vb_load(sl8 + 2)
            for bk in (0, 1, 2, 3, 4, 5, "t0", "t1"):
                for oe, v in pbank[bk]["r"].items():
                    wt("tensor", "c", oe, v)
                pbank[bk]["r"] = {}

            def head_tail(gc, c, hh):
                bo = 2 if hh % 2 == 0 else 5
                wt("tensor", "c", "vector", v_att[(gc, hh)])
                if (gc, hh - 2) in a_oev:
                    wt("tensor", "c", "scalar", a_oev[(gc, hh - 2)])
                elif (gc - 1, NH - 2 + (hh % 2)) in a_oev:
                    wt("tensor", "c", "scalar", a_oev[(gc - 1, NH - 2 + (hh % 2))])
                if (gc - 1, hh) in v_scast:
                    wt("tensor", "c", "vector", v_scast[(gc - 1, hh)])
                ins("tensor", lambda E, c=c, hh=hh, bo=bo: E.matmul(
                    ps[bo][:, 0:128],
                    u_tm[:, c * D + hh * HD:c * D + (hh + 1) * HD],
                    attm[:, (hh % 2) * 128:(hh % 2 + 1) * 128],
                    start=True, stop=False))
                t_o[(gc, hh)] = ins("tensor", lambda E, c=c, hh=hh, bo=bo: E.matmul(
                    ps[bo][:, 0:128],
                    S_bf[:, hh * HD:(hh + 1) * HD],
                    q_fm[:, hh * TS + c * 128:hh * TS + (c + 1) * 128],
                    start=False, stop=True), inc=True)
                bs = 3 + (hh % 2)
                wt("tensor", "c", "scalar", a_ktm[(gc, hh)])
                if (gc, hh - 2) in v_scast:
                    wt("tensor", "c", "vector", v_scast[(gc, hh - 2)])
                elif c > 0:
                    wt("tensor", "c", "vector", v_scast[(gc - 1, NH - 2 + (hh % 2))])
                t_sup[(gc, hh)] = ins("tensor", lambda E, c=c, hh=hh, bs=bs: E.matmul(
                    ps[bs][:, 0:128],
                    ktm[:, (hh % 2) * 128:(hh % 2 + 1) * 128],
                    u_tm[:, c * D + hh * HD:c * D + (hh + 1) * HD],
                    start=True, stop=True), inc=True)
                wt("scalar", "c", "tensor", t_o[(gc, hh)])
                a_oev[(gc, hh)] = ins("scalar", lambda E, c=c, hh=hh, bo=bo: E.copy(
                    out=o_fm[:, hh * TS + c * 128:hh * TS + (c + 1) * 128],
                    in_=ps[bo][:, 0:128]), inc=True)
                wt("vector", "c", "tensor", t_sup[(gc, hh)])
                ins("vector", lambda E, hh=hh, bs=bs: E.tensor_add(
                    S_f32[:, hh * HD:(hh + 1) * HD],
                    S_f32[:, hh * HD:(hh + 1) * HD],
                    ps[bs][:, 0:128]))
                v_scast[(gc, hh)] = ins("vector", lambda E, hh=hh: E.tensor_copy(
                    out=S_bf[:, hh * HD:(hh + 1) * HD],
                    in_=S_f32[:, hh * HD:(hh + 1) * HD]), inc=True)

            for c in range(NCH):
                gc = s * NCH + c
                for hh in range(NH):
                    slot = hh % 2
                    ba = hh % 2
                    if (gc, hh - 2) in a_ktm:
                        wt("tensor", "c", "scalar", a_ktm[(gc, hh - 2)])
                    elif (gc - 1, NH - 2 + slot) in a_ktm:
                        wt("tensor", "c", "scalar", a_ktm[(gc - 1, NH - 2 + slot)])
                    wt("tensor", "c", "scalar", a_qk[NH + hh])
                    wt("tensor", "c", "vector", v_u_last)
                    t_ktp = ins("tensor", lambda E, c=c, hh=hh, slot=slot: E.transpose(
                        pst[slot][:, 0:128],
                        k_fm[:, hh * TS + c * 128:hh * TS + (c + 1) * 128],
                        identb[:, :]), inc=True)
                    if (gc, hh - 2) in v_att:
                        wt("tensor", "c", "vector", v_att[(gc, hh - 2)])
                    elif c > 0:
                        wt("tensor", "c", "vector", v_att[(gc - 1, NH - 2 + ba)])
                    wt("tensor", "c", "scalar", a_qk[hh])
                    t_att = ins("tensor", lambda E, c=c, hh=hh, ba=ba: E.matmul(
                        ps[ba][:, 0:128],
                        k_fm[:, hh * TS + c * 128:hh * TS + (c + 1) * 128],
                        q_fm[:, hh * TS + c * 128:hh * TS + (c + 1) * 128],
                        start=True, stop=True), inc=True)
                    wt("scalar", "c", "tensor", t_ktp)
                    if (gc, hh - 2) in t_sup:
                        wt("scalar", "c", "tensor", t_sup[(gc, hh - 2)])
                    elif (gc - 1, NH - 2 + slot) in t_sup:
                        wt("scalar", "c", "tensor", t_sup[(gc - 1, NH - 2 + slot)])
                    a_ktm[(gc, hh)] = ins("scalar", lambda E, slot=slot: E.copy(
                        out=ktm[:, slot * 128:(slot + 1) * 128],
                        in_=pst[slot][:, 0:128]), inc=True)
                    wt("vector", "c", "tensor", t_att)
                    if (gc, hh - 2) in t_o:
                        wt("vector", "c", "tensor", t_o[(gc, hh - 2)])
                    elif (gc - 1, NH - 2 + ba) in t_o:
                        wt("vector", "c", "tensor", t_o[(gc - 1, NH - 2 + ba)])
                    v_att[(gc, hh)] = ins("vector", lambda E, ba=ba, slot=slot: E.tensor_mul(
                        attm[:, slot * 128:(slot + 1) * 128],
                        ps[ba][:, 0:128], maskc[:, :]), inc=True)
                    if hh > 0:
                        head_tail(gc, c, hh - 1)
                head_tail(gc, c, NH - 1)
            for bk in (0, 1, 2, 3, 4, 5):
                pbank[bk]["w"] = ("tensor", cnt["tensor"])
            pbank[0]["r"]["vector"] = cnt["vector"]
            pbank[1]["r"]["vector"] = cnt["vector"]
            pbank[2]["r"]["scalar"] = cnt["scalar"]
            pbank[5]["r"]["scalar"] = cnt["scalar"]
            pbank[3]["r"]["vector"] = cnt["vector"]
            pbank[4]["r"]["vector"] = cnt["vector"]
            pbank["t0"]["r"]["scalar"] = cnt["scalar"]
            pbank["t1"]["r"]["scalar"] = cnt["scalar"]

            gcl = s * NCH + NCH - 1
            t_wo = {}
            d_wo = {}

            def emit_wo_load(dt):
                b = dt % 2
                if dt >= 2:
                    wt("sync", "c", "tensor", t_wo[dt - 2])
                else:
                    wt("sync", "c", "tensor", t_qk[2 * NH - 2 + b])
                d_wo[dt] = dma("sync", qkw[b][:, :], sc_o[dt, :, :])

            emit_wo_load(0)
            emit_wo_load(1)
            for dt in range(KC):
                b = dt % 2
                bank = dt % 2
                wt_d("tensor", d_wo[dt])
                wt("tensor", "c", "scalar", a_oev[(gcl, NH - 1)])
                mm_first(bank)
                tw = None
                for kc in range(KC):
                    last = kc == KC - 1
                    tw = ins("tensor", lambda E, kc=kc, b=b, bank=bank,
                             first=(kc == 0), last=last: E.matmul(
                        ps[bank][:, 0:TS], qkw[b][:, kc * 128:(kc + 1) * 128],
                        o_fm[:, kc * TS:(kc + 1) * TS],
                        start=first, stop=last), inc=last)
                mm_last(bank, tw)
                t_wo[dt] = tw
                v_wo[(s, dt)] = prd(bank, "vector", lambda E, dt=dt, bank=bank:
                                    E.tensor_add(
                    x_sb[:, dt * TS:(dt + 1) * TS],
                    x_sb[:, dt * TS:(dt + 1) * TS],
                    ps[bank][:, 0:TS]))
                if dt + 2 < KC:
                    emit_wo_load(dt + 2)

        for s in range(NSLAB):
            for dc in range(KC):
                if s == 0:
                    wt("sync", "c", "scalar", castB_done)
                else:
                    wt_d("sync", st_x[(s - 1, dc)])
                d_x[(s, dc)] = dma("sync", x_sb[:, dc * TS:(dc + 1) * TS],
                                   x_in[dc * 128:(dc + 1) * 128, s * TS:(s + 1) * TS])

            v_xln1 = ln_phase(lambda dc, s=s: ("d", d_x[(s, dc)]))
            fi1 = ffn_phase(v_xln1)
            v_xln2 = ln_phase(lambda dc, fi=fi1: ("c", "vector", v_dn[(fi, dc)]))
            attn_phase(s, v_xln2)
            v_xln3 = ln_phase(lambda dc, s=s: ("c", "vector", v_wo[(s, dc)]))
            fi2 = ffn_phase(v_xln3)

            for dc in range(KC):
                wt("sync", "c", "vector", v_dn[(fi2, dc)])
                st_x[(s, dc)] = dma("sync",
                                    y_out[dc * 128:(dc + 1) * 128, s * TS:(s + 1) * TS],
                                    x_sb[:, dc * TS:(dc + 1) * TS])

        for e in DQUEUES:
            for i in range(NP):
                if dpc[e][i]:
                    q[e].append(lambda E, h=dpool[e][i], v=dpc[e][i]: E.wait_ge(h, v))

        block = es.enter_context(nc.Block())

        @block.sync
        def _(E):
            for f in q["sync"]:
                f(E)

        @block.gpsimd
        def _(E):
            for f in q["gpsimd"]:
                f(E)

        @block.vector
        def _(E):
            for f in q["vector"]:
                f(E)

        @block.scalar
        def _(E):
            for f in q["scalar"]:
                f(E)

        @block.tensor
        def _(E):
            for f in q["tensor"]:
                f(E)

    return nc


_CACHE = {}


def kernel(L0_attn_hidden_states, L0_attn_Wq, L0_attn_Wk, L0_attn_Wv,
           L0_attn_Wb, L0_attn_Wo, L0_ffn_lut, L0_ffn_gate_idx,
           L0_ffn_up_idx, L0_ffn_down_idx):
    x = np.asarray(L0_attn_hidden_states, dtype=np.float32)
    gidx = np.asarray(L0_ffn_gate_idx, dtype=np.int32)
    uidx = np.asarray(L0_ffn_up_idx, dtype=np.int32)
    didx = np.asarray(L0_ffn_down_idx, dtype=np.int32)
    gb = gidx.reshape(IT, 128, KC, 128).transpose(0, 2, 1, 3).reshape(-1, 128, 128)
    ub = uidx.reshape(IT, 128, KC, 128).transpose(0, 2, 1, 3).reshape(-1, 128, 128)
    db = didx.reshape(KC, 128, IT, 128).transpose(0, 2, 1, 3).reshape(-1, 128, 128)
    blocks = np.concatenate([gb, ub, db], axis=0)
    base = {
        "wq": np.ascontiguousarray(np.asarray(L0_attn_Wq, dtype=np.float32)),
        "wk": np.ascontiguousarray(np.asarray(L0_attn_Wk, dtype=np.float32)),
        "wv": np.ascontiguousarray(np.asarray(L0_attn_Wv, dtype=np.float32)),
        "wb": np.ascontiguousarray(np.asarray(L0_attn_Wb, dtype=np.float32)),
        "wo": np.ascontiguousarray(np.asarray(L0_attn_Wo, dtype=np.float32)),
        "lut": np.ascontiguousarray(np.asarray(L0_ffn_lut, dtype=np.float32).reshape(1, 16)),
    }
    if "nc" not in _CACHE:
        _CACHE["nc"] = build()
    nc = _CACHE["nc"]
    in_maps = []
    for c in range(8):
        m = dict(base)
        m["x_in"] = np.ascontiguousarray(x[c].T)
        m["myidx"] = np.ascontiguousarray(
            blocks[c * 258:(c + 1) * 258].transpose(1, 0, 2).reshape(128, 33024))
        in_maps.append(m)
    res = run_bass_kernel_spmd(nc, in_maps, core_ids=list(range(8)))
    out = np.stack([np.asarray(res.results[c]["y"]).T for c in range(8)], axis=0)
    return np.ascontiguousarray(out.astype(np.float32))


# version 23
# speedup vs baseline: 1.0895x; 1.0279x over previous
import numpy as np
import contextlib
import concourse.bass as bass
import concourse.mybir as mybir
from concourse.bass_utils import run_bass_kernel_spmd

F32 = mybir.dt.float32
BF16 = mybir.dt.bfloat16
I32 = mybir.dt.int32
AF = mybir.ActivationFunctionType
ALU = mybir.AluOpType

D = 2048
SEQ = 2048
NH = 16
HD = 128
INTER = 5504
KC = D // 128
IT = INTER // 128
TS = 512
NSLAB = SEQ // TS
CH = 128
NCH = TS // CH
EPS = 1e-5

ENGS = ("sync", "gpsimd", "vector", "scalar", "tensor")
DQUEUES = ("sync", "scalar", "gpsimd")

DEBUG_EDGES = False


def build():
    nc = bass.Bass("TRN2", target_bir_lowering=False,
                   detect_race_conditions=DEBUG_EDGES)

    x_in = nc.dram_tensor("x_in", [D, SEQ], F32, kind="ExternalInput").ap()
    wq_d = nc.dram_tensor("wq", [D, D], F32, kind="ExternalInput").ap()
    wk_d = nc.dram_tensor("wk", [D, D], F32, kind="ExternalInput").ap()
    wv_d = nc.dram_tensor("wv", [D, D], F32, kind="ExternalInput").ap()
    wb_d = nc.dram_tensor("wb", [D, D], F32, kind="ExternalInput").ap()
    wo_d = nc.dram_tensor("wo", [D, D], F32, kind="ExternalInput").ap()
    lut_d = nc.dram_tensor("lut", [1, 16], F32, kind="ExternalInput").ap()
    y_out = nc.dram_tensor("y", [D, SEQ], F32, kind="ExternalOutput").ap()

    myidx_d = nc.dram_tensor("myidx", [128, 33024], I32, kind="ExternalInput").ap()
    sc_part = nc.dram_tensor("sc_part", [128, 33024], BF16, kind="Internal").ap()
    sc_all = nc.dram_tensor("sc_all", [8, 128, 33024], BF16, kind="Internal",
                            addr_space="Shared").ap()
    sc_q = nc.dram_tensor("sc_q", [NH, 128, D], BF16, kind="Internal").ap()
    sc_k = nc.dram_tensor("sc_k", [NH, 128, D], BF16, kind="Internal").ap()
    sc_o = nc.dram_tensor("sc_o", [KC, 128, D], BF16, kind="Internal").ap()
    sc_v = nc.dram_tensor("sc_v", [8, 128, KC * 256], BF16, kind="Internal").ap()
    sc_b = nc.dram_tensor("sc_b", [8, 128, KC * 256], BF16, kind="Internal").ap()

    es = contextlib.ExitStack()
    with es:
        sbuf = lambda name, shp, dt: es.enter_context(nc.sbuf_tensor(name, shp, dt))
        psal = lambda name, shp, dt: es.enter_context(nc.psum_tensor(name, shp, dt))

        x_sb = sbuf("x_sb", [128, KC * TS], F32)
        xln = sbuf("xln", [128, KC * TS], BF16)
        pool = sbuf("pool", [128, 24576], BF16)
        h_sl = pool[:, 0:IT * TS]
        q_fm = pool[:, 0:8192]
        k_fm = pool[:, 8192:16384]
        u_tm = pool[:, 16384:24576]
        o_fm = sbuf("o_fm", [128, 8192], BF16)
        S_f32 = sbuf("S_f32", [128, NH * HD], F32)
        S_bf = sbuf("S_bf", [128, NH * HD], BF16)
        wpool = sbuf("wpool", [128, 20480], BF16)
        wg = [wpool[:, 0:2048], wpool[:, 2048:4096], wpool[:, 16384:18432]]
        wu = [wpool[:, 4096:6144], wpool[:, 6144:8192], wpool[:, 18432:20480]]
        wd = [wpool[:, 8192:8192 + INTER], wpool[:, 8192 + INTER:8192 + 2 * INTER]]
        qkw = [wpool[:, 0:2048], wpool[:, 2048:4096]]
        vbw = [wpool[:, 4096:12288], wpool[:, 12288:20480]]
        sg = sbuf("sg", [128, 2 * TS], BF16)
        sqb = sbuf("sqb", [128, 4 * TS], F32)
        lntmp = sbuf("lntmp", [128, TS], F32)
        ktm = sbuf("ktm", [128, 256], BF16)
        attm = sbuf("attm", [128, 256], BF16)
        stat = sbuf("stat", [1, 6 * TS], F32)
        m_sb = stat[0:1, 0:TS]
        ms_sb = stat[0:1, TS:2 * TS]
        m2_sb = stat[0:1, 2 * TS:3 * TS]
        var_sb = stat[0:1, 3 * TS:4 * TS]
        ln_sb = stat[0:1, 4 * TS:5 * TS]
        rstd_sb = stat[0:1, 5 * TS:6 * TS]
        lut_sb = sbuf("lut_sb", [1, 16], F32)
        lutb = sbuf("lutb", [128, 16], F32)
        ones_c = sbuf("ones_c", [1, 128], F32)
        onesk = sbuf("onesk", [128, 1], F32)
        maskc = sbuf("maskc", [128, 128], BF16)
        identb = sbuf("identb", [128, 128], BF16)
        rawi = sbuf("rawi", [128, 4096], I32)
        bfb = pool[:, 0:4096]
        accb = pool[:, 4096:8192]
        tmpb = pool[:, 8192:10240]
        wtb = pool[:, 10240:14336]
        cwt = pool[:, 14336:18432]
        rawf = x_sb[:, 0:4096]

        ps = [psal(f"ps{i}", [128, 512], F32) for i in range(6)]
        pst = [psal("pst0", [128, 1024], BF16), psal("pst1", [128, 1024], BF16)]

        csem = {e: es.enter_context(nc.semaphore("c_" + e)) for e in ENGS}
        NP = 1 if DEBUG_EDGES else 8
        dpool = {e: [es.enter_context(nc.semaphore(f"d_{e}{i}")) for i in range(NP)]
                 for e in DQUEUES}
        dpc = {e: [0] * NP for e in DQUEUES}
        dseq = {e: 0 for e in DQUEUES}
        cnt = {e: 0 for e in ENGS}
        q = {e: [] for e in ENGS}
        lastw = {e: {} for e in ENGS}

        def ins(e, f, inc=False):
            if DEBUG_EDGES:
                prev = cnt[e]
                h = csem[e]
                if prev:
                    q[e].append(lambda E, h=h, v=prev: E.wait_ge(h, v))
                cnt[e] += 1
                v = cnt[e]
                q[e].append(lambda E, f=f, h=h: f(E).then_inc(h, 1))
                return v
            if inc:
                cnt[e] += 1
                v = cnt[e]
                h = csem[e]
                q[e].append(lambda E, f=f, h=h: f(E).then_inc(h, 1))
                return v
            q[e].append(f)
            return None

        def dma(e, out_ap, in_ap):
            i = dseq[e] % NP
            dseq[e] += 1
            h = dpool[e][i]
            if dpc[e][i]:
                q[e].append(lambda E, h=h, v=dpc[e][i]: E.wait_ge(h, v))
            dpc[e][i] += 16
            v = dpc[e][i]
            q[e].append(lambda E, o=out_ap, i_=in_ap, h=h:
                        E.dma_start(out=o, in_=i_).then_inc(h, 16))
            return (e, i, v)

        def wt(e, kind, oe, v):
            if not v or v <= 0:
                return
            if kind == "c" and oe == e:
                return
            key = (kind, oe)
            if lastw[e].get(key, 0) >= v:
                return
            lastw[e][key] = v
            h = csem[oe]
            q[e].append(lambda E, h=h, v=v: E.wait_ge(h, v))

        def wts(e, v):
            if not v:
                return
            key = ("self",)
            if lastw[e].get(key, 0) >= v:
                return
            lastw[e][key] = v
            q[e].append(lambda E, h=csem[e], v=v: E.wait_ge(h, v))

        def wt_d(e, tokens):
            if tokens is None:
                return
            if isinstance(tokens, tuple):
                tokens = [tokens]
            for (oe, i, v) in tokens:
                key = ("d", oe, i)
                if lastw[e].get(key, 0) >= v:
                    continue
                lastw[e][key] = v
                h = dpool[oe][i]
                q[e].append(lambda E, h=h, v=v: E.wait_ge(h, v))

        def wt_m(e, m):
            if m[0] == "c":
                wt(e, "c", m[1], m[2])
            else:
                wt_d(e, m[1])

        pbank = {i: {"w": ("tensor", 0), "r": {}}
                 for i in list(range(6)) + ["t0", "t1"]}

        def mm_first(bank):
            for oe, v in pbank[bank]["r"].items():
                wt("tensor", "c", oe, v)
            pbank[bank]["r"] = {}

        def mm_last(bank, v):
            pbank[bank]["w"] = ("tensor", v)

        def prd(bank, e, f):
            we, wv = pbank[bank]["w"]
            wt(e, "c", we, wv)
            v = ins(e, f, inc=True)
            pbank[bank]["r"][e] = max(pbank[bank]["r"].get(e, 0), v)
            return v

        ins("gpsimd", lambda E: E.memset(ones_c[:, :], 1.0))
        ins("gpsimd", lambda E: E.memset(onesk[:, :], 1.0 / D))
        g1 = ins("gpsimd", lambda E: E.memset(identb[:, :], 0.0), inc=True)
        q["gpsimd"].append(lambda E, h=csem["gpsimd"], v=g1: E.wait_ge(h, v))
        ins("gpsimd", lambda E: E.affine_select(
            out=identb[:, :], in_=identb[:, :], compare_op=ALU.not_equal,
            fill=1.0, base=0, pattern=[[-1, 128]], channel_multiplier=1))
        g2 = ins("gpsimd", lambda E: E.memset(maskc[:, :], 1.0), inc=True)
        q["gpsimd"].append(lambda E, h=csem["gpsimd"], v=g2: E.wait_ge(h, v))
        ins("gpsimd", lambda E: E.affine_select(
            out=maskc[:, :], in_=maskc[:, :], compare_op=ALU.is_ge,
            fill=0.0, base=0, pattern=[[1, 128]], channel_multiplier=-1))
        ins("gpsimd", lambda E: E.memset(S_f32[:, :], 0.0))
        v_init = ins("gpsimd", lambda E: E.memset(S_bf[:, :], 0.0), inc=True)

        d_lut = dma("sync", lut_sb[:, :], lut_d)
        wt_d("tensor", d_lut)
        wt("tensor", "c", "gpsimd", v_init)
        mm_first(5)
        t_lut = ins("tensor", lambda E: E.matmul(
            ps[5][:, 0:16], ones_c[:, :], lut_sb[:, :], start=True, stop=True), inc=True)
        mm_last(5, t_lut)
        v_lut = prd(5, "vector", lambda E: E.tensor_copy(out=lutb[:, :], in_=ps[5][:, 0:16]))
        q["vector"].append(lambda E, h=csem["vector"], v=v_lut: E.wait_ge(h, v))
        wt("vector", "c", "gpsimd", v_init)

        mat = [(c, 2048) for c in range(16)] + [(16, 256)]
        NMAT = len(mat)

        cast_toks = {wn: [] for wn in ("q", "k", "v", "b", "o")}
        for wn, wdram in (("q", wq_d), ("k", wk_d), ("v", wv_d), ("b", wb_d),
                          ("o", wo_d)):
            for kc in range(KC):
                if wn in ("q", "k", "o"):
                    scd = {"q": sc_q, "k": sc_k, "o": sc_o}[wn]
                    tok = dma(
                        "gpsimd",
                        scd[:, :, kc * 128:(kc + 1) * 128].rearrange("h p c -> p h c"),
                        wdram[kc * 128:(kc + 1) * 128, :].rearrange(
                            "p (h c) -> p h c", c=128))
                else:
                    scd = sc_v if wn == "v" else sc_b
                    tok = dma(
                        "gpsimd",
                        scd[:, :, kc * 256:(kc + 1) * 256].rearrange("s p c -> p s c"),
                        wdram[kc * 128:(kc + 1) * 128, :].rearrange(
                            "p (s c) -> p s c", c=256))
                cast_toks[wn].append(tok)

        v_cast = {}
        t_tp = {}
        a_ev = {}
        g_wr = {}
        nbs = [w // 128 for (_, w) in mat]

        idx_ld = {}

        def emit_idx_load(k):
            c, w = mat[k]
            b = k % 2
            if k >= 2:
                wt("sync", "c", "vector", v_cast[k - 2])
            idx_ld[k] = dma("sync", rawi[:, b * 2048:b * 2048 + w],
                            myidx_d[:, c * 2048:c * 2048 + w])

        emit_idx_load(0)
        emit_idx_load(1)

        def emit_mat(k):
            c, w = mat[k]
            b = k % 2
            nb = nbs[k]
            ld = idx_ld[k]
            wt_d("vector", ld)
            v_cast[k] = ins("vector", lambda E, b=b, w=w: E.tensor_copy(
                out=bfb[:, b * 2048:b * 2048 + w],
                in_=rawi[:, b * 2048:b * 2048 + w]), inc=True)
            if k + 2 < NMAT:
                emit_idx_load(k + 2)
            if k >= 2:
                wt("vector", "c", "tensor", t_tp[(k - 2, nbs[k - 2] - 1)])
            for j16 in range(16):
                sc_ap = lutb[:, j16:j16 + 1]
                if j16 == 0:
                    ins("vector", lambda E, b=b, w=w, jj=j16, s=sc_ap: E.tensor_scalar(
                        out=accb[:, b * 2048:b * 2048 + w],
                        in0=bfb[:, b * 2048:b * 2048 + w],
                        scalar1=float(jj), scalar2=s,
                        op0=ALU.is_equal, op1=ALU.mult))
                else:
                    ins("vector", lambda E, b=b, w=w, jj=j16, s=sc_ap: E.tensor_scalar(
                        out=tmpb[:, 0:w],
                        in0=bfb[:, b * 2048:b * 2048 + w],
                        scalar1=float(jj), scalar2=s,
                        op0=ALU.is_equal, op1=ALU.mult))
                    ins("vector", lambda E, b=b, w=w: E.tensor_add(
                        accb[:, b * 2048:b * 2048 + w],
                        accb[:, b * 2048:b * 2048 + w],
                        tmpb[:, 0:w]))
            v_sel = ins("vector", lambda E: E.engine_nop(), inc=True)
            wt("tensor", "c", "vector", v_sel)
            if k >= 1:
                wt("tensor", "c", "scalar", a_ev[(k - 1, nbs[k - 1] - 1)])
            if k >= 2:
                wt_d("scalar", g_wr[k - 2])
            for j in range(nb):
                slot = j % 2
                if j >= 2:
                    wt("tensor", "c", "scalar", a_ev[(k, j - 2)])
                t_tp[(k, j)] = ins("tensor", lambda E, b=b, j=j, slot=slot: E.transpose(
                    pst[slot][:, 0:128],
                    accb[:, b * 2048 + j * 128:b * 2048 + (j + 1) * 128],
                    identb[:, :]), inc=True)
                wt("scalar", "c", "tensor", t_tp[(k, j)])
                a_ev[(k, j)] = ins("scalar", lambda E, b=b, j=j, slot=slot: E.copy(
                    out=wtb[:, b * 2048 + j * 128:b * 2048 + (j + 1) * 128],
                    in_=pst[slot][:, 0:128]), inc=True)
            wts("scalar", a_ev[(k, nb - 1)])
            g_wr[k] = dma("scalar", sc_part[:, c * 2048:c * 2048 + w],
                          wtb[:, b * 2048:b * 2048 + w])

        for k in range(NMAT):
            emit_mat(k)

        wt_d("gpsimd", [g_wr[k] for k in range(NMAT)])
        v_cc = ins("gpsimd", lambda E: E.collective_compute(
            "AllGather", ALU.bypass, replica_groups=[list(range(8))],
            ins=[sc_part.opt()], outs=[sc_all.opt()]), inc=True)
        for i in range(NP):
            if dpc["scalar"][i]:
                q["scalar"].append(lambda E, h=dpool["scalar"][i], v=dpc["scalar"][i]:
                                   E.wait_ge(h, v))
        cast_all_done = ins("scalar", lambda E: E.nop(), inc=True)

        PARTC = 33024
        GOFF = 0
        UOFF = IT * 2048
        DOFF = 2 * IT * 2048

        def load_image(dst_ap_fn, col0, w):
            toks = []
            off = 0
            while w > 0:
                r = col0 // PARTC
                lc = col0 % PARTC
                take = min(w, PARTC - lc)
                toks.append(dma("sync", dst_ap_fn(off, take),
                                sc_all[r, :, lc:lc + take]))
                col0 += take
                off += take
                w -= take
            return toks
        cast_all_done = ins("scalar", lambda E: E.nop(), inc=True)

        t_o = {}
        t_sup = {}
        a_oev = {}
        a_ktm = {}
        v_att = {}
        v_scast = {}
        v_dn = {}
        v_wo = {}
        t_gu = {}
        t_dn = {}
        ffn_i = [0]
        d_x = {}
        st_x = {}

        def ln_phase(marker):
            mm_first(4)
            vm = None
            for dc in range(KC):
                wt_m("tensor", marker(dc))
                last = dc == KC - 1
                vm = ins("tensor", lambda E, dc=dc, first=(dc == 0), last=last: E.matmul(
                    ps[4][0:1, 0:TS], onesk[:, :], x_sb[:, dc * TS:(dc + 1) * TS],
                    start=first, stop=last), inc=last)
            mm_last(4, vm)
            mm_first(5)
            a_sq = {}
            t_sqm = {}
            for dc in range(KC):
                wt_m("scalar", marker(dc))
                if dc >= 4:
                    wt("scalar", "c", "tensor", t_sqm[dc - 4])
                a_sq[dc] = ins("scalar", lambda E, dc=dc: E.activation(
                    out=sqb[:, (dc % 4) * TS:(dc % 4 + 1) * TS],
                    in_=x_sb[:, dc * TS:(dc + 1) * TS], func=AF.Square), inc=True)
                wt("tensor", "c", "scalar", a_sq[dc])
                last = dc == KC - 1
                t_sqm[dc] = ins("tensor", lambda E, dc=dc, first=(dc == 0), last=last: E.matmul(
                    ps[5][0:1, 0:TS], onesk[:, :],
                    sqb[:, (dc % 4) * TS:(dc % 4 + 1) * TS],
                    start=first, stop=last), inc=True)
            mm_last(5, t_sqm[KC - 1])
            prd(4, "vector", lambda E: E.tensor_copy(out=m_sb, in_=ps[4][0:1, 0:TS]))
            mm_first(4)
            t_bm = ins("tensor", lambda E: E.matmul(
                ps[4][:, 0:TS], ones_c[:, :], m_sb, start=True, stop=True), inc=True)
            mm_last(4, t_bm)
            prd(5, "vector", lambda E: E.tensor_copy(out=ms_sb, in_=ps[5][0:1, 0:TS]))
            ins("vector", lambda E: E.tensor_mul(m2_sb, m_sb, m_sb))
            ins("vector", lambda E: E.tensor_sub(var_sb, ms_sb, m2_sb))
            v_var = ins("vector", lambda E: E.tensor_scalar_add(var_sb, var_sb, EPS), inc=True)
            wt("scalar", "c", "vector", v_var)
            a_ln = ins("scalar", lambda E: E.activation(out=ln_sb, in_=var_sb, func=AF.Ln), inc=True)
            wt("vector", "c", "scalar", a_ln)
            v_l2 = ins("vector", lambda E: E.tensor_scalar_mul(ln_sb, ln_sb, -0.5), inc=True)
            wt("scalar", "c", "vector", v_l2)
            a_rstd = ins("scalar", lambda E: E.activation(
                out=rstd_sb, in_=ln_sb, func=AF.Exp), inc=True)
            wt("tensor", "c", "scalar", a_rstd)
            mm_first(5)
            t_br = ins("tensor", lambda E: E.matmul(
                ps[5][:, 0:TS], ones_c[:, :], rstd_sb, start=True, stop=True), inc=True)
            mm_last(5, t_br)
            wt("vector", "c", "tensor", t_br)
            v_app = []
            for dc in range(KC):
                wt_m("vector", marker(dc))
                ins("vector", lambda E, dc=dc: E.tensor_sub(
                    lntmp[:, :], x_sb[:, dc * TS:(dc + 1) * TS], ps[4][:, 0:TS]))
                v_app.append(ins("vector", lambda E, dc=dc: E.tensor_mul(
                    xln[:, dc * TS:(dc + 1) * TS], lntmp[:, :], ps[5][:, 0:TS]), inc=True))
            pbank[4]["r"]["vector"] = v_app[-1]
            pbank[5]["r"]["vector"] = v_app[-1]
            return v_app

        a_silu = {}
        v_h = {}

        def ffn_phase(v_xln):
            fi = ffn_i[0]
            ffn_i[0] += 1
            t_enter = cnt["tensor"]
            v_enter = cnt["vector"]
            d_g = {}
            d_u = {}

            def emit_gu_load(it):
                b = it % 3
                if it >= 3:
                    wt("sync", "c", "tensor", t_gu[(fi, it - 3)])
                else:
                    wt("sync", "c", "tensor", t_enter)
                wt("sync", "c", "gpsimd", v_cc)
                d_g[it] = load_image(
                    lambda off, take, b=b: wg[b][:, off:off + take],
                    GOFF + it * 2048, 2048)
                d_u[it] = load_image(
                    lambda off, take, b=b: wu[b][:, off:off + take],
                    UOFF + it * 2048, 2048)

            d_d = {}

            def emit_d_load(dt):
                b = dt % 2
                if dt >= 2:
                    wt("sync", "c", "tensor", t_dn[(fi, dt - 2)])
                elif dt == 1:
                    wt("sync", "c", "tensor", t_gu[(fi, IT - 2)])
                else:
                    wt("sync", "c", "tensor", t_enter)
                wt("sync", "c", "gpsimd", v_cc)
                d_d[dt] = load_image(
                    lambda off, take, b=b: wd[b][:, off:off + take],
                    DOFF + dt * INTER, INTER)

            emit_gu_load(0)
            emit_gu_load(1)
            emit_gu_load(2)
            for it in range(IT):
                b = it % 3
                b2 = it % 2
                bg, bu = 2 * b2, 2 * b2 + 1
                wt_d("tensor", d_g[it] + d_u[it])
                mm_first(bg)
                tg = None
                for kc in range(KC):
                    wt("tensor", "c", "vector", v_xln[kc])
                    last = kc == KC - 1
                    tg = ins("tensor", lambda E, kc=kc, b=b, bg=bg,
                             first=(kc == 0), last=last: E.matmul(
                        ps[bg][:, 0:TS], wg[b][:, kc * 128:(kc + 1) * 128],
                        xln[:, kc * TS:(kc + 1) * TS],
                        start=first, stop=last), inc=last)
                mm_last(bg, tg)
                mm_first(bu)
                tu = None
                for kc in range(KC):
                    last = kc == KC - 1
                    tu = ins("tensor", lambda E, kc=kc, b=b, bu=bu,
                             first=(kc == 0), last=last: E.matmul(
                        ps[bu][:, 0:TS], wu[b][:, kc * 128:(kc + 1) * 128],
                        xln[:, kc * TS:(kc + 1) * TS],
                        start=first, stop=last), inc=last)
                mm_last(bu, tu)
                t_gu[(fi, it)] = tu
                if it >= 2:
                    wt("scalar", "c", "vector", v_h[(fi, it - 2)])
                else:
                    wt("scalar", "c", "vector", v_enter)
                a_silu[(fi, it)] = prd(bg, "scalar", lambda E, b2=b2, bg=bg: E.activation(
                    out=sg[:, b2 * TS:(b2 + 1) * TS], in_=ps[bg][:, 0:TS], func=AF.Silu))
                wt("vector", "c", "scalar", a_silu[(fi, it)])
                if fi == 0:
                    wt("vector", "c", "scalar", cast_all_done)
                v_h[(fi, it)] = prd(bu, "vector", lambda E, it=it, b2=b2, bu=bu: E.tensor_mul(
                    h_sl[:, it * TS:(it + 1) * TS], sg[:, b2 * TS:(b2 + 1) * TS],
                    ps[bu][:, 0:TS]))
                if it + 3 < IT:
                    emit_gu_load(it + 3)
                if it == 6:
                    emit_d_load(0)
            emit_d_load(1)
            for dt in range(KC):
                b = dt % 2
                bank = dt % 2
                wt_d("tensor", d_d[dt])
                wt("tensor", "c", "vector", v_h[(fi, IT - 1)])
                mm_first(bank)
                td = None
                for ic in range(IT):
                    last = ic == IT - 1
                    td = ins("tensor", lambda E, ic=ic, b=b, bank=bank,
                             first=(ic == 0), last=last: E.matmul(
                        ps[bank][:, 0:TS], wd[b][:, ic * 128:(ic + 1) * 128],
                        h_sl[:, ic * TS:(ic + 1) * TS],
                        start=first, stop=last), inc=last)
                mm_last(bank, td)
                t_dn[(fi, dt)] = td
                v_dn[(fi, dt)] = prd(bank, "vector", lambda E, dt=dt, bank=bank:
                                     E.scalar_tensor_tensor(
                    out=x_sb[:, dt * TS:(dt + 1) * TS],
                    in0=ps[bank][:, 0:TS], scalar=0.5,
                    in1=x_sb[:, dt * TS:(dt + 1) * TS],
                    op0=ALU.mult, op1=ALU.add))
                if dt + 2 < KC:
                    emit_d_load(dt + 2)
            return fi

        def attn_phase(s, v_xln):
            t_enter = cnt["tensor"]
            v_enter = cnt["vector"]
            t_qk = {}
            a_qk = {}
            d_qk = {}
            qk_srcs = [(sc_q, q_fm, hh) for hh in range(NH)] + \
                      [(sc_k, k_fm, hh) for hh in range(NH)]

            def emit_qk_load(i):
                scd, dstv, hh = qk_srcs[i]
                b = i % 2
                if i >= 2:
                    wt("sync", "c", "tensor", t_qk[i - 2])
                else:
                    wt("sync", "c", "tensor", t_enter)
                wt_d("sync", cast_toks["q"] + cast_toks["k"])
                d_qk[i] = dma("sync", qkw[b][:, :], scd[hh, :, :])

            v_u_last = 0
            t_vb_last = {}
            a_sig_first = [True]
            d_vb = {}
            vb4 = [vbw[0][:, 0:4096], vbw[0][:, 4096:8192],
                   vbw[1][:, 0:4096], vbw[1][:, 4096:8192]]

            def emit_vb_load(sl8):
                if sl8 >= 2:
                    wt("sync", "c", "tensor", t_vb_last[sl8 - 2])
                else:
                    wt("sync", "c", "tensor", t_enter)
                    wt_d("sync", cast_toks["v"] + cast_toks["b"])
                bq = (sl8 % 2) * 2
                d_vb[sl8] = [dma("sync", vb4[bq], sc_v[sl8]),
                             dma("sync", vb4[bq + 1], sc_b[sl8])]

            emit_qk_load(0)
            emit_qk_load(1)
            emit_vb_load(0)
            emit_vb_load(1)
            for i in range(2 * NH):
                scd, dstv, hh = qk_srcs[i]
                b = i % 2
                bank = i % 2
                wt_d("tensor", d_qk[i])
                mm_first(bank)
                tl = None
                for kc in range(KC):
                    wt("tensor", "c", "vector", v_xln[kc])
                    last = kc == KC - 1
                    tl = ins("tensor", lambda E, kc=kc, b=b, bank=bank,
                             first=(kc == 0), last=last: E.matmul(
                        ps[bank][:, 0:TS], qkw[b][:, kc * 128:(kc + 1) * 128],
                        xln[:, kc * TS:(kc + 1) * TS],
                        start=first, stop=last), inc=last)
                mm_last(bank, tl)
                t_qk[i] = tl
                a_qk[i] = prd(bank, "scalar", lambda E, hh=hh, bank=bank, dstv=dstv:
                              E.copy(out=dstv[:, hh * TS:(hh + 1) * TS],
                                     in_=ps[bank][:, 0:TS]))
                if i + 2 < 2 * NH:
                    emit_qk_load(i + 2)
            for sl8 in range(8):
                bq = (sl8 % 2) * 2
                for tt in range(NCH):
                    bv, bb = 2 * (tt % 2), 2 * (tt % 2) + 1
                    wt_d("tensor", d_vb[sl8])
                    mm_first(bv)
                    tv = None
                    for kc in range(KC):
                        last = kc == KC - 1
                        tv = ins("tensor", lambda E, kc=kc, tt=tt, bv=bv, bq=bq,
                                 first=(kc == 0), last=last: E.matmul(
                            ps[bv][:, 0:256],
                            xln[:, kc * TS + tt * 128:kc * TS + (tt + 1) * 128],
                            vb4[bq][:, kc * 256:(kc + 1) * 256],
                            start=first, stop=last), inc=last)
                    mm_last(bv, tv)
                    mm_first(bb)
                    tb = None
                    for kc in range(KC):
                        last = kc == KC - 1
                        tb = ins("tensor", lambda E, kc=kc, tt=tt, bb=bb, bq=bq,
                                 first=(kc == 0), last=last: E.matmul(
                            ps[bb][:, 0:256],
                            xln[:, kc * TS + tt * 128:kc * TS + (tt + 1) * 128],
                            vb4[bq + 1][:, kc * 256:(kc + 1) * 256],
                            start=first, stop=last), inc=last)
                    mm_last(bb, tb)
                    t_vb_last[sl8] = tb
                    if a_sig_first[0]:
                        wt("scalar", "c", "vector", v_enter)
                        a_sig_first[0] = False
                    else:
                        wt("scalar", "c", "vector", v_u_last)
                    a_sig = prd(bb, "scalar", lambda E, bb=bb: E.activation(
                        out=sg[:, 0:256], in_=ps[bb][:, 0:256], func=AF.Sigmoid))
                    wt("vector", "c", "scalar", a_sig)
                    v_u_last = prd(bv, "vector", lambda E, sl8=sl8, tt=tt, bv=bv:
                                   E.tensor_mul(
                        u_tm[:, tt * D + sl8 * 256:tt * D + (sl8 + 1) * 256],
                        sg[:, 0:256], ps[bv][:, 0:256]))
                if sl8 + 2 < 8:
                    emit_vb_load(sl8 + 2)
            for bk in (0, 1, 2, 3, 4, 5, "t0", "t1"):
                for oe, v in pbank[bk]["r"].items():
                    wt("tensor", "c", oe, v)
                pbank[bk]["r"] = {}

            def head_tail(gc, c, hh):
                bo = 2 if hh % 2 == 0 else 5
                wt("tensor", "c", "vector", v_att[(gc, hh)])
                if (gc, hh - 2) in a_oev:
                    wt("tensor", "c", "scalar", a_oev[(gc, hh - 2)])
                elif (gc - 1, NH - 2 + (hh % 2)) in a_oev:
                    wt("tensor", "c", "scalar", a_oev[(gc - 1, NH - 2 + (hh % 2))])
                if (gc - 1, hh) in v_scast:
                    wt("tensor", "c", "vector", v_scast[(gc - 1, hh)])
                ins("tensor", lambda E, c=c, hh=hh, bo=bo: E.matmul(
                    ps[bo][:, 0:128],
                    u_tm[:, c * D + hh * HD:c * D + (hh + 1) * HD],
                    attm[:, (hh % 2) * 128:(hh % 2 + 1) * 128],
                    start=True, stop=False))
                t_o[(gc, hh)] = ins("tensor", lambda E, c=c, hh=hh, bo=bo: E.matmul(
                    ps[bo][:, 0:128],
                    S_bf[:, hh * HD:(hh + 1) * HD],
                    q_fm[:, hh * TS + c * 128:hh * TS + (c + 1) * 128],
                    start=False, stop=True), inc=True)
                bs = 3 + (hh % 2)
                wt("tensor", "c", "scalar", a_ktm[(gc, hh)])
                if (gc, hh - 2) in v_scast:
                    wt("tensor", "c", "vector", v_scast[(gc, hh - 2)])
                elif c > 0:
                    wt("tensor", "c", "vector", v_scast[(gc - 1, NH - 2 + (hh % 2))])
                t_sup[(gc, hh)] = ins("tensor", lambda E, c=c, hh=hh, bs=bs: E.matmul(
                    ps[bs][:, 0:128],
                    ktm[:, (hh % 2) * 128:(hh % 2 + 1) * 128],
                    u_tm[:, c * D + hh * HD:c * D + (hh + 1) * HD],
                    start=True, stop=True), inc=True)
                wt("scalar", "c", "tensor", t_o[(gc, hh)])
                a_oev[(gc, hh)] = ins("scalar", lambda E, c=c, hh=hh, bo=bo: E.copy(
                    out=o_fm[:, hh * TS + c * 128:hh * TS + (c + 1) * 128],
                    in_=ps[bo][:, 0:128]), inc=True)
                wt("vector", "c", "tensor", t_sup[(gc, hh)])
                ins("vector", lambda E, hh=hh, bs=bs: E.tensor_add(
                    S_f32[:, hh * HD:(hh + 1) * HD],
                    S_f32[:, hh * HD:(hh + 1) * HD],
                    ps[bs][:, 0:128]))
                v_scast[(gc, hh)] = ins("vector", lambda E, hh=hh: E.tensor_copy(
                    out=S_bf[:, hh * HD:(hh + 1) * HD],
                    in_=S_f32[:, hh * HD:(hh + 1) * HD]), inc=True)

            for c in range(NCH):
                gc = s * NCH + c
                for hh in range(NH):
                    slot = hh % 2
                    ba = hh % 2
                    if (gc, hh - 2) in a_ktm:
                        wt("tensor", "c", "scalar", a_ktm[(gc, hh - 2)])
                    elif (gc - 1, NH - 2 + slot) in a_ktm:
                        wt("tensor", "c", "scalar", a_ktm[(gc - 1, NH - 2 + slot)])
                    wt("tensor", "c", "scalar", a_qk[NH + hh])
                    wt("tensor", "c", "vector", v_u_last)
                    t_ktp = ins("tensor", lambda E, c=c, hh=hh, slot=slot: E.transpose(
                        pst[slot][:, 0:128],
                        k_fm[:, hh * TS + c * 128:hh * TS + (c + 1) * 128],
                        identb[:, :]), inc=True)
                    if (gc, hh - 2) in v_att:
                        wt("tensor", "c", "vector", v_att[(gc, hh - 2)])
                    elif c > 0:
                        wt("tensor", "c", "vector", v_att[(gc - 1, NH - 2 + ba)])
                    wt("tensor", "c", "scalar", a_qk[hh])
                    t_att = ins("tensor", lambda E, c=c, hh=hh, ba=ba: E.matmul(
                        ps[ba][:, 0:128],
                        k_fm[:, hh * TS + c * 128:hh * TS + (c + 1) * 128],
                        q_fm[:, hh * TS + c * 128:hh * TS + (c + 1) * 128],
                        start=True, stop=True), inc=True)
                    wt("scalar", "c", "tensor", t_ktp)
                    if (gc, hh - 2) in t_sup:
                        wt("scalar", "c", "tensor", t_sup[(gc, hh - 2)])
                    elif (gc - 1, NH - 2 + slot) in t_sup:
                        wt("scalar", "c", "tensor", t_sup[(gc - 1, NH - 2 + slot)])
                    a_ktm[(gc, hh)] = ins("scalar", lambda E, slot=slot: E.copy(
                        out=ktm[:, slot * 128:(slot + 1) * 128],
                        in_=pst[slot][:, 0:128]), inc=True)
                    wt("vector", "c", "tensor", t_att)
                    if (gc, hh - 2) in t_o:
                        wt("vector", "c", "tensor", t_o[(gc, hh - 2)])
                    elif (gc - 1, NH - 2 + ba) in t_o:
                        wt("vector", "c", "tensor", t_o[(gc - 1, NH - 2 + ba)])
                    v_att[(gc, hh)] = ins("vector", lambda E, ba=ba, slot=slot: E.tensor_mul(
                        attm[:, slot * 128:(slot + 1) * 128],
                        ps[ba][:, 0:128], maskc[:, :]), inc=True)
                    if hh > 0:
                        head_tail(gc, c, hh - 1)
                head_tail(gc, c, NH - 1)
            for bk in (0, 1, 2, 3, 4, 5):
                pbank[bk]["w"] = ("tensor", cnt["tensor"])
            pbank[0]["r"]["vector"] = cnt["vector"]
            pbank[1]["r"]["vector"] = cnt["vector"]
            pbank[2]["r"]["scalar"] = cnt["scalar"]
            pbank[5]["r"]["scalar"] = cnt["scalar"]
            pbank[3]["r"]["vector"] = cnt["vector"]
            pbank[4]["r"]["vector"] = cnt["vector"]
            pbank["t0"]["r"]["scalar"] = cnt["scalar"]
            pbank["t1"]["r"]["scalar"] = cnt["scalar"]

            gcl = s * NCH + NCH - 1
            t_wo = {}
            d_wo = {}

            def emit_wo_load(dt):
                b = dt % 2
                if dt >= 2:
                    wt("sync", "c", "tensor", t_wo[dt - 2])
                else:
                    wt("sync", "c", "tensor", t_qk[2 * NH - 2 + b])
                    wt_d("sync", cast_toks["o"])
                d_wo[dt] = dma("sync", qkw[b][:, :], sc_o[dt, :, :])

            emit_wo_load(0)
            emit_wo_load(1)
            for dt in range(KC):
                b = dt % 2
                bank = dt % 2
                wt_d("tensor", d_wo[dt])
                wt("tensor", "c", "scalar", a_oev[(gcl, NH - 1)])
                mm_first(bank)
                tw = None
                for kc in range(KC):
                    last = kc == KC - 1
                    tw = ins("tensor", lambda E, kc=kc, b=b, bank=bank,
                             first=(kc == 0), last=last: E.matmul(
                        ps[bank][:, 0:TS], qkw[b][:, kc * 128:(kc + 1) * 128],
                        o_fm[:, kc * TS:(kc + 1) * TS],
                        start=first, stop=last), inc=last)
                mm_last(bank, tw)
                t_wo[dt] = tw
                v_wo[(s, dt)] = prd(bank, "vector", lambda E, dt=dt, bank=bank:
                                    E.tensor_add(
                    x_sb[:, dt * TS:(dt + 1) * TS],
                    x_sb[:, dt * TS:(dt + 1) * TS],
                    ps[bank][:, 0:TS]))
                if dt + 2 < KC:
                    emit_wo_load(dt + 2)

        for s in range(NSLAB):
            for dc in range(KC):
                if s > 0:
                    wt_d("sync", st_x[(s - 1, dc)])
                d_x[(s, dc)] = dma("sync", x_sb[:, dc * TS:(dc + 1) * TS],
                                   x_in[dc * 128:(dc + 1) * 128, s * TS:(s + 1) * TS])

            v_xln1 = ln_phase(lambda dc, s=s: ("d", d_x[(s, dc)]))
            fi1 = ffn_phase(v_xln1)
            v_xln2 = ln_phase(lambda dc, fi=fi1: ("c", "vector", v_dn[(fi, dc)]))
            attn_phase(s, v_xln2)
            v_xln3 = ln_phase(lambda dc, s=s: ("c", "vector", v_wo[(s, dc)]))
            fi2 = ffn_phase(v_xln3)

            for dc in range(KC):
                wt("sync", "c", "vector", v_dn[(fi2, dc)])
                st_x[(s, dc)] = dma("sync",
                                    y_out[dc * 128:(dc + 1) * 128, s * TS:(s + 1) * TS],
                                    x_sb[:, dc * TS:(dc + 1) * TS])

        for e in DQUEUES:
            for i in range(NP):
                if dpc[e][i]:
                    q[e].append(lambda E, h=dpool[e][i], v=dpc[e][i]: E.wait_ge(h, v))

        block = es.enter_context(nc.Block())

        @block.sync
        def _(E):
            for f in q["sync"]:
                f(E)

        @block.gpsimd
        def _(E):
            for f in q["gpsimd"]:
                f(E)

        @block.vector
        def _(E):
            for f in q["vector"]:
                f(E)

        @block.scalar
        def _(E):
            for f in q["scalar"]:
                f(E)

        @block.tensor
        def _(E):
            for f in q["tensor"]:
                f(E)

    return nc


_CACHE = {}


def kernel(L0_attn_hidden_states, L0_attn_Wq, L0_attn_Wk, L0_attn_Wv,
           L0_attn_Wb, L0_attn_Wo, L0_ffn_lut, L0_ffn_gate_idx,
           L0_ffn_up_idx, L0_ffn_down_idx):
    x = np.asarray(L0_attn_hidden_states, dtype=np.float32)
    gidx = np.asarray(L0_ffn_gate_idx, dtype=np.int32)
    uidx = np.asarray(L0_ffn_up_idx, dtype=np.int32)
    didx = np.asarray(L0_ffn_down_idx, dtype=np.int32)
    gb = gidx.reshape(IT, 128, KC, 128).transpose(0, 2, 1, 3).reshape(-1, 128, 128)
    ub = uidx.reshape(IT, 128, KC, 128).transpose(0, 2, 1, 3).reshape(-1, 128, 128)
    db = didx.reshape(KC, 128, IT, 128).transpose(0, 2, 1, 3).reshape(-1, 128, 128)
    blocks = np.concatenate([gb, ub, db], axis=0)
    base = {
        "wq": np.ascontiguousarray(np.asarray(L0_attn_Wq, dtype=np.float32)),
        "wk": np.ascontiguousarray(np.asarray(L0_attn_Wk, dtype=np.float32)),
        "wv": np.ascontiguousarray(np.asarray(L0_attn_Wv, dtype=np.float32)),
        "wb": np.ascontiguousarray(np.asarray(L0_attn_Wb, dtype=np.float32)),
        "wo": np.ascontiguousarray(np.asarray(L0_attn_Wo, dtype=np.float32)),
        "lut": np.ascontiguousarray(np.asarray(L0_ffn_lut, dtype=np.float32).reshape(1, 16)),
    }
    if "nc" not in _CACHE:
        _CACHE["nc"] = build()
    nc = _CACHE["nc"]
    in_maps = []
    for c in range(8):
        m = dict(base)
        m["x_in"] = np.ascontiguousarray(x[c].T)
        m["myidx"] = np.ascontiguousarray(
            blocks[c * 258:(c + 1) * 258].transpose(1, 0, 2).reshape(128, 33024))
        in_maps.append(m)
    res = run_bass_kernel_spmd(nc, in_maps, core_ids=list(range(8)))
    out = np.stack([np.asarray(res.results[c]["y"]).T for c in range(8)], axis=0)
    return np.ascontiguousarray(out.astype(np.float32))


# version 24
# speedup vs baseline: 1.0986x; 1.0083x over previous
import numpy as np
import contextlib
import concourse.bass as bass
import concourse.mybir as mybir
from concourse.bass_utils import run_bass_kernel_spmd

F32 = mybir.dt.float32
BF16 = mybir.dt.bfloat16
I32 = mybir.dt.int32
AF = mybir.ActivationFunctionType
ALU = mybir.AluOpType

D = 2048
SEQ = 2048
NH = 16
HD = 128
INTER = 5504
KC = D // 128
IT = INTER // 128
TS = 512
NSLAB = SEQ // TS
CH = 128
NCH = TS // CH
EPS = 1e-5

ENGS = ("sync", "gpsimd", "vector", "scalar", "tensor")
DQUEUES = ("sync", "scalar", "gpsimd")

DEBUG_EDGES = False


def build():
    nc = bass.Bass("TRN2", target_bir_lowering=False,
                   detect_race_conditions=DEBUG_EDGES)

    x_in = nc.dram_tensor("x_in", [D, SEQ], F32, kind="ExternalInput").ap()
    wq_d = nc.dram_tensor("wq", [D, D], F32, kind="ExternalInput").ap()
    wk_d = nc.dram_tensor("wk", [D, D], F32, kind="ExternalInput").ap()
    wv_d = nc.dram_tensor("wv", [D, D], F32, kind="ExternalInput").ap()
    wb_d = nc.dram_tensor("wb", [D, D], F32, kind="ExternalInput").ap()
    wo_d = nc.dram_tensor("wo", [D, D], F32, kind="ExternalInput").ap()
    lut_d = nc.dram_tensor("lut", [1, 16], F32, kind="ExternalInput").ap()
    y_out = nc.dram_tensor("y", [D, SEQ], F32, kind="ExternalOutput").ap()

    myidx_d = nc.dram_tensor("myidx", [128, 33024], I32, kind="ExternalInput").ap()
    sc_partA = nc.dram_tensor("sc_partA", [128, 16512], BF16, kind="Internal").ap()
    sc_partB = nc.dram_tensor("sc_partB", [128, 16512], BF16, kind="Internal").ap()
    sc_allA = nc.dram_tensor("sc_allA", [8, 128, 16512], BF16, kind="Internal",
                             addr_space="Shared").ap()
    sc_allB = nc.dram_tensor("sc_allB", [8, 128, 16512], BF16, kind="Internal",
                             addr_space="Shared").ap()
    sc_q = nc.dram_tensor("sc_q", [NH, 128, D], BF16, kind="Internal").ap()
    sc_k = nc.dram_tensor("sc_k", [NH, 128, D], BF16, kind="Internal").ap()
    sc_o = nc.dram_tensor("sc_o", [KC, 128, D], BF16, kind="Internal").ap()
    sc_v = nc.dram_tensor("sc_v", [8, 128, KC * 256], BF16, kind="Internal").ap()
    sc_b = nc.dram_tensor("sc_b", [8, 128, KC * 256], BF16, kind="Internal").ap()

    es = contextlib.ExitStack()
    with es:
        sbuf = lambda name, shp, dt: es.enter_context(nc.sbuf_tensor(name, shp, dt))
        psal = lambda name, shp, dt: es.enter_context(nc.psum_tensor(name, shp, dt))

        x_sb = sbuf("x_sb", [128, KC * TS], F32)
        xln = sbuf("xln", [128, KC * TS], BF16)
        pool = sbuf("pool", [128, 24576], BF16)
        h_sl = pool[:, 0:IT * TS]
        q_fm = pool[:, 0:8192]
        k_fm = pool[:, 8192:16384]
        u_tm = pool[:, 16384:24576]
        o_fm = sbuf("o_fm", [128, 8192], BF16)
        S_f32 = sbuf("S_f32", [128, NH * HD], F32)
        S_bf = sbuf("S_bf", [128, NH * HD], BF16)
        wpool = sbuf("wpool", [128, 20480], BF16)
        wg = [wpool[:, 0:2048], wpool[:, 2048:4096], wpool[:, 16384:18432]]
        wu = [wpool[:, 4096:6144], wpool[:, 6144:8192], wpool[:, 18432:20480]]
        wd = [wpool[:, 8192:8192 + INTER], wpool[:, 8192 + INTER:8192 + 2 * INTER]]
        qkw = [wpool[:, 0:2048], wpool[:, 2048:4096]]
        vbw = [wpool[:, 4096:12288], wpool[:, 12288:20480]]
        sg = sbuf("sg", [128, 2 * TS], BF16)
        sqb = sbuf("sqb", [128, 4 * TS], F32)
        lntmp = sbuf("lntmp", [128, TS], F32)
        ktm = sbuf("ktm", [128, 256], BF16)
        attm = sbuf("attm", [128, 256], BF16)
        stat = sbuf("stat", [1, 6 * TS], F32)
        m_sb = stat[0:1, 0:TS]
        ms_sb = stat[0:1, TS:2 * TS]
        m2_sb = stat[0:1, 2 * TS:3 * TS]
        var_sb = stat[0:1, 3 * TS:4 * TS]
        ln_sb = stat[0:1, 4 * TS:5 * TS]
        rstd_sb = stat[0:1, 5 * TS:6 * TS]
        lut_sb = sbuf("lut_sb", [1, 16], F32)
        lutb = sbuf("lutb", [128, 16], F32)
        ones_c = sbuf("ones_c", [1, 128], F32)
        onesk = sbuf("onesk", [128, 1], F32)
        maskc = sbuf("maskc", [128, 128], BF16)
        identb = sbuf("identb", [128, 128], BF16)
        rawi = sbuf("rawi", [128, 4096], I32)
        bfb = pool[:, 0:4096]
        accb = pool[:, 4096:8192]
        tmpb = pool[:, 8192:10240]
        wtb = pool[:, 10240:14336]
        cwt = pool[:, 14336:18432]
        rawf = x_sb[:, 0:4096]

        ps = [psal(f"ps{i}", [128, 512], F32) for i in range(6)]
        pst = [psal("pst0", [128, 1024], BF16), psal("pst1", [128, 1024], BF16)]

        csem = {e: es.enter_context(nc.semaphore("c_" + e)) for e in ENGS}
        NP = 1 if DEBUG_EDGES else 8
        dpool = {e: [es.enter_context(nc.semaphore(f"d_{e}{i}")) for i in range(NP)]
                 for e in DQUEUES}
        dpc = {e: [0] * NP for e in DQUEUES}
        dseq = {e: 0 for e in DQUEUES}
        cnt = {e: 0 for e in ENGS}
        q = {e: [] for e in ENGS}
        lastw = {e: {} for e in ENGS}

        def ins(e, f, inc=False):
            if DEBUG_EDGES:
                prev = cnt[e]
                h = csem[e]
                if prev:
                    q[e].append(lambda E, h=h, v=prev: E.wait_ge(h, v))
                cnt[e] += 1
                v = cnt[e]
                q[e].append(lambda E, f=f, h=h: f(E).then_inc(h, 1))
                return v
            if inc:
                cnt[e] += 1
                v = cnt[e]
                h = csem[e]
                q[e].append(lambda E, f=f, h=h: f(E).then_inc(h, 1))
                return v
            q[e].append(f)
            return None

        def dma(e, out_ap, in_ap):
            i = dseq[e] % NP
            dseq[e] += 1
            h = dpool[e][i]
            if dpc[e][i]:
                q[e].append(lambda E, h=h, v=dpc[e][i]: E.wait_ge(h, v))
            dpc[e][i] += 16
            v = dpc[e][i]
            q[e].append(lambda E, o=out_ap, i_=in_ap, h=h:
                        E.dma_start(out=o, in_=i_).then_inc(h, 16))
            return (e, i, v)

        def wt(e, kind, oe, v):
            if not v or v <= 0:
                return
            if kind == "c" and oe == e:
                return
            key = (kind, oe)
            if lastw[e].get(key, 0) >= v:
                return
            lastw[e][key] = v
            h = csem[oe]
            q[e].append(lambda E, h=h, v=v: E.wait_ge(h, v))

        def wts(e, v):
            if not v:
                return
            key = ("self",)
            if lastw[e].get(key, 0) >= v:
                return
            lastw[e][key] = v
            q[e].append(lambda E, h=csem[e], v=v: E.wait_ge(h, v))

        def wt_d(e, tokens):
            if tokens is None:
                return
            if isinstance(tokens, tuple):
                tokens = [tokens]
            for (oe, i, v) in tokens:
                key = ("d", oe, i)
                if lastw[e].get(key, 0) >= v:
                    continue
                lastw[e][key] = v
                h = dpool[oe][i]
                q[e].append(lambda E, h=h, v=v: E.wait_ge(h, v))

        def wt_m(e, m):
            if m[0] == "c":
                wt(e, "c", m[1], m[2])
            else:
                wt_d(e, m[1])

        pbank = {i: {"w": ("tensor", 0), "r": {}}
                 for i in list(range(6)) + ["t0", "t1"]}

        def mm_first(bank):
            for oe, v in pbank[bank]["r"].items():
                wt("tensor", "c", oe, v)
            pbank[bank]["r"] = {}

        def mm_last(bank, v):
            pbank[bank]["w"] = ("tensor", v)

        def prd(bank, e, f):
            we, wv = pbank[bank]["w"]
            wt(e, "c", we, wv)
            v = ins(e, f, inc=True)
            pbank[bank]["r"][e] = max(pbank[bank]["r"].get(e, 0), v)
            return v

        ins("gpsimd", lambda E: E.memset(ones_c[:, :], 1.0))
        ins("gpsimd", lambda E: E.memset(onesk[:, :], 1.0 / D))
        g1 = ins("gpsimd", lambda E: E.memset(identb[:, :], 0.0), inc=True)
        q["gpsimd"].append(lambda E, h=csem["gpsimd"], v=g1: E.wait_ge(h, v))
        ins("gpsimd", lambda E: E.affine_select(
            out=identb[:, :], in_=identb[:, :], compare_op=ALU.not_equal,
            fill=1.0, base=0, pattern=[[-1, 128]], channel_multiplier=1))
        g2 = ins("gpsimd", lambda E: E.memset(maskc[:, :], 1.0), inc=True)
        q["gpsimd"].append(lambda E, h=csem["gpsimd"], v=g2: E.wait_ge(h, v))
        ins("gpsimd", lambda E: E.affine_select(
            out=maskc[:, :], in_=maskc[:, :], compare_op=ALU.is_ge,
            fill=0.0, base=0, pattern=[[1, 128]], channel_multiplier=-1))
        ins("gpsimd", lambda E: E.memset(S_f32[:, :], 0.0))
        v_init = ins("gpsimd", lambda E: E.memset(S_bf[:, :], 0.0), inc=True)

        d_lut = dma("sync", lut_sb[:, :], lut_d)
        wt_d("tensor", d_lut)
        wt("tensor", "c", "gpsimd", v_init)
        mm_first(5)
        t_lut = ins("tensor", lambda E: E.matmul(
            ps[5][:, 0:16], ones_c[:, :], lut_sb[:, :], start=True, stop=True), inc=True)
        mm_last(5, t_lut)
        v_lut = prd(5, "vector", lambda E: E.tensor_copy(out=lutb[:, :], in_=ps[5][:, 0:16]))
        q["vector"].append(lambda E, h=csem["vector"], v=v_lut: E.wait_ge(h, v))
        wt("vector", "c", "gpsimd", v_init)

        mat = [(c, 2048) for c in range(16)] + [(16, 256)]
        NMAT = len(mat)

        cast_toks = {wn: [] for wn in ("q", "k", "v", "b", "o")}
        for wn, wdram in (("q", wq_d), ("k", wk_d), ("v", wv_d), ("b", wb_d),
                          ("o", wo_d)):
            for kc in range(KC):
                if wn in ("q", "k", "o"):
                    scd = {"q": sc_q, "k": sc_k, "o": sc_o}[wn]
                    tok = dma(
                        "gpsimd",
                        scd[:, :, kc * 128:(kc + 1) * 128].rearrange("h p c -> p h c"),
                        wdram[kc * 128:(kc + 1) * 128, :].rearrange(
                            "p (h c) -> p h c", c=128))
                else:
                    scd = sc_v if wn == "v" else sc_b
                    tok = dma(
                        "gpsimd",
                        scd[:, :, kc * 256:(kc + 1) * 256].rearrange("s p c -> p s c"),
                        wdram[kc * 128:(kc + 1) * 128, :].rearrange(
                            "p (s c) -> p s c", c=256))
                cast_toks[wn].append(tok)

        v_cast = {}
        t_tp = {}
        a_ev = {}
        g_wr = {}
        nbs = [w // 128 for (_, w) in mat]

        idx_ld = {}

        def emit_idx_load(k):
            c, w = mat[k]
            b = k % 2
            if k >= 2:
                wt("sync", "c", "vector", v_cast[k - 2])
            idx_ld[k] = dma("sync", rawi[:, b * 2048:b * 2048 + w],
                            myidx_d[:, c * 2048:c * 2048 + w])

        emit_idx_load(0)
        emit_idx_load(1)

        def emit_mat(k):
            c, w = mat[k]
            b = k % 2
            nb = nbs[k]
            ld = idx_ld[k]
            wt_d("vector", ld)
            v_cast[k] = ins("vector", lambda E, b=b, w=w: E.tensor_copy(
                out=bfb[:, b * 2048:b * 2048 + w],
                in_=rawi[:, b * 2048:b * 2048 + w]), inc=True)
            if k + 2 < NMAT:
                emit_idx_load(k + 2)
            if k >= 2:
                wt("vector", "c", "tensor", t_tp[(k - 2, nbs[k - 2] - 1)])
            for j16 in range(16):
                sc_ap = lutb[:, j16:j16 + 1]
                if j16 == 0:
                    ins("vector", lambda E, b=b, w=w, jj=j16, s=sc_ap: E.tensor_scalar(
                        out=accb[:, b * 2048:b * 2048 + w],
                        in0=bfb[:, b * 2048:b * 2048 + w],
                        scalar1=float(jj), scalar2=s,
                        op0=ALU.is_equal, op1=ALU.mult))
                else:
                    ins("vector", lambda E, b=b, w=w, jj=j16, s=sc_ap: E.tensor_scalar(
                        out=tmpb[:, 0:w],
                        in0=bfb[:, b * 2048:b * 2048 + w],
                        scalar1=float(jj), scalar2=s,
                        op0=ALU.is_equal, op1=ALU.mult))
                    ins("vector", lambda E, b=b, w=w: E.tensor_add(
                        accb[:, b * 2048:b * 2048 + w],
                        accb[:, b * 2048:b * 2048 + w],
                        tmpb[:, 0:w]))
            v_sel = ins("vector", lambda E: E.engine_nop(), inc=True)
            wt("tensor", "c", "vector", v_sel)
            if k >= 1:
                wt("tensor", "c", "scalar", a_ev[(k - 1, nbs[k - 1] - 1)])
            if k >= 2:
                wt_d("scalar", g_wr[k - 2])
            for j in range(nb):
                slot = j % 2
                if j >= 2:
                    wt("tensor", "c", "scalar", a_ev[(k, j - 2)])
                t_tp[(k, j)] = ins("tensor", lambda E, b=b, j=j, slot=slot: E.transpose(
                    pst[slot][:, 0:128],
                    accb[:, b * 2048 + j * 128:b * 2048 + (j + 1) * 128],
                    identb[:, :]), inc=True)
                wt("scalar", "c", "tensor", t_tp[(k, j)])
                a_ev[(k, j)] = ins("scalar", lambda E, b=b, j=j, slot=slot: E.copy(
                    out=wtb[:, b * 2048 + j * 128:b * 2048 + (j + 1) * 128],
                    in_=pst[slot][:, 0:128]), inc=True)
            wts("scalar", a_ev[(k, nb - 1)])
            L0 = c * 2048
            GRP = 16512
            if L0 + w <= GRP:
                g_wr[k] = [dma("scalar", sc_partA[:, L0:L0 + w],
                               wtb[:, b * 2048:b * 2048 + w])]
            elif L0 >= GRP:
                g_wr[k] = [dma("scalar", sc_partB[:, L0 - GRP:L0 - GRP + w],
                               wtb[:, b * 2048:b * 2048 + w])]
            else:
                wa = GRP - L0
                g_wr[k] = [dma("scalar", sc_partA[:, L0:GRP],
                               wtb[:, b * 2048:b * 2048 + wa]),
                           dma("scalar", sc_partB[:, 0:w - wa],
                               wtb[:, b * 2048 + wa:b * 2048 + w])]

        for k in range(NMAT):
            emit_mat(k)

        a_deps = []
        b_deps = []
        for k in range(NMAT):
            c, w = mat[k]
            if c * 2048 < 16512:
                a_deps.append(g_wr[k][0])
            b_deps.extend(g_wr[k][1:] if c * 2048 < 16512 else g_wr[k])
        wt_d("gpsimd", a_deps)
        v_ccA = ins("gpsimd", lambda E: E.collective_compute(
            "AllGather", ALU.bypass, replica_groups=[list(range(8))],
            ins=[sc_partA.opt()], outs=[sc_allA.opt()]), inc=True)
        wt_d("gpsimd", b_deps)
        v_ccB = ins("gpsimd", lambda E: E.collective_compute(
            "AllGather", ALU.bypass, replica_groups=[list(range(8))],
            ins=[sc_partB.opt()], outs=[sc_allB.opt()]), inc=True)
        for i in range(NP):
            if dpc["scalar"][i]:
                q["scalar"].append(lambda E, h=dpool["scalar"][i], v=dpc["scalar"][i]:
                                   E.wait_ge(h, v))
        cast_all_done = ins("scalar", lambda E: E.nop(), inc=True)

        PARTC = 33024
        GOFF = 0
        UOFF = IT * 2048
        DOFF = 2 * IT * 2048

        HALF = 8 * 16512

        def load_image(dst_ap_fn, col0, w):
            toks = []
            off = 0
            while w > 0:
                g = col0 // HALF
                rem = col0 - g * HALF
                r = rem // 16512
                lc = rem % 16512
                take = min(w, 16512 - lc)
                src = (sc_allA if g == 0 else sc_allB)[r, :, lc:lc + take]
                toks.append(dma("sync", dst_ap_fn(off, take), src))
                col0 += take
                off += take
                w -= take
            return toks

        def wt_cc(e, col0, w):
            if col0 < HALF:
                wt(e, "c", "gpsimd", v_ccA)
            if col0 + w > HALF:
                wt(e, "c", "gpsimd", v_ccB)
        cast_all_done = ins("scalar", lambda E: E.nop(), inc=True)

        t_o = {}
        t_sup = {}
        a_oev = {}
        a_ktm = {}
        v_att = {}
        v_scast = {}
        v_dn = {}
        v_wo = {}
        t_gu = {}
        t_dn = {}
        ffn_i = [0]
        d_x = {}
        st_x = {}

        def ln_phase(marker):
            mm_first(4)
            vm = None
            for dc in range(KC):
                wt_m("tensor", marker(dc))
                last = dc == KC - 1
                vm = ins("tensor", lambda E, dc=dc, first=(dc == 0), last=last: E.matmul(
                    ps[4][0:1, 0:TS], onesk[:, :], x_sb[:, dc * TS:(dc + 1) * TS],
                    start=first, stop=last), inc=last)
            mm_last(4, vm)
            mm_first(5)
            a_sq = {}
            t_sqm = {}
            for dc in range(KC):
                wt_m("scalar", marker(dc))
                if dc >= 4:
                    wt("scalar", "c", "tensor", t_sqm[dc - 4])
                a_sq[dc] = ins("scalar", lambda E, dc=dc: E.activation(
                    out=sqb[:, (dc % 4) * TS:(dc % 4 + 1) * TS],
                    in_=x_sb[:, dc * TS:(dc + 1) * TS], func=AF.Square), inc=True)
                wt("tensor", "c", "scalar", a_sq[dc])
                last = dc == KC - 1
                t_sqm[dc] = ins("tensor", lambda E, dc=dc, first=(dc == 0), last=last: E.matmul(
                    ps[5][0:1, 0:TS], onesk[:, :],
                    sqb[:, (dc % 4) * TS:(dc % 4 + 1) * TS],
                    start=first, stop=last), inc=True)
            mm_last(5, t_sqm[KC - 1])
            prd(4, "vector", lambda E: E.tensor_copy(out=m_sb, in_=ps[4][0:1, 0:TS]))
            mm_first(4)
            t_bm = ins("tensor", lambda E: E.matmul(
                ps[4][:, 0:TS], ones_c[:, :], m_sb, start=True, stop=True), inc=True)
            mm_last(4, t_bm)
            prd(5, "vector", lambda E: E.tensor_copy(out=ms_sb, in_=ps[5][0:1, 0:TS]))
            ins("vector", lambda E: E.tensor_mul(m2_sb, m_sb, m_sb))
            ins("vector", lambda E: E.tensor_sub(var_sb, ms_sb, m2_sb))
            v_var = ins("vector", lambda E: E.tensor_scalar_add(var_sb, var_sb, EPS), inc=True)
            wt("scalar", "c", "vector", v_var)
            a_ln = ins("scalar", lambda E: E.activation(out=ln_sb, in_=var_sb, func=AF.Ln), inc=True)
            wt("vector", "c", "scalar", a_ln)
            v_l2 = ins("vector", lambda E: E.tensor_scalar_mul(ln_sb, ln_sb, -0.5), inc=True)
            wt("scalar", "c", "vector", v_l2)
            a_rstd = ins("scalar", lambda E: E.activation(
                out=rstd_sb, in_=ln_sb, func=AF.Exp), inc=True)
            wt("tensor", "c", "scalar", a_rstd)
            mm_first(5)
            t_br = ins("tensor", lambda E: E.matmul(
                ps[5][:, 0:TS], ones_c[:, :], rstd_sb, start=True, stop=True), inc=True)
            mm_last(5, t_br)
            wt("vector", "c", "tensor", t_br)
            v_app = []
            for dc in range(KC):
                wt_m("vector", marker(dc))
                ins("vector", lambda E, dc=dc: E.tensor_sub(
                    lntmp[:, :], x_sb[:, dc * TS:(dc + 1) * TS], ps[4][:, 0:TS]))
                v_app.append(ins("vector", lambda E, dc=dc: E.tensor_mul(
                    xln[:, dc * TS:(dc + 1) * TS], lntmp[:, :], ps[5][:, 0:TS]), inc=True))
            pbank[4]["r"]["vector"] = v_app[-1]
            pbank[5]["r"]["vector"] = v_app[-1]
            return v_app

        a_silu = {}
        v_h = {}

        def ffn_phase(v_xln):
            fi = ffn_i[0]
            ffn_i[0] += 1
            t_enter = cnt["tensor"]
            v_enter = cnt["vector"]
            d_g = {}
            d_u = {}

            def emit_gu_load(it):
                b = it % 3
                if it >= 3:
                    wt("sync", "c", "tensor", t_gu[(fi, it - 3)])
                else:
                    wt("sync", "c", "tensor", t_enter)
                wt_cc("sync", GOFF + it * 2048, 2048)
                wt_cc("sync", UOFF + it * 2048, 2048)
                d_g[it] = load_image(
                    lambda off, take, b=b: wg[b][:, off:off + take],
                    GOFF + it * 2048, 2048)
                d_u[it] = load_image(
                    lambda off, take, b=b: wu[b][:, off:off + take],
                    UOFF + it * 2048, 2048)

            d_d = {}

            def emit_d_load(dt):
                b = dt % 2
                if dt >= 2:
                    wt("sync", "c", "tensor", t_dn[(fi, dt - 2)])
                elif dt == 1:
                    wt("sync", "c", "tensor", t_gu[(fi, IT - 2)])
                else:
                    wt("sync", "c", "tensor", t_enter)
                wt_cc("sync", DOFF + dt * INTER, INTER)
                d_d[dt] = load_image(
                    lambda off, take, b=b: wd[b][:, off:off + take],
                    DOFF + dt * INTER, INTER)

            emit_gu_load(0)
            emit_gu_load(1)
            emit_gu_load(2)
            for it in range(IT):
                b = it % 3
                b2 = it % 2
                bg, bu = 2 * b2, 2 * b2 + 1
                wt_d("tensor", d_g[it] + d_u[it])
                mm_first(bg)
                tg = None
                for kc in range(KC):
                    wt("tensor", "c", "vector", v_xln[kc])
                    last = kc == KC - 1
                    tg = ins("tensor", lambda E, kc=kc, b=b, bg=bg,
                             first=(kc == 0), last=last: E.matmul(
                        ps[bg][:, 0:TS], wg[b][:, kc * 128:(kc + 1) * 128],
                        xln[:, kc * TS:(kc + 1) * TS],
                        start=first, stop=last), inc=last)
                mm_last(bg, tg)
                mm_first(bu)
                tu = None
                for kc in range(KC):
                    last = kc == KC - 1
                    tu = ins("tensor", lambda E, kc=kc, b=b, bu=bu,
                             first=(kc == 0), last=last: E.matmul(
                        ps[bu][:, 0:TS], wu[b][:, kc * 128:(kc + 1) * 128],
                        xln[:, kc * TS:(kc + 1) * TS],
                        start=first, stop=last), inc=last)
                mm_last(bu, tu)
                t_gu[(fi, it)] = tu
                if it >= 2:
                    wt("scalar", "c", "vector", v_h[(fi, it - 2)])
                else:
                    wt("scalar", "c", "vector", v_enter)
                a_silu[(fi, it)] = prd(bg, "scalar", lambda E, b2=b2, bg=bg: E.activation(
                    out=sg[:, b2 * TS:(b2 + 1) * TS], in_=ps[bg][:, 0:TS], func=AF.Silu))
                wt("vector", "c", "scalar", a_silu[(fi, it)])
                if fi == 0:
                    wt("vector", "c", "scalar", cast_all_done)
                v_h[(fi, it)] = prd(bu, "vector", lambda E, it=it, b2=b2, bu=bu: E.tensor_mul(
                    h_sl[:, it * TS:(it + 1) * TS], sg[:, b2 * TS:(b2 + 1) * TS],
                    ps[bu][:, 0:TS]))
                if it + 3 < IT:
                    emit_gu_load(it + 3)
                if it == 6:
                    emit_d_load(0)
            emit_d_load(1)
            for dt in range(KC):
                b = dt % 2
                bank = dt % 2
                wt_d("tensor", d_d[dt])
                wt("tensor", "c", "vector", v_h[(fi, IT - 1)])
                mm_first(bank)
                td = None
                for ic in range(IT):
                    last = ic == IT - 1
                    td = ins("tensor", lambda E, ic=ic, b=b, bank=bank,
                             first=(ic == 0), last=last: E.matmul(
                        ps[bank][:, 0:TS], wd[b][:, ic * 128:(ic + 1) * 128],
                        h_sl[:, ic * TS:(ic + 1) * TS],
                        start=first, stop=last), inc=last)
                mm_last(bank, td)
                t_dn[(fi, dt)] = td
                v_dn[(fi, dt)] = prd(bank, "vector", lambda E, dt=dt, bank=bank:
                                     E.scalar_tensor_tensor(
                    out=x_sb[:, dt * TS:(dt + 1) * TS],
                    in0=ps[bank][:, 0:TS], scalar=0.5,
                    in1=x_sb[:, dt * TS:(dt + 1) * TS],
                    op0=ALU.mult, op1=ALU.add))
                if dt + 2 < KC:
                    emit_d_load(dt + 2)
            return fi

        def attn_phase(s, v_xln):
            t_enter = cnt["tensor"]
            v_enter = cnt["vector"]
            t_qk = {}
            a_qk = {}
            d_qk = {}
            qk_srcs = [(sc_q, q_fm, hh) for hh in range(NH)] + \
                      [(sc_k, k_fm, hh) for hh in range(NH)]

            def emit_qk_load(i):
                scd, dstv, hh = qk_srcs[i]
                b = i % 2
                if i >= 2:
                    wt("sync", "c", "tensor", t_qk[i - 2])
                else:
                    wt("sync", "c", "tensor", t_enter)
                wt_d("sync", cast_toks["q"] + cast_toks["k"])
                d_qk[i] = dma("sync", qkw[b][:, :], scd[hh, :, :])

            v_u_last = 0
            t_vb_last = {}
            a_sig_first = [True]
            d_vb = {}
            vb4 = [vbw[0][:, 0:4096], vbw[0][:, 4096:8192],
                   vbw[1][:, 0:4096], vbw[1][:, 4096:8192]]

            def emit_vb_load(sl8):
                if sl8 >= 2:
                    wt("sync", "c", "tensor", t_vb_last[sl8 - 2])
                else:
                    wt("sync", "c", "tensor", t_enter)
                    wt_d("sync", cast_toks["v"] + cast_toks["b"])
                bq = (sl8 % 2) * 2
                d_vb[sl8] = [dma("sync", vb4[bq], sc_v[sl8]),
                             dma("sync", vb4[bq + 1], sc_b[sl8])]

            emit_qk_load(0)
            emit_qk_load(1)
            emit_vb_load(0)
            emit_vb_load(1)
            for i in range(2 * NH):
                scd, dstv, hh = qk_srcs[i]
                b = i % 2
                bank = i % 2
                wt_d("tensor", d_qk[i])
                mm_first(bank)
                tl = None
                for kc in range(KC):
                    wt("tensor", "c", "vector", v_xln[kc])
                    last = kc == KC - 1
                    tl = ins("tensor", lambda E, kc=kc, b=b, bank=bank,
                             first=(kc == 0), last=last: E.matmul(
                        ps[bank][:, 0:TS], qkw[b][:, kc * 128:(kc + 1) * 128],
                        xln[:, kc * TS:(kc + 1) * TS],
                        start=first, stop=last), inc=last)
                mm_last(bank, tl)
                t_qk[i] = tl
                a_qk[i] = prd(bank, "scalar", lambda E, hh=hh, bank=bank, dstv=dstv:
                              E.copy(out=dstv[:, hh * TS:(hh + 1) * TS],
                                     in_=ps[bank][:, 0:TS]))
                if i + 2 < 2 * NH:
                    emit_qk_load(i + 2)
            for sl8 in range(8):
                bq = (sl8 % 2) * 2
                for tt in range(NCH):
                    bv, bb = 2 * (tt % 2), 2 * (tt % 2) + 1
                    wt_d("tensor", d_vb[sl8])
                    mm_first(bv)
                    tv = None
                    for kc in range(KC):
                        last = kc == KC - 1
                        tv = ins("tensor", lambda E, kc=kc, tt=tt, bv=bv, bq=bq,
                                 first=(kc == 0), last=last: E.matmul(
                            ps[bv][:, 0:256],
                            xln[:, kc * TS + tt * 128:kc * TS + (tt + 1) * 128],
                            vb4[bq][:, kc * 256:(kc + 1) * 256],
                            start=first, stop=last), inc=last)
                    mm_last(bv, tv)
                    mm_first(bb)
                    tb = None
                    for kc in range(KC):
                        last = kc == KC - 1
                        tb = ins("tensor", lambda E, kc=kc, tt=tt, bb=bb, bq=bq,
                                 first=(kc == 0), last=last: E.matmul(
                            ps[bb][:, 0:256],
                            xln[:, kc * TS + tt * 128:kc * TS + (tt + 1) * 128],
                            vb4[bq + 1][:, kc * 256:(kc + 1) * 256],
                            start=first, stop=last), inc=last)
                    mm_last(bb, tb)
                    t_vb_last[sl8] = tb
                    if a_sig_first[0]:
                        wt("scalar", "c", "vector", v_enter)
                        a_sig_first[0] = False
                    else:
                        wt("scalar", "c", "vector", v_u_last)
                    a_sig = prd(bb, "scalar", lambda E, bb=bb: E.activation(
                        out=sg[:, 0:256], in_=ps[bb][:, 0:256], func=AF.Sigmoid))
                    wt("vector", "c", "scalar", a_sig)
                    v_u_last = prd(bv, "vector", lambda E, sl8=sl8, tt=tt, bv=bv:
                                   E.tensor_mul(
                        u_tm[:, tt * D + sl8 * 256:tt * D + (sl8 + 1) * 256],
                        sg[:, 0:256], ps[bv][:, 0:256]))
                if sl8 + 2 < 8:
                    emit_vb_load(sl8 + 2)
            for bk in (0, 1, 2, 3, 4, 5, "t0", "t1"):
                for oe, v in pbank[bk]["r"].items():
                    wt("tensor", "c", oe, v)
                pbank[bk]["r"] = {}

            def head_tail(gc, c, hh):
                bo = 2 if hh % 2 == 0 else 5
                wt("tensor", "c", "vector", v_att[(gc, hh)])
                if (gc, hh - 2) in a_oev:
                    wt("tensor", "c", "scalar", a_oev[(gc, hh - 2)])
                elif (gc - 1, NH - 2 + (hh % 2)) in a_oev:
                    wt("tensor", "c", "scalar", a_oev[(gc - 1, NH - 2 + (hh % 2))])
                if (gc - 1, hh) in v_scast:
                    wt("tensor", "c", "vector", v_scast[(gc - 1, hh)])
                ins("tensor", lambda E, c=c, hh=hh, bo=bo: E.matmul(
                    ps[bo][:, 0:128],
                    u_tm[:, c * D + hh * HD:c * D + (hh + 1) * HD],
                    attm[:, (hh % 2) * 128:(hh % 2 + 1) * 128],
                    start=True, stop=False))
                t_o[(gc, hh)] = ins("tensor", lambda E, c=c, hh=hh, bo=bo: E.matmul(
                    ps[bo][:, 0:128],
                    S_bf[:, hh * HD:(hh + 1) * HD],
                    q_fm[:, hh * TS + c * 128:hh * TS + (c + 1) * 128],
                    start=False, stop=True), inc=True)
                bs = 3 + (hh % 2)
                wt("tensor", "c", "scalar", a_ktm[(gc, hh)])
                if (gc, hh - 2) in v_scast:
                    wt("tensor", "c", "vector", v_scast[(gc, hh - 2)])
                elif c > 0:
                    wt("tensor", "c", "vector", v_scast[(gc - 1, NH - 2 + (hh % 2))])
                t_sup[(gc, hh)] = ins("tensor", lambda E, c=c, hh=hh, bs=bs: E.matmul(
                    ps[bs][:, 0:128],
                    ktm[:, (hh % 2) * 128:(hh % 2 + 1) * 128],
                    u_tm[:, c * D + hh * HD:c * D + (hh + 1) * HD],
                    start=True, stop=True), inc=True)
                wt("scalar", "c", "tensor", t_o[(gc, hh)])
                a_oev[(gc, hh)] = ins("scalar", lambda E, c=c, hh=hh, bo=bo: E.copy(
                    out=o_fm[:, hh * TS + c * 128:hh * TS + (c + 1) * 128],
                    in_=ps[bo][:, 0:128]), inc=True)
                wt("vector", "c", "tensor", t_sup[(gc, hh)])
                ins("vector", lambda E, hh=hh, bs=bs: E.tensor_add(
                    S_f32[:, hh * HD:(hh + 1) * HD],
                    S_f32[:, hh * HD:(hh + 1) * HD],
                    ps[bs][:, 0:128]))
                v_scast[(gc, hh)] = ins("vector", lambda E, hh=hh: E.tensor_copy(
                    out=S_bf[:, hh * HD:(hh + 1) * HD],
                    in_=S_f32[:, hh * HD:(hh + 1) * HD]), inc=True)

            for c in range(NCH):
                gc = s * NCH + c
                for hh in range(NH):
                    slot = hh % 2
                    ba = hh % 2
                    if (gc, hh - 2) in a_ktm:
                        wt("tensor", "c", "scalar", a_ktm[(gc, hh - 2)])
                    elif (gc - 1, NH - 2 + slot) in a_ktm:
                        wt("tensor", "c", "scalar", a_ktm[(gc - 1, NH - 2 + slot)])
                    wt("tensor", "c", "scalar", a_qk[NH + hh])
                    wt("tensor", "c", "vector", v_u_last)
                    t_ktp = ins("tensor", lambda E, c=c, hh=hh, slot=slot: E.transpose(
                        pst[slot][:, 0:128],
                        k_fm[:, hh * TS + c * 128:hh * TS + (c + 1) * 128],
                        identb[:, :]), inc=True)
                    if (gc, hh - 2) in v_att:
                        wt("tensor", "c", "vector", v_att[(gc, hh - 2)])
                    elif c > 0:
                        wt("tensor", "c", "vector", v_att[(gc - 1, NH - 2 + ba)])
                    wt("tensor", "c", "scalar", a_qk[hh])
                    t_att = ins("tensor", lambda E, c=c, hh=hh, ba=ba: E.matmul(
                        ps[ba][:, 0:128],
                        k_fm[:, hh * TS + c * 128:hh * TS + (c + 1) * 128],
                        q_fm[:, hh * TS + c * 128:hh * TS + (c + 1) * 128],
                        start=True, stop=True), inc=True)
                    wt("scalar", "c", "tensor", t_ktp)
                    if (gc, hh - 2) in t_sup:
                        wt("scalar", "c", "tensor", t_sup[(gc, hh - 2)])
                    elif (gc - 1, NH - 2 + slot) in t_sup:
                        wt("scalar", "c", "tensor", t_sup[(gc - 1, NH - 2 + slot)])
                    a_ktm[(gc, hh)] = ins("scalar", lambda E, slot=slot: E.copy(
                        out=ktm[:, slot * 128:(slot + 1) * 128],
                        in_=pst[slot][:, 0:128]), inc=True)
                    wt("vector", "c", "tensor", t_att)
                    if (gc, hh - 2) in t_o:
                        wt("vector", "c", "tensor", t_o[(gc, hh - 2)])
                    elif (gc - 1, NH - 2 + ba) in t_o:
                        wt("vector", "c", "tensor", t_o[(gc - 1, NH - 2 + ba)])
                    v_att[(gc, hh)] = ins("vector", lambda E, ba=ba, slot=slot: E.tensor_mul(
                        attm[:, slot * 128:(slot + 1) * 128],
                        ps[ba][:, 0:128], maskc[:, :]), inc=True)
                    if hh > 0:
                        head_tail(gc, c, hh - 1)
                head_tail(gc, c, NH - 1)
            for bk in (0, 1, 2, 3, 4, 5):
                pbank[bk]["w"] = ("tensor", cnt["tensor"])
            pbank[0]["r"]["vector"] = cnt["vector"]
            pbank[1]["r"]["vector"] = cnt["vector"]
            pbank[2]["r"]["scalar"] = cnt["scalar"]
            pbank[5]["r"]["scalar"] = cnt["scalar"]
            pbank[3]["r"]["vector"] = cnt["vector"]
            pbank[4]["r"]["vector"] = cnt["vector"]
            pbank["t0"]["r"]["scalar"] = cnt["scalar"]
            pbank["t1"]["r"]["scalar"] = cnt["scalar"]

            gcl = s * NCH + NCH - 1
            t_wo = {}
            d_wo = {}

            def emit_wo_load(dt):
                b = dt % 2
                if dt >= 2:
                    wt("sync", "c", "tensor", t_wo[dt - 2])
                else:
                    wt("sync", "c", "tensor", t_qk[2 * NH - 2 + b])
                    wt_d("sync", cast_toks["o"])
                d_wo[dt] = dma("sync", qkw[b][:, :], sc_o[dt, :, :])

            emit_wo_load(0)
            emit_wo_load(1)
            for dt in range(KC):
                b = dt % 2
                bank = dt % 2
                wt_d("tensor", d_wo[dt])
                wt("tensor", "c", "scalar", a_oev[(gcl, NH - 1)])
                mm_first(bank)
                tw = None
                for kc in range(KC):
                    last = kc == KC - 1
                    tw = ins("tensor", lambda E, kc=kc, b=b, bank=bank,
                             first=(kc == 0), last=last: E.matmul(
                        ps[bank][:, 0:TS], qkw[b][:, kc * 128:(kc + 1) * 128],
                        o_fm[:, kc * TS:(kc + 1) * TS],
                        start=first, stop=last), inc=last)
                mm_last(bank, tw)
                t_wo[dt] = tw
                v_wo[(s, dt)] = prd(bank, "vector", lambda E, dt=dt, bank=bank:
                                    E.tensor_add(
                    x_sb[:, dt * TS:(dt + 1) * TS],
                    x_sb[:, dt * TS:(dt + 1) * TS],
                    ps[bank][:, 0:TS]))
                if dt + 2 < KC:
                    emit_wo_load(dt + 2)

        for s in range(NSLAB):
            for dc in range(KC):
                if s > 0:
                    wt_d("sync", st_x[(s - 1, dc)])
                d_x[(s, dc)] = dma("sync", x_sb[:, dc * TS:(dc + 1) * TS],
                                   x_in[dc * 128:(dc + 1) * 128, s * TS:(s + 1) * TS])

            v_xln1 = ln_phase(lambda dc, s=s: ("d", d_x[(s, dc)]))
            fi1 = ffn_phase(v_xln1)
            v_xln2 = ln_phase(lambda dc, fi=fi1: ("c", "vector", v_dn[(fi, dc)]))
            attn_phase(s, v_xln2)
            v_xln3 = ln_phase(lambda dc, s=s: ("c", "vector", v_wo[(s, dc)]))
            fi2 = ffn_phase(v_xln3)

            for dc in range(KC):
                wt("sync", "c", "vector", v_dn[(fi2, dc)])
                st_x[(s, dc)] = dma("sync",
                                    y_out[dc * 128:(dc + 1) * 128, s * TS:(s + 1) * TS],
                                    x_sb[:, dc * TS:(dc + 1) * TS])

        for e in DQUEUES:
            for i in range(NP):
                if dpc[e][i]:
                    q[e].append(lambda E, h=dpool[e][i], v=dpc[e][i]: E.wait_ge(h, v))

        block = es.enter_context(nc.Block())

        @block.sync
        def _(E):
            for f in q["sync"]:
                f(E)

        @block.gpsimd
        def _(E):
            for f in q["gpsimd"]:
                f(E)

        @block.vector
        def _(E):
            for f in q["vector"]:
                f(E)

        @block.scalar
        def _(E):
            for f in q["scalar"]:
                f(E)

        @block.tensor
        def _(E):
            for f in q["tensor"]:
                f(E)

    return nc


_CACHE = {}


def kernel(L0_attn_hidden_states, L0_attn_Wq, L0_attn_Wk, L0_attn_Wv,
           L0_attn_Wb, L0_attn_Wo, L0_ffn_lut, L0_ffn_gate_idx,
           L0_ffn_up_idx, L0_ffn_down_idx):
    x = np.asarray(L0_attn_hidden_states, dtype=np.float32)
    gidx = np.asarray(L0_ffn_gate_idx, dtype=np.int32)
    uidx = np.asarray(L0_ffn_up_idx, dtype=np.int32)
    didx = np.asarray(L0_ffn_down_idx, dtype=np.int32)
    gb = gidx.reshape(IT, 128, KC, 128).transpose(0, 2, 1, 3).reshape(-1, 128, 128)
    ub = uidx.reshape(IT, 128, KC, 128).transpose(0, 2, 1, 3).reshape(-1, 128, 128)
    db = didx.reshape(KC, 128, IT, 128).transpose(0, 2, 1, 3).reshape(-1, 128, 128)
    blocks = np.concatenate([gb, ub, db], axis=0)
    base = {
        "wq": np.ascontiguousarray(np.asarray(L0_attn_Wq, dtype=np.float32)),
        "wk": np.ascontiguousarray(np.asarray(L0_attn_Wk, dtype=np.float32)),
        "wv": np.ascontiguousarray(np.asarray(L0_attn_Wv, dtype=np.float32)),
        "wb": np.ascontiguousarray(np.asarray(L0_attn_Wb, dtype=np.float32)),
        "wo": np.ascontiguousarray(np.asarray(L0_attn_Wo, dtype=np.float32)),
        "lut": np.ascontiguousarray(np.asarray(L0_ffn_lut, dtype=np.float32).reshape(1, 16)),
    }
    if "nc" not in _CACHE:
        _CACHE["nc"] = build()
    nc = _CACHE["nc"]
    in_maps = []
    for c in range(8):
        m = dict(base)
        m["x_in"] = np.ascontiguousarray(x[c].T)
        own = np.concatenate([blocks[c * 129:(c + 1) * 129],
                              blocks[1032 + c * 129:1032 + (c + 1) * 129]], axis=0)
        m["myidx"] = np.ascontiguousarray(
            own.transpose(1, 0, 2).reshape(128, 33024))
        in_maps.append(m)
    res = run_bass_kernel_spmd(nc, in_maps, core_ids=list(range(8)))
    out = np.stack([np.asarray(res.results[c]["y"]).T for c in range(8)], axis=0)
    return np.ascontiguousarray(out.astype(np.float32))


# version 25
# speedup vs baseline: 1.1011x; 1.0023x over previous
import numpy as np
import contextlib
import concourse.bass as bass
import concourse.mybir as mybir
from concourse.bass_utils import run_bass_kernel_spmd

F32 = mybir.dt.float32
BF16 = mybir.dt.bfloat16
I32 = mybir.dt.int32
AF = mybir.ActivationFunctionType
ALU = mybir.AluOpType

D = 2048
SEQ = 2048
NH = 16
HD = 128
INTER = 5504
KC = D // 128
IT = INTER // 128
TS = 512
NSLAB = SEQ // TS
CH = 128
NCH = TS // CH
EPS = 1e-5

ENGS = ("sync", "gpsimd", "vector", "scalar", "tensor")
DQUEUES = ("sync", "scalar", "gpsimd")

DEBUG_EDGES = False


def build():
    nc = bass.Bass("TRN2", target_bir_lowering=False,
                   detect_race_conditions=DEBUG_EDGES)

    x_in = nc.dram_tensor("x_in", [D, SEQ], F32, kind="ExternalInput").ap()
    wq_d = nc.dram_tensor("wq", [D, D], F32, kind="ExternalInput").ap()
    wk_d = nc.dram_tensor("wk", [D, D], F32, kind="ExternalInput").ap()
    wv_d = nc.dram_tensor("wv", [D, D], F32, kind="ExternalInput").ap()
    wb_d = nc.dram_tensor("wb", [D, D], F32, kind="ExternalInput").ap()
    wo_d = nc.dram_tensor("wo", [D, D], F32, kind="ExternalInput").ap()
    lut_d = nc.dram_tensor("lut", [1, 16], F32, kind="ExternalInput").ap()
    y_out = nc.dram_tensor("y", [D, SEQ], F32, kind="ExternalOutput").ap()

    myidx_d = nc.dram_tensor("myidx", [128, 33024], I32, kind="ExternalInput").ap()
    GSZ = (64, 64, 64, 66)
    GW = [s * 128 for s in GSZ]
    GOFFL = [0, 8192, 16384, 24576]
    GOFFG = [0, 65536, 131072, 196608]
    sc_part = [nc.dram_tensor(f"sc_part{g}", [128, GW[g]], BF16,
                              kind="Internal").ap() for g in range(4)]
    sc_allg = [nc.dram_tensor(f"sc_all{g}", [8, 128, GW[g]], BF16,
                              kind="Internal", addr_space="Shared").ap()
               for g in range(4)]
    sc_q = nc.dram_tensor("sc_q", [NH, 128, D], BF16, kind="Internal").ap()
    sc_k = nc.dram_tensor("sc_k", [NH, 128, D], BF16, kind="Internal").ap()
    sc_o = nc.dram_tensor("sc_o", [KC, 128, D], BF16, kind="Internal").ap()
    sc_v = nc.dram_tensor("sc_v", [8, 128, KC * 256], BF16, kind="Internal").ap()
    sc_b = nc.dram_tensor("sc_b", [8, 128, KC * 256], BF16, kind="Internal").ap()

    es = contextlib.ExitStack()
    with es:
        sbuf = lambda name, shp, dt: es.enter_context(nc.sbuf_tensor(name, shp, dt))
        psal = lambda name, shp, dt: es.enter_context(nc.psum_tensor(name, shp, dt))

        x_sb = sbuf("x_sb", [128, KC * TS], F32)
        xln = sbuf("xln", [128, KC * TS], BF16)
        pool = sbuf("pool", [128, 24576], BF16)
        h_sl = pool[:, 0:IT * TS]
        q_fm = pool[:, 0:8192]
        k_fm = pool[:, 8192:16384]
        u_tm = pool[:, 16384:24576]
        o_fm = sbuf("o_fm", [128, 8192], BF16)
        S_f32 = sbuf("S_f32", [128, NH * HD], F32)
        S_bf = sbuf("S_bf", [128, NH * HD], BF16)
        wpool = sbuf("wpool", [128, 20480], BF16)
        wg = [wpool[:, 0:2048], wpool[:, 2048:4096], wpool[:, 16384:18432]]
        wu = [wpool[:, 4096:6144], wpool[:, 6144:8192], wpool[:, 18432:20480]]
        wd = [wpool[:, 8192:8192 + INTER], wpool[:, 8192 + INTER:8192 + 2 * INTER]]
        qkw = [wpool[:, 0:2048], wpool[:, 2048:4096]]
        vbw = [wpool[:, 4096:12288], wpool[:, 12288:20480]]
        sg = sbuf("sg", [128, 2 * TS], BF16)
        sqb = sbuf("sqb", [128, 4 * TS], F32)
        lntmp = sbuf("lntmp", [128, TS], F32)
        ktm = sbuf("ktm", [128, 256], BF16)
        attm = sbuf("attm", [128, 256], BF16)
        stat = sbuf("stat", [1, 6 * TS], F32)
        m_sb = stat[0:1, 0:TS]
        ms_sb = stat[0:1, TS:2 * TS]
        m2_sb = stat[0:1, 2 * TS:3 * TS]
        var_sb = stat[0:1, 3 * TS:4 * TS]
        ln_sb = stat[0:1, 4 * TS:5 * TS]
        rstd_sb = stat[0:1, 5 * TS:6 * TS]
        lut_sb = sbuf("lut_sb", [1, 16], F32)
        lutb = sbuf("lutb", [128, 16], F32)
        ones_c = sbuf("ones_c", [1, 128], F32)
        onesk = sbuf("onesk", [128, 1], F32)
        maskc = sbuf("maskc", [128, 128], BF16)
        identb = sbuf("identb", [128, 128], BF16)
        rawi = sbuf("rawi", [128, 4096], I32)
        bfb = pool[:, 0:4096]
        accb = pool[:, 4096:8192]
        tmpb = pool[:, 8192:10240]
        wtb = pool[:, 10240:14336]
        cwt = pool[:, 14336:18432]
        rawf = x_sb[:, 0:4096]

        ps = [psal(f"ps{i}", [128, 512], F32) for i in range(6)]
        pst = [psal("pst0", [128, 1024], BF16), psal("pst1", [128, 1024], BF16)]

        csem = {e: es.enter_context(nc.semaphore("c_" + e)) for e in ENGS}
        NP = 1 if DEBUG_EDGES else 8
        dpool = {e: [es.enter_context(nc.semaphore(f"d_{e}{i}")) for i in range(NP)]
                 for e in DQUEUES}
        dpc = {e: [0] * NP for e in DQUEUES}
        dseq = {e: 0 for e in DQUEUES}
        cnt = {e: 0 for e in ENGS}
        q = {e: [] for e in ENGS}
        lastw = {e: {} for e in ENGS}

        def ins(e, f, inc=False):
            if DEBUG_EDGES:
                prev = cnt[e]
                h = csem[e]
                if prev:
                    q[e].append(lambda E, h=h, v=prev: E.wait_ge(h, v))
                cnt[e] += 1
                v = cnt[e]
                q[e].append(lambda E, f=f, h=h: f(E).then_inc(h, 1))
                return v
            if inc:
                cnt[e] += 1
                v = cnt[e]
                h = csem[e]
                q[e].append(lambda E, f=f, h=h: f(E).then_inc(h, 1))
                return v
            q[e].append(f)
            return None

        def dma(e, out_ap, in_ap):
            i = dseq[e] % NP
            dseq[e] += 1
            h = dpool[e][i]
            if dpc[e][i]:
                q[e].append(lambda E, h=h, v=dpc[e][i]: E.wait_ge(h, v))
            dpc[e][i] += 16
            v = dpc[e][i]
            q[e].append(lambda E, o=out_ap, i_=in_ap, h=h:
                        E.dma_start(out=o, in_=i_).then_inc(h, 16))
            return (e, i, v)

        def wt(e, kind, oe, v):
            if not v or v <= 0:
                return
            if kind == "c" and oe == e:
                return
            key = (kind, oe)
            if lastw[e].get(key, 0) >= v:
                return
            lastw[e][key] = v
            h = csem[oe]
            q[e].append(lambda E, h=h, v=v: E.wait_ge(h, v))

        def wts(e, v):
            if not v:
                return
            key = ("self",)
            if lastw[e].get(key, 0) >= v:
                return
            lastw[e][key] = v
            q[e].append(lambda E, h=csem[e], v=v: E.wait_ge(h, v))

        def wt_d(e, tokens):
            if tokens is None:
                return
            if isinstance(tokens, tuple):
                tokens = [tokens]
            for (oe, i, v) in tokens:
                key = ("d", oe, i)
                if lastw[e].get(key, 0) >= v:
                    continue
                lastw[e][key] = v
                h = dpool[oe][i]
                q[e].append(lambda E, h=h, v=v: E.wait_ge(h, v))

        def wt_m(e, m):
            if m[0] == "c":
                wt(e, "c", m[1], m[2])
            else:
                wt_d(e, m[1])

        pbank = {i: {"w": ("tensor", 0), "r": {}}
                 for i in list(range(6)) + ["t0", "t1"]}

        def mm_first(bank):
            for oe, v in pbank[bank]["r"].items():
                wt("tensor", "c", oe, v)
            pbank[bank]["r"] = {}

        def mm_last(bank, v):
            pbank[bank]["w"] = ("tensor", v)

        def prd(bank, e, f):
            we, wv = pbank[bank]["w"]
            wt(e, "c", we, wv)
            v = ins(e, f, inc=True)
            pbank[bank]["r"][e] = max(pbank[bank]["r"].get(e, 0), v)
            return v

        ins("gpsimd", lambda E: E.memset(ones_c[:, :], 1.0))
        ins("gpsimd", lambda E: E.memset(onesk[:, :], 1.0 / D))
        g1 = ins("gpsimd", lambda E: E.memset(identb[:, :], 0.0), inc=True)
        q["gpsimd"].append(lambda E, h=csem["gpsimd"], v=g1: E.wait_ge(h, v))
        ins("gpsimd", lambda E: E.affine_select(
            out=identb[:, :], in_=identb[:, :], compare_op=ALU.not_equal,
            fill=1.0, base=0, pattern=[[-1, 128]], channel_multiplier=1))
        g2 = ins("gpsimd", lambda E: E.memset(maskc[:, :], 1.0), inc=True)
        q["gpsimd"].append(lambda E, h=csem["gpsimd"], v=g2: E.wait_ge(h, v))
        ins("gpsimd", lambda E: E.affine_select(
            out=maskc[:, :], in_=maskc[:, :], compare_op=ALU.is_ge,
            fill=0.0, base=0, pattern=[[1, 128]], channel_multiplier=-1))
        ins("gpsimd", lambda E: E.memset(S_f32[:, :], 0.0))
        v_init = ins("gpsimd", lambda E: E.memset(S_bf[:, :], 0.0), inc=True)

        d_lut = dma("sync", lut_sb[:, :], lut_d)
        wt_d("tensor", d_lut)
        wt("tensor", "c", "gpsimd", v_init)
        mm_first(5)
        t_lut = ins("tensor", lambda E: E.matmul(
            ps[5][:, 0:16], ones_c[:, :], lut_sb[:, :], start=True, stop=True), inc=True)
        mm_last(5, t_lut)
        v_lut = prd(5, "vector", lambda E: E.tensor_copy(out=lutb[:, :], in_=ps[5][:, 0:16]))
        q["vector"].append(lambda E, h=csem["vector"], v=v_lut: E.wait_ge(h, v))
        wt("vector", "c", "gpsimd", v_init)

        mat = [(c, 2048) for c in range(16)] + [(16, 256)]
        NMAT = len(mat)

        cast_toks = {wn: [] for wn in ("q", "k", "v", "b", "o")}
        for wn, wdram in (("q", wq_d), ("k", wk_d), ("v", wv_d), ("b", wb_d),
                          ("o", wo_d)):
            for kc in range(KC):
                if wn in ("q", "k", "o"):
                    scd = {"q": sc_q, "k": sc_k, "o": sc_o}[wn]
                    tok = dma(
                        "gpsimd",
                        scd[:, :, kc * 128:(kc + 1) * 128].rearrange("h p c -> p h c"),
                        wdram[kc * 128:(kc + 1) * 128, :].rearrange(
                            "p (h c) -> p h c", c=128))
                else:
                    scd = sc_v if wn == "v" else sc_b
                    tok = dma(
                        "gpsimd",
                        scd[:, :, kc * 256:(kc + 1) * 256].rearrange("s p c -> p s c"),
                        wdram[kc * 128:(kc + 1) * 128, :].rearrange(
                            "p (s c) -> p s c", c=256))
                cast_toks[wn].append(tok)

        v_cast = {}
        t_tp = {}
        a_ev = {}
        g_wr = {}
        nbs = [w // 128 for (_, w) in mat]

        idx_ld = {}

        def emit_idx_load(k):
            c, w = mat[k]
            b = k % 2
            if k >= 2:
                wt("sync", "c", "vector", v_cast[k - 2])
            idx_ld[k] = dma("sync", rawi[:, b * 2048:b * 2048 + w],
                            myidx_d[:, c * 2048:c * 2048 + w])

        emit_idx_load(0)
        emit_idx_load(1)

        def emit_mat(k):
            c, w = mat[k]
            b = k % 2
            nb = nbs[k]
            ld = idx_ld[k]
            wt_d("vector", ld)
            v_cast[k] = ins("vector", lambda E, b=b, w=w: E.tensor_copy(
                out=bfb[:, b * 2048:b * 2048 + w],
                in_=rawi[:, b * 2048:b * 2048 + w]), inc=True)
            if k + 2 < NMAT:
                emit_idx_load(k + 2)
            if k >= 2:
                wt("vector", "c", "tensor", t_tp[(k - 2, nbs[k - 2] - 1)])
            for j16 in range(16):
                sc_ap = lutb[:, j16:j16 + 1]
                if j16 == 0:
                    ins("vector", lambda E, b=b, w=w, jj=j16, s=sc_ap: E.tensor_scalar(
                        out=accb[:, b * 2048:b * 2048 + w],
                        in0=bfb[:, b * 2048:b * 2048 + w],
                        scalar1=float(jj), scalar2=s,
                        op0=ALU.is_equal, op1=ALU.mult))
                else:
                    ins("vector", lambda E, b=b, w=w, jj=j16, s=sc_ap: E.tensor_scalar(
                        out=tmpb[:, 0:w],
                        in0=bfb[:, b * 2048:b * 2048 + w],
                        scalar1=float(jj), scalar2=s,
                        op0=ALU.is_equal, op1=ALU.mult))
                    ins("vector", lambda E, b=b, w=w: E.tensor_add(
                        accb[:, b * 2048:b * 2048 + w],
                        accb[:, b * 2048:b * 2048 + w],
                        tmpb[:, 0:w]))
            v_sel = ins("vector", lambda E: E.engine_nop(), inc=True)
            wt("tensor", "c", "vector", v_sel)
            if k >= 1:
                wt("tensor", "c", "scalar", a_ev[(k - 1, nbs[k - 1] - 1)])
            if k >= 2:
                wt_d("scalar", g_wr[k - 2])
            for j in range(nb):
                slot = j % 2
                if j >= 2:
                    wt("tensor", "c", "scalar", a_ev[(k, j - 2)])
                t_tp[(k, j)] = ins("tensor", lambda E, b=b, j=j, slot=slot: E.transpose(
                    pst[slot][:, 0:128],
                    accb[:, b * 2048 + j * 128:b * 2048 + (j + 1) * 128],
                    identb[:, :]), inc=True)
                wt("scalar", "c", "tensor", t_tp[(k, j)])
                a_ev[(k, j)] = ins("scalar", lambda E, b=b, j=j, slot=slot: E.copy(
                    out=wtb[:, b * 2048 + j * 128:b * 2048 + (j + 1) * 128],
                    in_=pst[slot][:, 0:128]), inc=True)
            wts("scalar", a_ev[(k, nb - 1)])
            L0 = c * 2048
            gg = min(L0 // 8192, 3)
            g_wr[k] = [dma("scalar", sc_part[gg][:, L0 - GOFFL[gg]:
                                                 L0 - GOFFL[gg] + w],
                           wtb[:, b * 2048:b * 2048 + w])]

        v_ccg = []
        for k in range(NMAT):
            emit_mat(k)
            c, _ = mat[k]
            if c in (3, 7, 11, 16):
                gg = len(v_ccg)
                wt_d("gpsimd", [t for kk in range(k + 1) for t in g_wr[kk]])
                v_ccg.append(ins("gpsimd", lambda E, gg=gg: E.collective_compute(
                    "AllGather", ALU.bypass, replica_groups=[list(range(8))],
                    ins=[sc_part[gg].opt()], outs=[sc_allg[gg].opt()]),
                    inc=True))
        for i in range(NP):
            if dpc["scalar"][i]:
                q["scalar"].append(lambda E, h=dpool["scalar"][i], v=dpc["scalar"][i]:
                                   E.wait_ge(h, v))
        cast_all_done = ins("scalar", lambda E: E.nop(), inc=True)

        PARTC = 33024
        GOFF = 0
        UOFF = IT * 2048
        DOFF = 2 * IT * 2048

        def load_image(dst_ap_fn, col0, w):
            toks = []
            off = 0
            while w > 0:
                g = min(col0 // 65536, 3)
                rem = col0 - GOFFG[g]
                r = rem // GW[g]
                lc = rem % GW[g]
                take = min(w, GW[g] - lc)
                toks.append(dma("sync", dst_ap_fn(off, take),
                                sc_allg[g][r, :, lc:lc + take]))
                col0 += take
                off += take
                w -= take
            return toks

        def wt_cc(e, col0, w):
            for g in range(4):
                if col0 < GOFFG[g] + 8 * GW[g] and col0 + w > GOFFG[g]:
                    wt(e, "c", "gpsimd", v_ccg[g])
        cast_all_done = ins("scalar", lambda E: E.nop(), inc=True)

        t_o = {}
        t_sup = {}
        a_oev = {}
        a_ktm = {}
        v_att = {}
        v_scast = {}
        v_dn = {}
        v_wo = {}
        t_gu = {}
        t_dn = {}
        ffn_i = [0]
        d_x = {}
        st_x = {}

        def ln_phase(marker):
            mm_first(4)
            vm = None
            for dc in range(KC):
                wt_m("tensor", marker(dc))
                last = dc == KC - 1
                vm = ins("tensor", lambda E, dc=dc, first=(dc == 0), last=last: E.matmul(
                    ps[4][0:1, 0:TS], onesk[:, :], x_sb[:, dc * TS:(dc + 1) * TS],
                    start=first, stop=last), inc=last)
            mm_last(4, vm)
            mm_first(5)
            a_sq = {}
            t_sqm = {}
            for dc in range(KC):
                wt_m("scalar", marker(dc))
                if dc >= 4:
                    wt("scalar", "c", "tensor", t_sqm[dc - 4])
                a_sq[dc] = ins("scalar", lambda E, dc=dc: E.activation(
                    out=sqb[:, (dc % 4) * TS:(dc % 4 + 1) * TS],
                    in_=x_sb[:, dc * TS:(dc + 1) * TS], func=AF.Square), inc=True)
                wt("tensor", "c", "scalar", a_sq[dc])
                last = dc == KC - 1
                t_sqm[dc] = ins("tensor", lambda E, dc=dc, first=(dc == 0), last=last: E.matmul(
                    ps[5][0:1, 0:TS], onesk[:, :],
                    sqb[:, (dc % 4) * TS:(dc % 4 + 1) * TS],
                    start=first, stop=last), inc=True)
            mm_last(5, t_sqm[KC - 1])
            prd(4, "vector", lambda E: E.tensor_copy(out=m_sb, in_=ps[4][0:1, 0:TS]))
            mm_first(4)
            t_bm = ins("tensor", lambda E: E.matmul(
                ps[4][:, 0:TS], ones_c[:, :], m_sb, start=True, stop=True), inc=True)
            mm_last(4, t_bm)
            prd(5, "vector", lambda E: E.tensor_copy(out=ms_sb, in_=ps[5][0:1, 0:TS]))
            ins("vector", lambda E: E.tensor_mul(m2_sb, m_sb, m_sb))
            ins("vector", lambda E: E.tensor_sub(var_sb, ms_sb, m2_sb))
            v_var = ins("vector", lambda E: E.tensor_scalar_add(var_sb, var_sb, EPS), inc=True)
            wt("scalar", "c", "vector", v_var)
            a_ln = ins("scalar", lambda E: E.activation(out=ln_sb, in_=var_sb, func=AF.Ln), inc=True)
            wt("vector", "c", "scalar", a_ln)
            v_l2 = ins("vector", lambda E: E.tensor_scalar_mul(ln_sb, ln_sb, -0.5), inc=True)
            wt("scalar", "c", "vector", v_l2)
            a_rstd = ins("scalar", lambda E: E.activation(
                out=rstd_sb, in_=ln_sb, func=AF.Exp), inc=True)
            wt("tensor", "c", "scalar", a_rstd)
            mm_first(5)
            t_br = ins("tensor", lambda E: E.matmul(
                ps[5][:, 0:TS], ones_c[:, :], rstd_sb, start=True, stop=True), inc=True)
            mm_last(5, t_br)
            wt("vector", "c", "tensor", t_br)
            v_app = []
            for dc in range(KC):
                wt_m("vector", marker(dc))
                ins("vector", lambda E, dc=dc: E.tensor_sub(
                    lntmp[:, :], x_sb[:, dc * TS:(dc + 1) * TS], ps[4][:, 0:TS]))
                v_app.append(ins("vector", lambda E, dc=dc: E.tensor_mul(
                    xln[:, dc * TS:(dc + 1) * TS], lntmp[:, :], ps[5][:, 0:TS]), inc=True))
            pbank[4]["r"]["vector"] = v_app[-1]
            pbank[5]["r"]["vector"] = v_app[-1]
            return v_app

        a_silu = {}
        v_h = {}

        def ffn_phase(v_xln):
            fi = ffn_i[0]
            ffn_i[0] += 1
            t_enter = cnt["tensor"]
            v_enter = cnt["vector"]
            d_g = {}
            d_u = {}

            def emit_gu_load(it):
                b = it % 3
                if it >= 3:
                    wt("sync", "c", "tensor", t_gu[(fi, it - 3)])
                else:
                    wt("sync", "c", "tensor", t_enter)
                wt_cc("sync", GOFF + it * 2048, 2048)
                wt_cc("sync", UOFF + it * 2048, 2048)
                d_g[it] = load_image(
                    lambda off, take, b=b: wg[b][:, off:off + take],
                    GOFF + it * 2048, 2048)
                d_u[it] = load_image(
                    lambda off, take, b=b: wu[b][:, off:off + take],
                    UOFF + it * 2048, 2048)

            d_d = {}

            def emit_d_load(dt):
                b = dt % 2
                if dt >= 2:
                    wt("sync", "c", "tensor", t_dn[(fi, dt - 2)])
                elif dt == 1:
                    wt("sync", "c", "tensor", t_gu[(fi, IT - 2)])
                else:
                    wt("sync", "c", "tensor", t_enter)
                wt_cc("sync", DOFF + dt * INTER, INTER)
                d_d[dt] = load_image(
                    lambda off, take, b=b: wd[b][:, off:off + take],
                    DOFF + dt * INTER, INTER)

            emit_gu_load(0)
            emit_gu_load(1)
            emit_gu_load(2)
            for it in range(IT):
                b = it % 3
                b2 = it % 2
                bg, bu = 2 * b2, 2 * b2 + 1
                wt_d("tensor", d_g[it] + d_u[it])
                mm_first(bg)
                tg = None
                for kc in range(KC):
                    wt("tensor", "c", "vector", v_xln[kc])
                    last = kc == KC - 1
                    tg = ins("tensor", lambda E, kc=kc, b=b, bg=bg,
                             first=(kc == 0), last=last: E.matmul(
                        ps[bg][:, 0:TS], wg[b][:, kc * 128:(kc + 1) * 128],
                        xln[:, kc * TS:(kc + 1) * TS],
                        start=first, stop=last), inc=last)
                mm_last(bg, tg)
                mm_first(bu)
                tu = None
                for kc in range(KC):
                    last = kc == KC - 1
                    tu = ins("tensor", lambda E, kc=kc, b=b, bu=bu,
                             first=(kc == 0), last=last: E.matmul(
                        ps[bu][:, 0:TS], wu[b][:, kc * 128:(kc + 1) * 128],
                        xln[:, kc * TS:(kc + 1) * TS],
                        start=first, stop=last), inc=last)
                mm_last(bu, tu)
                t_gu[(fi, it)] = tu
                if it >= 2:
                    wt("scalar", "c", "vector", v_h[(fi, it - 2)])
                else:
                    wt("scalar", "c", "vector", v_enter)
                a_silu[(fi, it)] = prd(bg, "scalar", lambda E, b2=b2, bg=bg: E.activation(
                    out=sg[:, b2 * TS:(b2 + 1) * TS], in_=ps[bg][:, 0:TS], func=AF.Silu))
                wt("vector", "c", "scalar", a_silu[(fi, it)])
                if fi == 0:
                    wt("vector", "c", "scalar", cast_all_done)
                v_h[(fi, it)] = prd(bu, "vector", lambda E, it=it, b2=b2, bu=bu: E.tensor_mul(
                    h_sl[:, it * TS:(it + 1) * TS], sg[:, b2 * TS:(b2 + 1) * TS],
                    ps[bu][:, 0:TS]))
                if it + 3 < IT:
                    emit_gu_load(it + 3)
                if it == 6:
                    emit_d_load(0)
            emit_d_load(1)
            for dt in range(KC):
                b = dt % 2
                bank = dt % 2
                wt_d("tensor", d_d[dt])
                wt("tensor", "c", "vector", v_h[(fi, IT - 1)])
                mm_first(bank)
                td = None
                for ic in range(IT):
                    last = ic == IT - 1
                    td = ins("tensor", lambda E, ic=ic, b=b, bank=bank,
                             first=(ic == 0), last=last: E.matmul(
                        ps[bank][:, 0:TS], wd[b][:, ic * 128:(ic + 1) * 128],
                        h_sl[:, ic * TS:(ic + 1) * TS],
                        start=first, stop=last), inc=last)
                mm_last(bank, td)
                t_dn[(fi, dt)] = td
                v_dn[(fi, dt)] = prd(bank, "vector", lambda E, dt=dt, bank=bank:
                                     E.scalar_tensor_tensor(
                    out=x_sb[:, dt * TS:(dt + 1) * TS],
                    in0=ps[bank][:, 0:TS], scalar=0.5,
                    in1=x_sb[:, dt * TS:(dt + 1) * TS],
                    op0=ALU.mult, op1=ALU.add))
                if dt + 2 < KC:
                    emit_d_load(dt + 2)
            return fi

        def attn_phase(s, v_xln):
            t_enter = cnt["tensor"]
            v_enter = cnt["vector"]
            t_qk = {}
            a_qk = {}
            d_qk = {}
            qk_srcs = [(sc_q, q_fm, hh) for hh in range(NH)] + \
                      [(sc_k, k_fm, hh) for hh in range(NH)]

            def emit_qk_load(i):
                scd, dstv, hh = qk_srcs[i]
                b = i % 2
                if i >= 2:
                    wt("sync", "c", "tensor", t_qk[i - 2])
                else:
                    wt("sync", "c", "tensor", t_enter)
                wt_d("sync", cast_toks["q"] + cast_toks["k"])
                d_qk[i] = dma("sync", qkw[b][:, :], scd[hh, :, :])

            v_u_last = 0
            t_vb_last = {}
            a_sig_first = [True]
            d_vb = {}
            vb4 = [vbw[0][:, 0:4096], vbw[0][:, 4096:8192],
                   vbw[1][:, 0:4096], vbw[1][:, 4096:8192]]

            def emit_vb_load(sl8):
                if sl8 >= 2:
                    wt("sync", "c", "tensor", t_vb_last[sl8 - 2])
                else:
                    wt("sync", "c", "tensor", t_enter)
                    wt_d("sync", cast_toks["v"] + cast_toks["b"])
                bq = (sl8 % 2) * 2
                d_vb[sl8] = [dma("sync", vb4[bq], sc_v[sl8]),
                             dma("sync", vb4[bq + 1], sc_b[sl8])]

            emit_qk_load(0)
            emit_qk_load(1)
            emit_vb_load(0)
            emit_vb_load(1)
            for i in range(2 * NH):
                scd, dstv, hh = qk_srcs[i]
                b = i % 2
                bank = i % 2
                wt_d("tensor", d_qk[i])
                mm_first(bank)
                tl = None
                for kc in range(KC):
                    wt("tensor", "c", "vector", v_xln[kc])
                    last = kc == KC - 1
                    tl = ins("tensor", lambda E, kc=kc, b=b, bank=bank,
                             first=(kc == 0), last=last: E.matmul(
                        ps[bank][:, 0:TS], qkw[b][:, kc * 128:(kc + 1) * 128],
                        xln[:, kc * TS:(kc + 1) * TS],
                        start=first, stop=last), inc=last)
                mm_last(bank, tl)
                t_qk[i] = tl
                a_qk[i] = prd(bank, "scalar", lambda E, hh=hh, bank=bank, dstv=dstv:
                              E.copy(out=dstv[:, hh * TS:(hh + 1) * TS],
                                     in_=ps[bank][:, 0:TS]))
                if i + 2 < 2 * NH:
                    emit_qk_load(i + 2)
            for sl8 in range(8):
                bq = (sl8 % 2) * 2
                for tt in range(NCH):
                    bv, bb = 2 * (tt % 2), 2 * (tt % 2) + 1
                    wt_d("tensor", d_vb[sl8])
                    mm_first(bv)
                    tv = None
                    for kc in range(KC):
                        last = kc == KC - 1
                        tv = ins("tensor", lambda E, kc=kc, tt=tt, bv=bv, bq=bq,
                                 first=(kc == 0), last=last: E.matmul(
                            ps[bv][:, 0:256],
                            xln[:, kc * TS + tt * 128:kc * TS + (tt + 1) * 128],
                            vb4[bq][:, kc * 256:(kc + 1) * 256],
                            start=first, stop=last), inc=last)
                    mm_last(bv, tv)
                    mm_first(bb)
                    tb = None
                    for kc in range(KC):
                        last = kc == KC - 1
                        tb = ins("tensor", lambda E, kc=kc, tt=tt, bb=bb, bq=bq,
                                 first=(kc == 0), last=last: E.matmul(
                            ps[bb][:, 0:256],
                            xln[:, kc * TS + tt * 128:kc * TS + (tt + 1) * 128],
                            vb4[bq + 1][:, kc * 256:(kc + 1) * 256],
                            start=first, stop=last), inc=last)
                    mm_last(bb, tb)
                    t_vb_last[sl8] = tb
                    if a_sig_first[0]:
                        wt("scalar", "c", "vector", v_enter)
                        a_sig_first[0] = False
                    else:
                        wt("scalar", "c", "vector", v_u_last)
                    a_sig = prd(bb, "scalar", lambda E, bb=bb: E.activation(
                        out=sg[:, 0:256], in_=ps[bb][:, 0:256], func=AF.Sigmoid))
                    wt("vector", "c", "scalar", a_sig)
                    v_u_last = prd(bv, "vector", lambda E, sl8=sl8, tt=tt, bv=bv:
                                   E.tensor_mul(
                        u_tm[:, tt * D + sl8 * 256:tt * D + (sl8 + 1) * 256],
                        sg[:, 0:256], ps[bv][:, 0:256]))
                if sl8 + 2 < 8:
                    emit_vb_load(sl8 + 2)
            for bk in (0, 1, 2, 3, 4, 5, "t0", "t1"):
                for oe, v in pbank[bk]["r"].items():
                    wt("tensor", "c", oe, v)
                pbank[bk]["r"] = {}

            def head_tail(gc, c, hh):
                bo = 2 if hh % 2 == 0 else 5
                wt("tensor", "c", "vector", v_att[(gc, hh)])
                if (gc, hh - 2) in a_oev:
                    wt("tensor", "c", "scalar", a_oev[(gc, hh - 2)])
                elif (gc - 1, NH - 2 + (hh % 2)) in a_oev:
                    wt("tensor", "c", "scalar", a_oev[(gc - 1, NH - 2 + (hh % 2))])
                if (gc - 1, hh) in v_scast:
                    wt("tensor", "c", "vector", v_scast[(gc - 1, hh)])
                ins("tensor", lambda E, c=c, hh=hh, bo=bo: E.matmul(
                    ps[bo][:, 0:128],
                    u_tm[:, c * D + hh * HD:c * D + (hh + 1) * HD],
                    attm[:, (hh % 2) * 128:(hh % 2 + 1) * 128],
                    start=True, stop=False))
                t_o[(gc, hh)] = ins("tensor", lambda E, c=c, hh=hh, bo=bo: E.matmul(
                    ps[bo][:, 0:128],
                    S_bf[:, hh * HD:(hh + 1) * HD],
                    q_fm[:, hh * TS + c * 128:hh * TS + (c + 1) * 128],
                    start=False, stop=True), inc=True)
                bs = 3 + (hh % 2)
                wt("tensor", "c", "scalar", a_ktm[(gc, hh)])
                if (gc, hh - 2) in v_scast:
                    wt("tensor", "c", "vector", v_scast[(gc, hh - 2)])
                elif c > 0:
                    wt("tensor", "c", "vector", v_scast[(gc - 1, NH - 2 + (hh % 2))])
                t_sup[(gc, hh)] = ins("tensor", lambda E, c=c, hh=hh, bs=bs: E.matmul(
                    ps[bs][:, 0:128],
                    ktm[:, (hh % 2) * 128:(hh % 2 + 1) * 128],
                    u_tm[:, c * D + hh * HD:c * D + (hh + 1) * HD],
                    start=True, stop=True), inc=True)
                wt("scalar", "c", "tensor", t_o[(gc, hh)])
                a_oev[(gc, hh)] = ins("scalar", lambda E, c=c, hh=hh, bo=bo: E.copy(
                    out=o_fm[:, hh * TS + c * 128:hh * TS + (c + 1) * 128],
                    in_=ps[bo][:, 0:128]), inc=True)
                wt("vector", "c", "tensor", t_sup[(gc, hh)])
                ins("vector", lambda E, hh=hh, bs=bs: E.tensor_add(
                    S_f32[:, hh * HD:(hh + 1) * HD],
                    S_f32[:, hh * HD:(hh + 1) * HD],
                    ps[bs][:, 0:128]))
                v_scast[(gc, hh)] = ins("vector", lambda E, hh=hh: E.tensor_copy(
                    out=S_bf[:, hh * HD:(hh + 1) * HD],
                    in_=S_f32[:, hh * HD:(hh + 1) * HD]), inc=True)

            for c in range(NCH):
                gc = s * NCH + c
                for hh in range(NH):
                    slot = hh % 2
                    ba = hh % 2
                    if (gc, hh - 2) in a_ktm:
                        wt("tensor", "c", "scalar", a_ktm[(gc, hh - 2)])
                    elif (gc - 1, NH - 2 + slot) in a_ktm:
                        wt("tensor", "c", "scalar", a_ktm[(gc - 1, NH - 2 + slot)])
                    wt("tensor", "c", "scalar", a_qk[NH + hh])
                    wt("tensor", "c", "vector", v_u_last)
                    t_ktp = ins("tensor", lambda E, c=c, hh=hh, slot=slot: E.transpose(
                        pst[slot][:, 0:128],
                        k_fm[:, hh * TS + c * 128:hh * TS + (c + 1) * 128],
                        identb[:, :]), inc=True)
                    if (gc, hh - 2) in v_att:
                        wt("tensor", "c", "vector", v_att[(gc, hh - 2)])
                    elif c > 0:
                        wt("tensor", "c", "vector", v_att[(gc - 1, NH - 2 + ba)])
                    wt("tensor", "c", "scalar", a_qk[hh])
                    t_att = ins("tensor", lambda E, c=c, hh=hh, ba=ba: E.matmul(
                        ps[ba][:, 0:128],
                        k_fm[:, hh * TS + c * 128:hh * TS + (c + 1) * 128],
                        q_fm[:, hh * TS + c * 128:hh * TS + (c + 1) * 128],
                        start=True, stop=True), inc=True)
                    wt("scalar", "c", "tensor", t_ktp)
                    if (gc, hh - 2) in t_sup:
                        wt("scalar", "c", "tensor", t_sup[(gc, hh - 2)])
                    elif (gc - 1, NH - 2 + slot) in t_sup:
                        wt("scalar", "c", "tensor", t_sup[(gc - 1, NH - 2 + slot)])
                    a_ktm[(gc, hh)] = ins("scalar", lambda E, slot=slot: E.copy(
                        out=ktm[:, slot * 128:(slot + 1) * 128],
                        in_=pst[slot][:, 0:128]), inc=True)
                    wt("vector", "c", "tensor", t_att)
                    if (gc, hh - 2) in t_o:
                        wt("vector", "c", "tensor", t_o[(gc, hh - 2)])
                    elif (gc - 1, NH - 2 + ba) in t_o:
                        wt("vector", "c", "tensor", t_o[(gc - 1, NH - 2 + ba)])
                    v_att[(gc, hh)] = ins("vector", lambda E, ba=ba, slot=slot: E.tensor_mul(
                        attm[:, slot * 128:(slot + 1) * 128],
                        ps[ba][:, 0:128], maskc[:, :]), inc=True)
                    if hh > 0:
                        head_tail(gc, c, hh - 1)
                head_tail(gc, c, NH - 1)
            for bk in (0, 1, 2, 3, 4, 5):
                pbank[bk]["w"] = ("tensor", cnt["tensor"])
            pbank[0]["r"]["vector"] = cnt["vector"]
            pbank[1]["r"]["vector"] = cnt["vector"]
            pbank[2]["r"]["scalar"] = cnt["scalar"]
            pbank[5]["r"]["scalar"] = cnt["scalar"]
            pbank[3]["r"]["vector"] = cnt["vector"]
            pbank[4]["r"]["vector"] = cnt["vector"]
            pbank["t0"]["r"]["scalar"] = cnt["scalar"]
            pbank["t1"]["r"]["scalar"] = cnt["scalar"]

            gcl = s * NCH + NCH - 1
            t_wo = {}
            d_wo = {}

            def emit_wo_load(dt):
                b = dt % 2
                if dt >= 2:
                    wt("sync", "c", "tensor", t_wo[dt - 2])
                else:
                    wt("sync", "c", "tensor", t_qk[2 * NH - 2 + b])
                    wt_d("sync", cast_toks["o"])
                d_wo[dt] = dma("sync", qkw[b][:, :], sc_o[dt, :, :])

            emit_wo_load(0)
            emit_wo_load(1)
            for dt in range(KC):
                b = dt % 2
                bank = dt % 2
                wt_d("tensor", d_wo[dt])
                wt("tensor", "c", "scalar", a_oev[(gcl, NH - 1)])
                mm_first(bank)
                tw = None
                for kc in range(KC):
                    last = kc == KC - 1
                    tw = ins("tensor", lambda E, kc=kc, b=b, bank=bank,
                             first=(kc == 0), last=last: E.matmul(
                        ps[bank][:, 0:TS], qkw[b][:, kc * 128:(kc + 1) * 128],
                        o_fm[:, kc * TS:(kc + 1) * TS],
                        start=first, stop=last), inc=last)
                mm_last(bank, tw)
                t_wo[dt] = tw
                v_wo[(s, dt)] = prd(bank, "vector", lambda E, dt=dt, bank=bank:
                                    E.tensor_add(
                    x_sb[:, dt * TS:(dt + 1) * TS],
                    x_sb[:, dt * TS:(dt + 1) * TS],
                    ps[bank][:, 0:TS]))
                if dt + 2 < KC:
                    emit_wo_load(dt + 2)

        for s in range(NSLAB):
            for dc in range(KC):
                if s > 0:
                    wt_d("sync", st_x[(s - 1, dc)])
                d_x[(s, dc)] = dma("sync", x_sb[:, dc * TS:(dc + 1) * TS],
                                   x_in[dc * 128:(dc + 1) * 128, s * TS:(s + 1) * TS])

            v_xln1 = ln_phase(lambda dc, s=s: ("d", d_x[(s, dc)]))
            fi1 = ffn_phase(v_xln1)
            v_xln2 = ln_phase(lambda dc, fi=fi1: ("c", "vector", v_dn[(fi, dc)]))
            attn_phase(s, v_xln2)
            v_xln3 = ln_phase(lambda dc, s=s: ("c", "vector", v_wo[(s, dc)]))
            fi2 = ffn_phase(v_xln3)

            for dc in range(KC):
                wt("sync", "c", "vector", v_dn[(fi2, dc)])
                st_x[(s, dc)] = dma("sync",
                                    y_out[dc * 128:(dc + 1) * 128, s * TS:(s + 1) * TS],
                                    x_sb[:, dc * TS:(dc + 1) * TS])

        for e in DQUEUES:
            for i in range(NP):
                if dpc[e][i]:
                    q[e].append(lambda E, h=dpool[e][i], v=dpc[e][i]: E.wait_ge(h, v))

        block = es.enter_context(nc.Block())

        @block.sync
        def _(E):
            for f in q["sync"]:
                f(E)

        @block.gpsimd
        def _(E):
            for f in q["gpsimd"]:
                f(E)

        @block.vector
        def _(E):
            for f in q["vector"]:
                f(E)

        @block.scalar
        def _(E):
            for f in q["scalar"]:
                f(E)

        @block.tensor
        def _(E):
            for f in q["tensor"]:
                f(E)

    return nc


_CACHE = {}


def kernel(L0_attn_hidden_states, L0_attn_Wq, L0_attn_Wk, L0_attn_Wv,
           L0_attn_Wb, L0_attn_Wo, L0_ffn_lut, L0_ffn_gate_idx,
           L0_ffn_up_idx, L0_ffn_down_idx):
    x = np.asarray(L0_attn_hidden_states, dtype=np.float32)
    gidx = np.asarray(L0_ffn_gate_idx, dtype=np.int32)
    uidx = np.asarray(L0_ffn_up_idx, dtype=np.int32)
    didx = np.asarray(L0_ffn_down_idx, dtype=np.int32)
    gb = gidx.reshape(IT, 128, KC, 128).transpose(0, 2, 1, 3).reshape(-1, 128, 128)
    ub = uidx.reshape(IT, 128, KC, 128).transpose(0, 2, 1, 3).reshape(-1, 128, 128)
    db = didx.reshape(KC, 128, IT, 128).transpose(0, 2, 1, 3).reshape(-1, 128, 128)
    blocks = np.concatenate([gb, ub, db], axis=0)
    base = {
        "wq": np.ascontiguousarray(np.asarray(L0_attn_Wq, dtype=np.float32)),
        "wk": np.ascontiguousarray(np.asarray(L0_attn_Wk, dtype=np.float32)),
        "wv": np.ascontiguousarray(np.asarray(L0_attn_Wv, dtype=np.float32)),
        "wb": np.ascontiguousarray(np.asarray(L0_attn_Wb, dtype=np.float32)),
        "wo": np.ascontiguousarray(np.asarray(L0_attn_Wo, dtype=np.float32)),
        "lut": np.ascontiguousarray(np.asarray(L0_ffn_lut, dtype=np.float32).reshape(1, 16)),
    }
    if "nc" not in _CACHE:
        _CACHE["nc"] = build()
    nc = _CACHE["nc"]
    in_maps = []
    for c in range(8):
        m = dict(base)
        m["x_in"] = np.ascontiguousarray(x[c].T)
        own = np.concatenate([blocks[512 * 0 + c * 64:512 * 0 + (c + 1) * 64],
                              blocks[512 * 1 + c * 64:512 * 1 + (c + 1) * 64],
                              blocks[512 * 2 + c * 64:512 * 2 + (c + 1) * 64],
                              blocks[1536 + c * 66:1536 + (c + 1) * 66]], axis=0)
        m["myidx"] = np.ascontiguousarray(
            own.transpose(1, 0, 2).reshape(128, 33024))
        in_maps.append(m)
    res = run_bass_kernel_spmd(nc, in_maps, core_ids=list(range(8)))
    out = np.stack([np.asarray(res.results[c]["y"]).T for c in range(8)], axis=0)
    return np.ascontiguousarray(out.astype(np.float32))


# version 26 (final)
# speedup vs baseline: 1.1084x; 1.0066x over previous
import numpy as np
import contextlib
import concourse.bass as bass
import concourse.mybir as mybir
from concourse.bass_utils import run_bass_kernel_spmd

F32 = mybir.dt.float32
BF16 = mybir.dt.bfloat16
I32 = mybir.dt.int32
AF = mybir.ActivationFunctionType
ALU = mybir.AluOpType

D = 2048
SEQ = 2048
NH = 16
HD = 128
INTER = 5504
KC = D // 128
IT = INTER // 128
TS = 512
NSLAB = SEQ // TS
CH = 128
NCH = TS // CH
EPS = 1e-5

ENGS = ("sync", "gpsimd", "vector", "scalar", "tensor")
DQUEUES = ("sync", "scalar", "gpsimd")

DEBUG_EDGES = False


def build():
    nc = bass.Bass("TRN2", target_bir_lowering=False,
                   detect_race_conditions=DEBUG_EDGES)

    x_in = nc.dram_tensor("x_in", [D, SEQ], F32, kind="ExternalInput").ap()
    wq_d = nc.dram_tensor("wq", [D, D], F32, kind="ExternalInput").ap()
    wk_d = nc.dram_tensor("wk", [D, D], F32, kind="ExternalInput").ap()
    wv_d = nc.dram_tensor("wv", [D, D], F32, kind="ExternalInput").ap()
    wb_d = nc.dram_tensor("wb", [D, D], F32, kind="ExternalInput").ap()
    wo_d = nc.dram_tensor("wo", [D, D], F32, kind="ExternalInput").ap()
    lut_d = nc.dram_tensor("lut", [1, 16], F32, kind="ExternalInput").ap()
    y_out = nc.dram_tensor("y", [D, SEQ], F32, kind="ExternalOutput").ap()

    myidx_d = nc.dram_tensor("myidx", [128, 33024], I32, kind="ExternalInput").ap()
    GSZ = (64, 64, 64, 66)
    GW = [s * 128 for s in GSZ]
    GOFFL = [0, 8192, 16384, 24576]
    GOFFG = [0, 65536, 131072, 196608]
    sc_part = [nc.dram_tensor(f"sc_part{g}", [128, GW[g]], BF16,
                              kind="Internal").ap() for g in range(4)]
    sc_allg = [nc.dram_tensor(f"sc_all{g}", [8, 128, GW[g]], BF16,
                              kind="Internal", addr_space="Shared").ap()
               for g in range(4)]
    sc_q = nc.dram_tensor("sc_q", [NH, 128, D], BF16, kind="Internal").ap()
    sc_k = nc.dram_tensor("sc_k", [NH, 128, D], BF16, kind="Internal").ap()
    sc_o = nc.dram_tensor("sc_o", [KC, 128, D], BF16, kind="Internal").ap()
    sc_v = nc.dram_tensor("sc_v", [8, 128, KC * 256], BF16, kind="Internal").ap()
    sc_b = nc.dram_tensor("sc_b", [8, 128, KC * 256], BF16, kind="Internal").ap()

    es = contextlib.ExitStack()
    with es:
        sbuf = lambda name, shp, dt: es.enter_context(nc.sbuf_tensor(name, shp, dt))
        psal = lambda name, shp, dt: es.enter_context(nc.psum_tensor(name, shp, dt))

        x_sb = sbuf("x_sb", [128, KC * TS], F32)
        xln = sbuf("xln", [128, KC * TS], BF16)
        pool = sbuf("pool", [128, 24576], BF16)
        h_sl = pool[:, 0:IT * TS]
        q_fm = pool[:, 0:8192]
        k_fm = pool[:, 8192:16384]
        u_tm = pool[:, 16384:24576]
        o_fm = sbuf("o_fm", [128, 8192], BF16)
        S_f32 = sbuf("S_f32", [128, NH * HD], F32)
        S_bf = sbuf("S_bf", [128, NH * HD], BF16)
        wpool = sbuf("wpool", [128, 20480], BF16)
        wg = [wpool[:, 0:2048], wpool[:, 2048:4096], wpool[:, 16384:18432]]
        wu = [wpool[:, 4096:6144], wpool[:, 6144:8192], wpool[:, 18432:20480]]
        wd = [wpool[:, 8192:8192 + INTER], wpool[:, 8192 + INTER:8192 + 2 * INTER]]
        qkw = [wpool[:, 0:2048], wpool[:, 2048:4096]]
        vbw = [wpool[:, 4096:12288], wpool[:, 12288:20480]]
        sg = sbuf("sg", [128, 2 * TS], BF16)
        sqb = sbuf("sqb", [128, 4 * TS], F32)
        lntmp = sbuf("lntmp", [128, TS], F32)
        ktm = sbuf("ktm", [128, 256], BF16)
        attm = sbuf("attm", [128, 256], BF16)
        stat = sbuf("stat", [1, 6 * TS], F32)
        m_sb = stat[0:1, 0:TS]
        ms_sb = stat[0:1, TS:2 * TS]
        m2_sb = stat[0:1, 2 * TS:3 * TS]
        var_sb = stat[0:1, 3 * TS:4 * TS]
        ln_sb = stat[0:1, 4 * TS:5 * TS]
        rstd_sb = stat[0:1, 5 * TS:6 * TS]
        lut_sb = sbuf("lut_sb", [1, 16], F32)
        lutb = sbuf("lutb", [128, 16], F32)
        ones_c = sbuf("ones_c", [1, 128], F32)
        onesk = sbuf("onesk", [128, 1], F32)
        maskc = sbuf("maskc", [128, 128], BF16)
        identb = sbuf("identb", [128, 128], BF16)
        rawi = sbuf("rawi", [128, 4096], I32)
        bfb = pool[:, 0:4096]
        accb = pool[:, 4096:8192]
        tmpb = pool[:, 8192:10240]
        wtb = pool[:, 10240:14336]
        cwt = pool[:, 14336:18432]
        rawf = x_sb[:, 0:4096]

        ps = [psal(f"ps{i}", [128, 512], F32) for i in range(6)]
        pst = [psal("pst0", [128, 1024], BF16), psal("pst1", [128, 1024], BF16)]

        csem = {e: es.enter_context(nc.semaphore("c_" + e)) for e in ENGS}
        NP = 1 if DEBUG_EDGES else 8
        dpool = {e: [es.enter_context(nc.semaphore(f"d_{e}{i}")) for i in range(NP)]
                 for e in DQUEUES}
        dpc = {e: [0] * NP for e in DQUEUES}
        dseq = {e: 0 for e in DQUEUES}
        cnt = {e: 0 for e in ENGS}
        q = {e: [] for e in ENGS}
        lastw = {e: {} for e in ENGS}

        def ins(e, f, inc=False):
            if DEBUG_EDGES:
                prev = cnt[e]
                h = csem[e]
                if prev:
                    q[e].append(lambda E, h=h, v=prev: E.wait_ge(h, v))
                cnt[e] += 1
                v = cnt[e]
                q[e].append(lambda E, f=f, h=h: f(E).then_inc(h, 1))
                return v
            if inc:
                cnt[e] += 1
                v = cnt[e]
                h = csem[e]
                q[e].append(lambda E, f=f, h=h: f(E).then_inc(h, 1))
                return v
            q[e].append(f)
            return None

        def dma(e, out_ap, in_ap):
            i = dseq[e] % NP
            dseq[e] += 1
            h = dpool[e][i]
            if dpc[e][i]:
                q[e].append(lambda E, h=h, v=dpc[e][i]: E.wait_ge(h, v))
            dpc[e][i] += 16
            v = dpc[e][i]
            q[e].append(lambda E, o=out_ap, i_=in_ap, h=h:
                        E.dma_start(out=o, in_=i_).then_inc(h, 16))
            return (e, i, v)

        def wt(e, kind, oe, v):
            if not v or v <= 0:
                return
            if kind == "c" and oe == e:
                return
            key = (kind, oe)
            if lastw[e].get(key, 0) >= v:
                return
            lastw[e][key] = v
            h = csem[oe]
            q[e].append(lambda E, h=h, v=v: E.wait_ge(h, v))

        def wts(e, v):
            if not v:
                return
            key = ("self",)
            if lastw[e].get(key, 0) >= v:
                return
            lastw[e][key] = v
            q[e].append(lambda E, h=csem[e], v=v: E.wait_ge(h, v))

        def wt_d(e, tokens):
            if tokens is None:
                return
            if isinstance(tokens, tuple):
                tokens = [tokens]
            for (oe, i, v) in tokens:
                key = ("d", oe, i)
                if lastw[e].get(key, 0) >= v:
                    continue
                lastw[e][key] = v
                h = dpool[oe][i]
                q[e].append(lambda E, h=h, v=v: E.wait_ge(h, v))

        def wt_m(e, m):
            if m[0] == "c":
                wt(e, "c", m[1], m[2])
            else:
                wt_d(e, m[1])

        pbank = {i: {"w": ("tensor", 0), "r": {}}
                 for i in list(range(6)) + ["t0", "t1"]}

        def mm_first(bank):
            for oe, v in pbank[bank]["r"].items():
                wt("tensor", "c", oe, v)
            pbank[bank]["r"] = {}

        def mm_last(bank, v):
            pbank[bank]["w"] = ("tensor", v)

        def prd(bank, e, f):
            we, wv = pbank[bank]["w"]
            wt(e, "c", we, wv)
            v = ins(e, f, inc=True)
            pbank[bank]["r"][e] = max(pbank[bank]["r"].get(e, 0), v)
            return v

        ins("gpsimd", lambda E: E.memset(ones_c[:, :], 1.0))
        ins("gpsimd", lambda E: E.memset(onesk[:, :], 1.0 / D))
        g1 = ins("gpsimd", lambda E: E.memset(identb[:, :], 0.0), inc=True)
        q["gpsimd"].append(lambda E, h=csem["gpsimd"], v=g1: E.wait_ge(h, v))
        ins("gpsimd", lambda E: E.affine_select(
            out=identb[:, :], in_=identb[:, :], compare_op=ALU.not_equal,
            fill=1.0, base=0, pattern=[[-1, 128]], channel_multiplier=1))
        g2 = ins("gpsimd", lambda E: E.memset(maskc[:, :], 1.0), inc=True)
        q["gpsimd"].append(lambda E, h=csem["gpsimd"], v=g2: E.wait_ge(h, v))
        ins("gpsimd", lambda E: E.affine_select(
            out=maskc[:, :], in_=maskc[:, :], compare_op=ALU.is_ge,
            fill=0.0, base=0, pattern=[[1, 128]], channel_multiplier=-1))
        ins("gpsimd", lambda E: E.memset(S_f32[:, :], 0.0))
        v_init = ins("gpsimd", lambda E: E.memset(S_bf[:, :], 0.0), inc=True)

        d_lut = dma("sync", lut_sb[:, :], lut_d)
        wt_d("tensor", d_lut)
        wt("tensor", "c", "gpsimd", v_init)
        mm_first(5)
        t_lut = ins("tensor", lambda E: E.matmul(
            ps[5][:, 0:16], ones_c[:, :], lut_sb[:, :], start=True, stop=True), inc=True)
        mm_last(5, t_lut)
        v_lut = prd(5, "vector", lambda E: E.tensor_copy(out=lutb[:, :], in_=ps[5][:, 0:16]))
        q["vector"].append(lambda E, h=csem["vector"], v=v_lut: E.wait_ge(h, v))
        wt("vector", "c", "gpsimd", v_init)

        mat = [(c, 2048) for c in range(16)] + [(16, 256)]
        NMAT = len(mat)

        cast_toks = {wn: [] for wn in ("q", "k", "v", "b", "o")}
        for wn, wdram in (("q", wq_d), ("k", wk_d), ("v", wv_d), ("b", wb_d),
                          ("o", wo_d)):
            for kc in range(KC):
                if wn in ("q", "k", "o"):
                    scd = {"q": sc_q, "k": sc_k, "o": sc_o}[wn]
                    tok = dma(
                        "gpsimd",
                        scd[:, :, kc * 128:(kc + 1) * 128].rearrange("h p c -> p h c"),
                        wdram[kc * 128:(kc + 1) * 128, :].rearrange(
                            "p (h c) -> p h c", c=128))
                else:
                    scd = sc_v if wn == "v" else sc_b
                    tok = dma(
                        "gpsimd",
                        scd[:, :, kc * 256:(kc + 1) * 256].rearrange("s p c -> p s c"),
                        wdram[kc * 128:(kc + 1) * 128, :].rearrange(
                            "p (s c) -> p s c", c=256))
                cast_toks[wn].append(tok)

        v_cast = {}
        t_tp = {}
        a_ev = {}
        g_wr = {}
        nbs = [w // 128 for (_, w) in mat]

        idx_ld = {}

        def emit_idx_load(k):
            c, w = mat[k]
            b = k % 2
            if k >= 2:
                wt("sync", "c", "vector", v_cast[k - 2])
            idx_ld[k] = dma("sync", rawi[:, b * 2048:b * 2048 + w],
                            myidx_d[:, c * 2048:c * 2048 + w])

        emit_idx_load(0)
        emit_idx_load(1)

        def emit_mat(k):
            c, w = mat[k]
            b = k % 2
            nb = nbs[k]
            ld = idx_ld[k]
            wt_d("vector", ld)
            v_cast[k] = ins("vector", lambda E, b=b, w=w: E.tensor_copy(
                out=bfb[:, b * 2048:b * 2048 + w],
                in_=rawi[:, b * 2048:b * 2048 + w]), inc=True)
            if k + 2 < NMAT:
                emit_idx_load(k + 2)
            if k >= 2:
                wt("vector", "c", "tensor", t_tp[(k - 2, nbs[k - 2] - 1)])
            for j16 in range(16):
                sc_ap = lutb[:, j16:j16 + 1]
                if j16 == 0:
                    ins("vector", lambda E, b=b, w=w, jj=j16, s=sc_ap: E.tensor_scalar(
                        out=accb[:, b * 2048:b * 2048 + w],
                        in0=bfb[:, b * 2048:b * 2048 + w],
                        scalar1=float(jj), scalar2=s,
                        op0=ALU.is_equal, op1=ALU.mult))
                else:
                    ins("vector", lambda E, b=b, w=w, jj=j16, s=sc_ap: E.tensor_scalar(
                        out=tmpb[:, 0:w],
                        in0=bfb[:, b * 2048:b * 2048 + w],
                        scalar1=float(jj), scalar2=s,
                        op0=ALU.is_equal, op1=ALU.mult))
                    ins("vector", lambda E, b=b, w=w: E.tensor_add(
                        accb[:, b * 2048:b * 2048 + w],
                        accb[:, b * 2048:b * 2048 + w],
                        tmpb[:, 0:w]))
            v_sel = ins("vector", lambda E: E.engine_nop(), inc=True)
            wt("tensor", "c", "vector", v_sel)
            if k >= 1:
                wt("tensor", "c", "scalar", a_ev[(k - 1, nbs[k - 1] - 1)])
            if k >= 2:
                wt_d("scalar", g_wr[k - 2])
            for j in range(nb):
                slot = j % 2
                if j >= 2:
                    wt("tensor", "c", "scalar", a_ev[(k, j - 2)])
                t_tp[(k, j)] = ins("tensor", lambda E, b=b, j=j, slot=slot: E.transpose(
                    pst[slot][:, 0:128],
                    accb[:, b * 2048 + j * 128:b * 2048 + (j + 1) * 128],
                    identb[:, :]), inc=True)
                wt("scalar", "c", "tensor", t_tp[(k, j)])
                a_ev[(k, j)] = ins("scalar", lambda E, b=b, j=j, slot=slot: E.copy(
                    out=wtb[:, b * 2048 + j * 128:b * 2048 + (j + 1) * 128],
                    in_=pst[slot][:, 0:128]), inc=True)
            wts("scalar", a_ev[(k, nb - 1)])
            L0 = c * 2048
            gg = min(L0 // 8192, 3)
            g_wr[k] = [dma("scalar", sc_part[gg][:, L0 - GOFFL[gg]:
                                                 L0 - GOFFL[gg] + w],
                           wtb[:, b * 2048:b * 2048 + w])]

        v_ccg = []
        for k in range(NMAT):
            emit_mat(k)
            c, _ = mat[k]
            if c in (3, 7, 11, 16):
                gg = len(v_ccg)
                wt_d("gpsimd", [t for kk in range(k + 1) for t in g_wr[kk]])
                v_ccg.append(ins("gpsimd", lambda E, gg=gg: E.collective_compute(
                    "AllGather", ALU.bypass, replica_groups=[list(range(8))],
                    ins=[sc_part[gg].opt()], outs=[sc_allg[gg].opt()]),
                    inc=True))
        for i in range(NP):
            if dpc["scalar"][i]:
                q["scalar"].append(lambda E, h=dpool["scalar"][i], v=dpc["scalar"][i]:
                                   E.wait_ge(h, v))
        cast_all_done = ins("scalar", lambda E: E.nop(), inc=True)

        PARTC = 33024
        GOFF = 0
        UOFF = IT * 2048
        DOFF = 2 * IT * 2048

        def load_image(dst_ap_fn, col0, w):
            toks = []
            off = 0
            while w > 0:
                g = min(col0 // 65536, 3)
                rem = col0 - GOFFG[g]
                r = rem // GW[g]
                lc = rem % GW[g]
                take = min(w, GW[g] - lc)
                toks.append(dma("sync", dst_ap_fn(off, take),
                                sc_allg[g][r, :, lc:lc + take]))
                col0 += take
                off += take
                w -= take
            return toks

        def wt_cc(e, col0, w):
            for g in range(4):
                if col0 < GOFFG[g] + 8 * GW[g] and col0 + w > GOFFG[g]:
                    wt(e, "c", "gpsimd", v_ccg[g])
        cast_all_done = ins("scalar", lambda E: E.nop(), inc=True)

        t_o = {}
        t_sup = {}
        a_oev = {}
        a_ktm = {}
        v_att = {}
        v_scast = {}
        v_dn = {}
        v_wo = {}
        t_gu = {}
        t_dn = {}
        ffn_i = [0]
        d_x = {}
        st_x = {}

        def ln_phase(marker):
            mm_first(4)
            vm = None
            for dc in range(KC):
                wt_m("tensor", marker(dc))
                last = dc == KC - 1
                vm = ins("tensor", lambda E, dc=dc, first=(dc == 0), last=last: E.matmul(
                    ps[4][0:1, 0:TS], onesk[:, :], x_sb[:, dc * TS:(dc + 1) * TS],
                    start=first, stop=last), inc=last)
            mm_last(4, vm)
            mm_first(5)
            a_sq = {}
            t_sqm = {}
            for dc in range(KC):
                wt_m("scalar", marker(dc))
                if dc >= 4:
                    wt("scalar", "c", "tensor", t_sqm[dc - 4])
                a_sq[dc] = ins("scalar", lambda E, dc=dc: E.activation(
                    out=sqb[:, (dc % 4) * TS:(dc % 4 + 1) * TS],
                    in_=x_sb[:, dc * TS:(dc + 1) * TS], func=AF.Square), inc=True)
                wt("tensor", "c", "scalar", a_sq[dc])
                last = dc == KC - 1
                t_sqm[dc] = ins("tensor", lambda E, dc=dc, first=(dc == 0), last=last: E.matmul(
                    ps[5][0:1, 0:TS], onesk[:, :],
                    sqb[:, (dc % 4) * TS:(dc % 4 + 1) * TS],
                    start=first, stop=last), inc=True)
            mm_last(5, t_sqm[KC - 1])
            prd(4, "vector", lambda E: E.tensor_copy(out=m_sb, in_=ps[4][0:1, 0:TS]))
            mm_first(4)
            t_bm = ins("tensor", lambda E: E.matmul(
                ps[4][:, 0:TS], ones_c[:, :], m_sb, start=True, stop=True), inc=True)
            mm_last(4, t_bm)
            prd(5, "vector", lambda E: E.tensor_copy(out=ms_sb, in_=ps[5][0:1, 0:TS]))
            ins("vector", lambda E: E.tensor_mul(m2_sb, m_sb, m_sb))
            ins("vector", lambda E: E.tensor_sub(var_sb, ms_sb, m2_sb))
            v_var = ins("vector", lambda E: E.tensor_scalar_add(var_sb, var_sb, EPS), inc=True)
            wt("scalar", "c", "vector", v_var)
            a_ln = ins("scalar", lambda E: E.activation(out=ln_sb, in_=var_sb, func=AF.Ln), inc=True)
            wt("vector", "c", "scalar", a_ln)
            v_l2 = ins("vector", lambda E: E.tensor_scalar_mul(ln_sb, ln_sb, -0.5), inc=True)
            wt("scalar", "c", "vector", v_l2)
            a_rstd = ins("scalar", lambda E: E.activation(
                out=rstd_sb, in_=ln_sb, func=AF.Exp), inc=True)
            wt("tensor", "c", "scalar", a_rstd)
            mm_first(5)
            t_br = ins("tensor", lambda E: E.matmul(
                ps[5][:, 0:TS], ones_c[:, :], rstd_sb, start=True, stop=True), inc=True)
            mm_last(5, t_br)
            wt("vector", "c", "tensor", t_br)
            v_app = []
            for dc in range(KC):
                wt_m("vector", marker(dc))
                ins("vector", lambda E, dc=dc: E.tensor_sub(
                    lntmp[:, :], x_sb[:, dc * TS:(dc + 1) * TS], ps[4][:, 0:TS]))
                v_app.append(ins("vector", lambda E, dc=dc: E.tensor_mul(
                    xln[:, dc * TS:(dc + 1) * TS], lntmp[:, :], ps[5][:, 0:TS]), inc=True))
            pbank[4]["r"]["vector"] = v_app[-1]
            pbank[5]["r"]["vector"] = v_app[-1]
            return v_app

        a_silu = {}
        v_h = {}

        def ffn_phase(v_xln):
            fi = ffn_i[0]
            ffn_i[0] += 1
            t_enter = cnt["tensor"]
            v_enter = cnt["vector"]
            d_g = {}
            d_u = {}

            def emit_gu_load(it):
                b = it % 3
                if it >= 3:
                    wt("sync", "c", "tensor", t_gu[(fi, it - 3)])
                else:
                    wt("sync", "c", "tensor", t_enter)
                wt_cc("sync", GOFF + it * 2048, 2048)
                wt_cc("sync", UOFF + it * 2048, 2048)
                d_g[it] = load_image(
                    lambda off, take, b=b: wg[b][:, off:off + take],
                    GOFF + it * 2048, 2048)
                d_u[it] = load_image(
                    lambda off, take, b=b: wu[b][:, off:off + take],
                    UOFF + it * 2048, 2048)

            d_d = {}

            def emit_d_load(dt):
                b = dt % 2
                if dt >= 2:
                    wt("sync", "c", "tensor", t_dn[(fi, dt - 2)])
                elif dt == 1:
                    wt("sync", "c", "tensor", t_gu[(fi, IT - 2)])
                else:
                    wt("sync", "c", "tensor", t_enter)
                wt_cc("sync", DOFF + dt * INTER, INTER)
                d_d[dt] = load_image(
                    lambda off, take, b=b: wd[b][:, off:off + take],
                    DOFF + dt * INTER, INTER)

            emit_gu_load(0)
            emit_gu_load(1)
            emit_gu_load(2)
            for it in range(IT):
                b = it % 3
                b2 = it % 2
                bg, bu = 2 * b2, 2 * b2 + 1
                wt_d("tensor", d_g[it] + d_u[it])
                mm_first(bg)
                tg = None
                for kc in range(KC):
                    wt("tensor", "c", "vector", v_xln[kc])
                    last = kc == KC - 1
                    tg = ins("tensor", lambda E, kc=kc, b=b, bg=bg,
                             first=(kc == 0), last=last: E.matmul(
                        ps[bg][:, 0:TS], wg[b][:, kc * 128:(kc + 1) * 128],
                        xln[:, kc * TS:(kc + 1) * TS],
                        start=first, stop=last), inc=last)
                mm_last(bg, tg)
                mm_first(bu)
                tu = None
                for kc in range(KC):
                    last = kc == KC - 1
                    tu = ins("tensor", lambda E, kc=kc, b=b, bu=bu,
                             first=(kc == 0), last=last: E.matmul(
                        ps[bu][:, 0:TS], wu[b][:, kc * 128:(kc + 1) * 128],
                        xln[:, kc * TS:(kc + 1) * TS],
                        start=first, stop=last), inc=last)
                mm_last(bu, tu)
                t_gu[(fi, it)] = tu
                if it >= 2:
                    wt("scalar", "c", "vector", v_h[(fi, it - 2)])
                else:
                    wt("scalar", "c", "vector", v_enter)
                a_silu[(fi, it)] = prd(bg, "scalar", lambda E, b2=b2, bg=bg: E.activation(
                    out=sg[:, b2 * TS:(b2 + 1) * TS], in_=ps[bg][:, 0:TS], func=AF.Silu))
                wt("vector", "c", "scalar", a_silu[(fi, it)])
                if fi == 0:
                    wt("vector", "c", "scalar", cast_all_done)
                v_h[(fi, it)] = prd(bu, "vector", lambda E, it=it, b2=b2, bu=bu: E.tensor_mul(
                    h_sl[:, it * TS:(it + 1) * TS], sg[:, b2 * TS:(b2 + 1) * TS],
                    ps[bu][:, 0:TS]))
                if it + 3 < IT:
                    emit_gu_load(it + 3)
                if it == 6:
                    emit_d_load(0)
            emit_d_load(1)
            for dt in range(KC):
                b = dt % 2
                bank = dt % 2
                wt_d("tensor", d_d[dt])
                wt("tensor", "c", "vector", v_h[(fi, IT - 1)])
                mm_first(bank)
                td = None
                for ic in range(IT):
                    last = ic == IT - 1
                    td = ins("tensor", lambda E, ic=ic, b=b, bank=bank,
                             first=(ic == 0), last=last: E.matmul(
                        ps[bank][:, 0:TS], wd[b][:, ic * 128:(ic + 1) * 128],
                        h_sl[:, ic * TS:(ic + 1) * TS],
                        start=first, stop=last), inc=last)
                mm_last(bank, td)
                t_dn[(fi, dt)] = td
                v_dn[(fi, dt)] = prd(bank, "vector", lambda E, dt=dt, bank=bank:
                                     E.scalar_tensor_tensor(
                    out=x_sb[:, dt * TS:(dt + 1) * TS],
                    in0=ps[bank][:, 0:TS], scalar=0.5,
                    in1=x_sb[:, dt * TS:(dt + 1) * TS],
                    op0=ALU.mult, op1=ALU.add))
                if dt + 2 < KC:
                    emit_d_load(dt + 2)
            return fi

        def attn_phase(s, v_xln):
            t_enter = cnt["tensor"]
            v_enter = cnt["vector"]
            t_qk = {}
            a_qk = {}
            d_qk = {}
            qk_srcs = [(sc_q, q_fm, hh) for hh in range(NH)] + \
                      [(sc_k, k_fm, hh) for hh in range(NH)]

            def emit_qk_load(i):
                scd, dstv, hh = qk_srcs[i]
                b = i % 2
                if i >= 2:
                    wt("sync", "c", "tensor", t_qk[i - 2])
                else:
                    wt("sync", "c", "tensor", t_enter)
                wt_d("sync", cast_toks["q"] + cast_toks["k"])
                d_qk[i] = dma("sync", qkw[b][:, :], scd[hh, :, :])

            v_u_last = 0
            t_vb_last = {}
            a_sig_first = [True]
            d_vb = {}
            vb4 = [vbw[0][:, 0:4096], vbw[0][:, 4096:8192],
                   vbw[1][:, 0:4096], vbw[1][:, 4096:8192]]

            def emit_vb_load(sl8):
                if sl8 >= 2:
                    wt("sync", "c", "tensor", t_vb_last[sl8 - 2])
                    wt("gpsimd", "c", "tensor", t_vb_last[sl8 - 2])
                else:
                    wt("sync", "c", "tensor", t_enter)
                    wt("gpsimd", "c", "tensor", t_enter)
                    wt_d("sync", cast_toks["v"])
                    wt_d("gpsimd", cast_toks["b"])
                bq = (sl8 % 2) * 2
                d_vb[sl8] = [dma("sync", vb4[bq], sc_v[sl8]),
                             dma("gpsimd", vb4[bq + 1], sc_b[sl8])]

            emit_qk_load(0)
            emit_qk_load(1)
            emit_vb_load(0)
            emit_vb_load(1)
            for i in range(2 * NH):
                scd, dstv, hh = qk_srcs[i]
                b = i % 2
                bank = i % 2
                wt_d("tensor", d_qk[i])
                mm_first(bank)
                tl = None
                for kc in range(KC):
                    wt("tensor", "c", "vector", v_xln[kc])
                    last = kc == KC - 1
                    tl = ins("tensor", lambda E, kc=kc, b=b, bank=bank,
                             first=(kc == 0), last=last: E.matmul(
                        ps[bank][:, 0:TS], qkw[b][:, kc * 128:(kc + 1) * 128],
                        xln[:, kc * TS:(kc + 1) * TS],
                        start=first, stop=last), inc=last)
                mm_last(bank, tl)
                t_qk[i] = tl
                a_qk[i] = prd(bank, "scalar", lambda E, hh=hh, bank=bank, dstv=dstv:
                              E.copy(out=dstv[:, hh * TS:(hh + 1) * TS],
                                     in_=ps[bank][:, 0:TS]))
                if i + 2 < 2 * NH:
                    emit_qk_load(i + 2)
            for sl8 in range(8):
                bq = (sl8 % 2) * 2
                for tt in range(NCH):
                    bv, bb = 2 * (tt % 2), 2 * (tt % 2) + 1
                    wt_d("tensor", d_vb[sl8])
                    mm_first(bv)
                    tv = None
                    for kc in range(KC):
                        last = kc == KC - 1
                        tv = ins("tensor", lambda E, kc=kc, tt=tt, bv=bv, bq=bq,
                                 first=(kc == 0), last=last: E.matmul(
                            ps[bv][:, 0:256],
                            xln[:, kc * TS + tt * 128:kc * TS + (tt + 1) * 128],
                            vb4[bq][:, kc * 256:(kc + 1) * 256],
                            start=first, stop=last), inc=last)
                    mm_last(bv, tv)
                    mm_first(bb)
                    tb = None
                    for kc in range(KC):
                        last = kc == KC - 1
                        tb = ins("tensor", lambda E, kc=kc, tt=tt, bb=bb, bq=bq,
                                 first=(kc == 0), last=last: E.matmul(
                            ps[bb][:, 0:256],
                            xln[:, kc * TS + tt * 128:kc * TS + (tt + 1) * 128],
                            vb4[bq + 1][:, kc * 256:(kc + 1) * 256],
                            start=first, stop=last), inc=last)
                    mm_last(bb, tb)
                    t_vb_last[sl8] = tb
                    if a_sig_first[0]:
                        wt("scalar", "c", "vector", v_enter)
                        a_sig_first[0] = False
                    else:
                        wt("scalar", "c", "vector", v_u_last)
                    a_sig = prd(bb, "scalar", lambda E, bb=bb: E.activation(
                        out=sg[:, 0:256], in_=ps[bb][:, 0:256], func=AF.Sigmoid))
                    wt("vector", "c", "scalar", a_sig)
                    v_u_last = prd(bv, "vector", lambda E, sl8=sl8, tt=tt, bv=bv:
                                   E.tensor_mul(
                        u_tm[:, tt * D + sl8 * 256:tt * D + (sl8 + 1) * 256],
                        sg[:, 0:256], ps[bv][:, 0:256]))
                if sl8 + 2 < 8:
                    emit_vb_load(sl8 + 2)
            for bk in (0, 1, 2, 3, 4, 5, "t0", "t1"):
                for oe, v in pbank[bk]["r"].items():
                    wt("tensor", "c", oe, v)
                pbank[bk]["r"] = {}

            def head_tail(gc, c, hh):
                bo = 2 if hh % 2 == 0 else 5
                wt("tensor", "c", "vector", v_att[(gc, hh)])
                if (gc, hh - 2) in a_oev:
                    wt("tensor", "c", "scalar", a_oev[(gc, hh - 2)])
                elif (gc - 1, NH - 2 + (hh % 2)) in a_oev:
                    wt("tensor", "c", "scalar", a_oev[(gc - 1, NH - 2 + (hh % 2))])
                if (gc - 1, hh) in v_scast:
                    wt("tensor", "c", "vector", v_scast[(gc - 1, hh)])
                ins("tensor", lambda E, c=c, hh=hh, bo=bo: E.matmul(
                    ps[bo][:, 0:128],
                    u_tm[:, c * D + hh * HD:c * D + (hh + 1) * HD],
                    attm[:, (hh % 2) * 128:(hh % 2 + 1) * 128],
                    start=True, stop=False))
                t_o[(gc, hh)] = ins("tensor", lambda E, c=c, hh=hh, bo=bo: E.matmul(
                    ps[bo][:, 0:128],
                    S_bf[:, hh * HD:(hh + 1) * HD],
                    q_fm[:, hh * TS + c * 128:hh * TS + (c + 1) * 128],
                    start=False, stop=True), inc=True)
                bs = 3 + (hh % 2)
                wt("tensor", "c", "scalar", a_ktm[(gc, hh)])
                if (gc, hh - 2) in v_scast:
                    wt("tensor", "c", "vector", v_scast[(gc, hh - 2)])
                elif c > 0:
                    wt("tensor", "c", "vector", v_scast[(gc - 1, NH - 2 + (hh % 2))])
                t_sup[(gc, hh)] = ins("tensor", lambda E, c=c, hh=hh, bs=bs: E.matmul(
                    ps[bs][:, 0:128],
                    ktm[:, (hh % 2) * 128:(hh % 2 + 1) * 128],
                    u_tm[:, c * D + hh * HD:c * D + (hh + 1) * HD],
                    start=True, stop=True), inc=True)
                wt("scalar", "c", "tensor", t_o[(gc, hh)])
                a_oev[(gc, hh)] = ins("scalar", lambda E, c=c, hh=hh, bo=bo: E.copy(
                    out=o_fm[:, hh * TS + c * 128:hh * TS + (c + 1) * 128],
                    in_=ps[bo][:, 0:128]), inc=True)
                wt("vector", "c", "tensor", t_sup[(gc, hh)])
                ins("vector", lambda E, hh=hh, bs=bs: E.tensor_add(
                    S_f32[:, hh * HD:(hh + 1) * HD],
                    S_f32[:, hh * HD:(hh + 1) * HD],
                    ps[bs][:, 0:128]))
                v_scast[(gc, hh)] = ins("vector", lambda E, hh=hh: E.tensor_copy(
                    out=S_bf[:, hh * HD:(hh + 1) * HD],
                    in_=S_f32[:, hh * HD:(hh + 1) * HD]), inc=True)

            for c in range(NCH):
                gc = s * NCH + c
                for hh in range(NH):
                    slot = hh % 2
                    ba = hh % 2
                    if (gc, hh - 2) in a_ktm:
                        wt("tensor", "c", "scalar", a_ktm[(gc, hh - 2)])
                    elif (gc - 1, NH - 2 + slot) in a_ktm:
                        wt("tensor", "c", "scalar", a_ktm[(gc - 1, NH - 2 + slot)])
                    wt("tensor", "c", "scalar", a_qk[NH + hh])
                    wt("tensor", "c", "vector", v_u_last)
                    t_ktp = ins("tensor", lambda E, c=c, hh=hh, slot=slot: E.transpose(
                        pst[slot][:, 0:128],
                        k_fm[:, hh * TS + c * 128:hh * TS + (c + 1) * 128],
                        identb[:, :]), inc=True)
                    if (gc, hh - 2) in v_att:
                        wt("tensor", "c", "vector", v_att[(gc, hh - 2)])
                    elif c > 0:
                        wt("tensor", "c", "vector", v_att[(gc - 1, NH - 2 + ba)])
                    wt("tensor", "c", "scalar", a_qk[hh])
                    t_att = ins("tensor", lambda E, c=c, hh=hh, ba=ba: E.matmul(
                        ps[ba][:, 0:128],
                        k_fm[:, hh * TS + c * 128:hh * TS + (c + 1) * 128],
                        q_fm[:, hh * TS + c * 128:hh * TS + (c + 1) * 128],
                        start=True, stop=True), inc=True)
                    wt("scalar", "c", "tensor", t_ktp)
                    if (gc, hh - 2) in t_sup:
                        wt("scalar", "c", "tensor", t_sup[(gc, hh - 2)])
                    elif (gc - 1, NH - 2 + slot) in t_sup:
                        wt("scalar", "c", "tensor", t_sup[(gc - 1, NH - 2 + slot)])
                    a_ktm[(gc, hh)] = ins("scalar", lambda E, slot=slot: E.copy(
                        out=ktm[:, slot * 128:(slot + 1) * 128],
                        in_=pst[slot][:, 0:128]), inc=True)
                    wt("vector", "c", "tensor", t_att)
                    if (gc, hh - 2) in t_o:
                        wt("vector", "c", "tensor", t_o[(gc, hh - 2)])
                    elif (gc - 1, NH - 2 + ba) in t_o:
                        wt("vector", "c", "tensor", t_o[(gc - 1, NH - 2 + ba)])
                    v_att[(gc, hh)] = ins("vector", lambda E, ba=ba, slot=slot: E.tensor_mul(
                        attm[:, slot * 128:(slot + 1) * 128],
                        ps[ba][:, 0:128], maskc[:, :]), inc=True)
                    if hh > 0:
                        head_tail(gc, c, hh - 1)
                head_tail(gc, c, NH - 1)
            for bk in (0, 1, 2, 3, 4, 5):
                pbank[bk]["w"] = ("tensor", cnt["tensor"])
            pbank[0]["r"]["vector"] = cnt["vector"]
            pbank[1]["r"]["vector"] = cnt["vector"]
            pbank[2]["r"]["scalar"] = cnt["scalar"]
            pbank[5]["r"]["scalar"] = cnt["scalar"]
            pbank[3]["r"]["vector"] = cnt["vector"]
            pbank[4]["r"]["vector"] = cnt["vector"]
            pbank["t0"]["r"]["scalar"] = cnt["scalar"]
            pbank["t1"]["r"]["scalar"] = cnt["scalar"]

            gcl = s * NCH + NCH - 1
            t_wo = {}
            d_wo = {}

            def emit_wo_load(dt):
                b = dt % 2
                if dt >= 2:
                    wt("sync", "c", "tensor", t_wo[dt - 2])
                else:
                    wt("sync", "c", "tensor", t_qk[2 * NH - 2 + b])
                    wt_d("sync", cast_toks["o"])
                d_wo[dt] = dma("sync", qkw[b][:, :], sc_o[dt, :, :])

            emit_wo_load(0)
            emit_wo_load(1)
            for dt in range(KC):
                b = dt % 2
                bank = dt % 2
                wt_d("tensor", d_wo[dt])
                wt("tensor", "c", "scalar", a_oev[(gcl, NH - 1)])
                mm_first(bank)
                tw = None
                for kc in range(KC):
                    last = kc == KC - 1
                    tw = ins("tensor", lambda E, kc=kc, b=b, bank=bank,
                             first=(kc == 0), last=last: E.matmul(
                        ps[bank][:, 0:TS], qkw[b][:, kc * 128:(kc + 1) * 128],
                        o_fm[:, kc * TS:(kc + 1) * TS],
                        start=first, stop=last), inc=last)
                mm_last(bank, tw)
                t_wo[dt] = tw
                v_wo[(s, dt)] = prd(bank, "vector", lambda E, dt=dt, bank=bank:
                                    E.tensor_add(
                    x_sb[:, dt * TS:(dt + 1) * TS],
                    x_sb[:, dt * TS:(dt + 1) * TS],
                    ps[bank][:, 0:TS]))
                if dt + 2 < KC:
                    emit_wo_load(dt + 2)

        for s in range(NSLAB):
            for dc in range(KC):
                if s > 0:
                    wt_d("sync", st_x[(s - 1, dc)])
                d_x[(s, dc)] = dma("sync", x_sb[:, dc * TS:(dc + 1) * TS],
                                   x_in[dc * 128:(dc + 1) * 128, s * TS:(s + 1) * TS])

            v_xln1 = ln_phase(lambda dc, s=s: ("d", d_x[(s, dc)]))
            fi1 = ffn_phase(v_xln1)
            v_xln2 = ln_phase(lambda dc, fi=fi1: ("c", "vector", v_dn[(fi, dc)]))
            attn_phase(s, v_xln2)
            v_xln3 = ln_phase(lambda dc, s=s: ("c", "vector", v_wo[(s, dc)]))
            fi2 = ffn_phase(v_xln3)

            for dc in range(KC):
                wt("gpsimd", "c", "vector", v_dn[(fi2, dc)])
                st_x[(s, dc)] = dma("gpsimd",
                                    y_out[dc * 128:(dc + 1) * 128, s * TS:(s + 1) * TS],
                                    x_sb[:, dc * TS:(dc + 1) * TS])

        for e in DQUEUES:
            for i in range(NP):
                if dpc[e][i]:
                    q[e].append(lambda E, h=dpool[e][i], v=dpc[e][i]: E.wait_ge(h, v))

        block = es.enter_context(nc.Block())

        @block.sync
        def _(E):
            for f in q["sync"]:
                f(E)

        @block.gpsimd
        def _(E):
            for f in q["gpsimd"]:
                f(E)

        @block.vector
        def _(E):
            for f in q["vector"]:
                f(E)

        @block.scalar
        def _(E):
            for f in q["scalar"]:
                f(E)

        @block.tensor
        def _(E):
            for f in q["tensor"]:
                f(E)

    return nc


_CACHE = {}


def kernel(L0_attn_hidden_states, L0_attn_Wq, L0_attn_Wk, L0_attn_Wv,
           L0_attn_Wb, L0_attn_Wo, L0_ffn_lut, L0_ffn_gate_idx,
           L0_ffn_up_idx, L0_ffn_down_idx):
    x = np.asarray(L0_attn_hidden_states, dtype=np.float32)
    gidx = np.asarray(L0_ffn_gate_idx, dtype=np.int32)
    uidx = np.asarray(L0_ffn_up_idx, dtype=np.int32)
    didx = np.asarray(L0_ffn_down_idx, dtype=np.int32)
    gb = gidx.reshape(IT, 128, KC, 128).transpose(0, 2, 1, 3).reshape(-1, 128, 128)
    ub = uidx.reshape(IT, 128, KC, 128).transpose(0, 2, 1, 3).reshape(-1, 128, 128)
    db = didx.reshape(KC, 128, IT, 128).transpose(0, 2, 1, 3).reshape(-1, 128, 128)
    blocks = np.concatenate([gb, ub, db], axis=0)
    base = {
        "wq": np.ascontiguousarray(np.asarray(L0_attn_Wq, dtype=np.float32)),
        "wk": np.ascontiguousarray(np.asarray(L0_attn_Wk, dtype=np.float32)),
        "wv": np.ascontiguousarray(np.asarray(L0_attn_Wv, dtype=np.float32)),
        "wb": np.ascontiguousarray(np.asarray(L0_attn_Wb, dtype=np.float32)),
        "wo": np.ascontiguousarray(np.asarray(L0_attn_Wo, dtype=np.float32)),
        "lut": np.ascontiguousarray(np.asarray(L0_ffn_lut, dtype=np.float32).reshape(1, 16)),
    }
    if "nc" not in _CACHE:
        _CACHE["nc"] = build()
    nc = _CACHE["nc"]
    in_maps = []
    for c in range(8):
        m = dict(base)
        m["x_in"] = np.ascontiguousarray(x[c].T)
        own = np.concatenate([blocks[512 * 0 + c * 64:512 * 0 + (c + 1) * 64],
                              blocks[512 * 1 + c * 64:512 * 1 + (c + 1) * 64],
                              blocks[512 * 2 + c * 64:512 * 2 + (c + 1) * 64],
                              blocks[1536 + c * 66:1536 + (c + 1) * 66]], axis=0)
        m["myidx"] = np.ascontiguousarray(
            own.transpose(1, 0, 2).reshape(128, 33024))
        in_maps.append(m)
    res = run_bass_kernel_spmd(nc, in_maps, core_ids=list(range(8)))
    out = np.stack([np.asarray(res.results[c]["y"]).T for c in range(8)], axis=0)
    return np.ascontiguousarray(out.astype(np.float32))


# version 27
# speedup vs baseline: 1.1127x; 1.0039x over previous
import numpy as np
import contextlib
import concourse.bass as bass
import concourse.mybir as mybir
from concourse.bass_utils import run_bass_kernel_spmd

F32 = mybir.dt.float32
BF16 = mybir.dt.bfloat16
I32 = mybir.dt.int32
AF = mybir.ActivationFunctionType
ALU = mybir.AluOpType

D = 2048
SEQ = 2048
NH = 16
HD = 128
INTER = 5504
KC = D // 128
IT = INTER // 128
TS = 512
NSLAB = SEQ // TS
CH = 128
NCH = TS // CH
EPS = 1e-5

ENGS = ("sync", "gpsimd", "vector", "scalar", "tensor")
DQUEUES = ("sync", "scalar", "gpsimd")

DEBUG_EDGES = False


def build():
    nc = bass.Bass("TRN2", target_bir_lowering=False,
                   detect_race_conditions=DEBUG_EDGES)

    x_in = nc.dram_tensor("x_in", [D, SEQ], F32, kind="ExternalInput").ap()
    wq_d = nc.dram_tensor("wq", [D, D], F32, kind="ExternalInput").ap()
    wk_d = nc.dram_tensor("wk", [D, D], F32, kind="ExternalInput").ap()
    wv_d = nc.dram_tensor("wv", [D, D], F32, kind="ExternalInput").ap()
    wb_d = nc.dram_tensor("wb", [D, D], F32, kind="ExternalInput").ap()
    wo_d = nc.dram_tensor("wo", [D, D], F32, kind="ExternalInput").ap()
    lut_d = nc.dram_tensor("lut", [1, 16], F32, kind="ExternalInput").ap()
    y_out = nc.dram_tensor("y", [D, SEQ], F32, kind="ExternalOutput").ap()

    myidx_d = nc.dram_tensor("myidx", [128, 33024], I32, kind="ExternalInput").ap()
    GSZ = (64, 64, 64, 66)
    GW = [s * 128 for s in GSZ]
    GOFFL = [0, 8192, 16384, 24576]
    GOFFG = [0, 65536, 131072, 196608]
    sc_part = [nc.dram_tensor(f"sc_part{g}", [128, GW[g]], BF16,
                              kind="Internal").ap() for g in range(4)]
    sc_allg = [nc.dram_tensor(f"sc_all{g}", [8, 128, GW[g]], BF16,
                              kind="Internal", addr_space="Shared").ap()
               for g in range(4)]
    sc_q = nc.dram_tensor("sc_q", [NH, 128, D], BF16, kind="Internal").ap()
    sc_k = nc.dram_tensor("sc_k", [NH, 128, D], BF16, kind="Internal").ap()
    sc_o = nc.dram_tensor("sc_o", [KC, 128, D], BF16, kind="Internal").ap()
    sc_v = nc.dram_tensor("sc_v", [8, 128, KC * 256], BF16, kind="Internal").ap()
    sc_b = nc.dram_tensor("sc_b", [8, 128, KC * 256], BF16, kind="Internal").ap()

    es = contextlib.ExitStack()
    with es:
        sbuf = lambda name, shp, dt: es.enter_context(nc.sbuf_tensor(name, shp, dt))
        psal = lambda name, shp, dt: es.enter_context(nc.psum_tensor(name, shp, dt))

        x_sb = sbuf("x_sb", [128, KC * TS], F32)
        xln = sbuf("xln", [128, KC * TS], BF16)
        pool = sbuf("pool", [128, 24576], BF16)
        h_sl = pool[:, 0:IT * TS]
        q_fm = pool[:, 0:8192]
        k_fm = pool[:, 8192:16384]
        u_tm = pool[:, 16384:24576]
        o_fm = sbuf("o_fm", [128, 8192], BF16)
        S_f32 = sbuf("S_f32", [128, NH * HD], F32)
        S_bf = sbuf("S_bf", [128, NH * HD], BF16)
        wpool = sbuf("wpool", [128, 20480], BF16)
        wg = [wpool[:, 0:2048], wpool[:, 2048:4096], wpool[:, 16384:18432]]
        wu = [wpool[:, 4096:6144], wpool[:, 6144:8192], wpool[:, 18432:20480]]
        wd = [wpool[:, 8192:8192 + INTER], wpool[:, 8192 + INTER:8192 + 2 * INTER]]
        qkw = [wpool[:, 0:2048], wpool[:, 2048:4096]]
        vbw = [wpool[:, 4096:12288], wpool[:, 12288:20480]]
        sg = sbuf("sg", [128, 2 * TS], BF16)
        sqb = sbuf("sqb", [128, 4 * TS], F32)
        lntmp = sbuf("lntmp", [128, TS], F32)
        ktm = sbuf("ktm", [128, 256], BF16)
        attm = sbuf("attm", [128, 256], BF16)
        stat = sbuf("stat", [1, 6 * TS], F32)
        m_sb = stat[0:1, 0:TS]
        ms_sb = stat[0:1, TS:2 * TS]
        m2_sb = stat[0:1, 2 * TS:3 * TS]
        var_sb = stat[0:1, 3 * TS:4 * TS]
        ln_sb = stat[0:1, 4 * TS:5 * TS]
        rstd_sb = stat[0:1, 5 * TS:6 * TS]
        lut_sb = sbuf("lut_sb", [1, 16], F32)
        lutb = sbuf("lutb", [128, 16], F32)
        ones_c = sbuf("ones_c", [1, 128], F32)
        onesk = sbuf("onesk", [128, 1], F32)
        maskc = sbuf("maskc", [128, 128], BF16)
        identb = sbuf("identb", [128, 128], BF16)
        rawi = sbuf("rawi", [128, 4096], I32)
        bfb = pool[:, 0:4096]
        accb = pool[:, 4096:8192]
        tmpb = pool[:, 8192:10240]
        wtb = pool[:, 10240:14336]
        cwt = pool[:, 14336:18432]
        rawf = x_sb[:, 0:4096]

        ps = [psal(f"ps{i}", [128, 512], F32) for i in range(6)]
        pst = [psal("pst0", [128, 1024], BF16), psal("pst1", [128, 1024], BF16)]

        csem = {e: es.enter_context(nc.semaphore("c_" + e)) for e in ENGS}
        NP = 1 if DEBUG_EDGES else 8
        dpool = {e: [es.enter_context(nc.semaphore(f"d_{e}{i}")) for i in range(NP)]
                 for e in DQUEUES}
        dpc = {e: [0] * NP for e in DQUEUES}
        dseq = {e: 0 for e in DQUEUES}
        cnt = {e: 0 for e in ENGS}
        q = {e: [] for e in ENGS}
        lastw = {e: {} for e in ENGS}

        def ins(e, f, inc=False):
            if DEBUG_EDGES:
                prev = cnt[e]
                h = csem[e]
                if prev:
                    q[e].append(lambda E, h=h, v=prev: E.wait_ge(h, v))
                cnt[e] += 1
                v = cnt[e]
                q[e].append(lambda E, f=f, h=h: f(E).then_inc(h, 1))
                return v
            if inc:
                cnt[e] += 1
                v = cnt[e]
                h = csem[e]
                q[e].append(lambda E, f=f, h=h: f(E).then_inc(h, 1))
                return v
            q[e].append(f)
            return None

        def dma(e, out_ap, in_ap):
            i = dseq[e] % NP
            dseq[e] += 1
            h = dpool[e][i]
            if dpc[e][i]:
                q[e].append(lambda E, h=h, v=dpc[e][i]: E.wait_ge(h, v))
            dpc[e][i] += 16
            v = dpc[e][i]
            q[e].append(lambda E, o=out_ap, i_=in_ap, h=h:
                        E.dma_start(out=o, in_=i_).then_inc(h, 16))
            return (e, i, v)

        def wt(e, kind, oe, v):
            if not v or v <= 0:
                return
            if kind == "c" and oe == e:
                return
            key = (kind, oe)
            if lastw[e].get(key, 0) >= v:
                return
            lastw[e][key] = v
            h = csem[oe]
            q[e].append(lambda E, h=h, v=v: E.wait_ge(h, v))

        def wts(e, v):
            if not v:
                return
            key = ("self",)
            if lastw[e].get(key, 0) >= v:
                return
            lastw[e][key] = v
            q[e].append(lambda E, h=csem[e], v=v: E.wait_ge(h, v))

        def wt_d(e, tokens):
            if tokens is None:
                return
            if isinstance(tokens, tuple):
                tokens = [tokens]
            for (oe, i, v) in tokens:
                key = ("d", oe, i)
                if lastw[e].get(key, 0) >= v:
                    continue
                lastw[e][key] = v
                h = dpool[oe][i]
                q[e].append(lambda E, h=h, v=v: E.wait_ge(h, v))

        def wt_m(e, m):
            if m[0] == "c":
                wt(e, "c", m[1], m[2])
            else:
                wt_d(e, m[1])

        pbank = {i: {"w": ("tensor", 0), "r": {}}
                 for i in list(range(6)) + ["t0", "t1"]}

        def mm_first(bank):
            for oe, v in pbank[bank]["r"].items():
                wt("tensor", "c", oe, v)
            pbank[bank]["r"] = {}

        def mm_last(bank, v):
            pbank[bank]["w"] = ("tensor", v)

        def prd(bank, e, f):
            we, wv = pbank[bank]["w"]
            wt(e, "c", we, wv)
            v = ins(e, f, inc=True)
            pbank[bank]["r"][e] = max(pbank[bank]["r"].get(e, 0), v)
            return v

        ins("gpsimd", lambda E: E.memset(ones_c[:, :], 1.0))
        ins("gpsimd", lambda E: E.memset(onesk[:, :], 1.0 / D))
        g1 = ins("gpsimd", lambda E: E.memset(identb[:, :], 0.0), inc=True)
        q["gpsimd"].append(lambda E, h=csem["gpsimd"], v=g1: E.wait_ge(h, v))
        ins("gpsimd", lambda E: E.affine_select(
            out=identb[:, :], in_=identb[:, :], compare_op=ALU.not_equal,
            fill=1.0, base=0, pattern=[[-1, 128]], channel_multiplier=1))
        g2 = ins("gpsimd", lambda E: E.memset(maskc[:, :], 1.0), inc=True)
        q["gpsimd"].append(lambda E, h=csem["gpsimd"], v=g2: E.wait_ge(h, v))
        ins("gpsimd", lambda E: E.affine_select(
            out=maskc[:, :], in_=maskc[:, :], compare_op=ALU.is_ge,
            fill=0.0, base=0, pattern=[[1, 128]], channel_multiplier=-1))
        ins("gpsimd", lambda E: E.memset(S_f32[:, :], 0.0))
        v_init = ins("gpsimd", lambda E: E.memset(S_bf[:, :], 0.0), inc=True)

        d_lut = dma("sync", lut_sb[:, :], lut_d)
        wt_d("tensor", d_lut)
        wt("tensor", "c", "gpsimd", v_init)
        mm_first(5)
        t_lut = ins("tensor", lambda E: E.matmul(
            ps[5][:, 0:16], ones_c[:, :], lut_sb[:, :], start=True, stop=True), inc=True)
        mm_last(5, t_lut)
        v_lut = prd(5, "vector", lambda E: E.tensor_copy(out=lutb[:, :], in_=ps[5][:, 0:16]))
        q["vector"].append(lambda E, h=csem["vector"], v=v_lut: E.wait_ge(h, v))
        wt("vector", "c", "gpsimd", v_init)

        mat = [(c, 2048) for c in range(16)] + [(16, 256)]
        NMAT = len(mat)

        cast_toks = {wn: [] for wn in ("q", "k", "v", "b", "o")}
        for wn, wdram in (("q", wq_d), ("k", wk_d), ("v", wv_d), ("b", wb_d),
                          ("o", wo_d)):
            for kc in range(KC):
                if wn in ("q", "k", "o"):
                    scd = {"q": sc_q, "k": sc_k, "o": sc_o}[wn]
                    tok = dma(
                        "gpsimd",
                        scd[:, :, kc * 128:(kc + 1) * 128].rearrange("h p c -> p h c"),
                        wdram[kc * 128:(kc + 1) * 128, :].rearrange(
                            "p (h c) -> p h c", c=128))
                else:
                    scd = sc_v if wn == "v" else sc_b
                    tok = dma(
                        "gpsimd",
                        scd[:, :, kc * 256:(kc + 1) * 256].rearrange("s p c -> p s c"),
                        wdram[kc * 128:(kc + 1) * 128, :].rearrange(
                            "p (s c) -> p s c", c=256))
                cast_toks[wn].append(tok)

        v_cast = {}
        t_tp = {}
        a_ev = {}
        g_wr = {}
        nbs = [w // 128 for (_, w) in mat]

        idx_ld = {}

        def emit_idx_load(k):
            c, w = mat[k]
            b = k % 2
            if k >= 2:
                wt("sync", "c", "vector", v_cast[k - 2])
            idx_ld[k] = dma("sync", rawi[:, b * 2048:b * 2048 + w],
                            myidx_d[:, c * 2048:c * 2048 + w])

        emit_idx_load(0)
        emit_idx_load(1)

        def emit_mat(k):
            c, w = mat[k]
            b = k % 2
            nb = nbs[k]
            ld = idx_ld[k]
            wt_d("vector", ld)
            v_cast[k] = ins("vector", lambda E, b=b, w=w: E.tensor_copy(
                out=bfb[:, b * 2048:b * 2048 + w],
                in_=rawi[:, b * 2048:b * 2048 + w]), inc=True)
            if k + 2 < NMAT:
                emit_idx_load(k + 2)
            if k >= 2:
                wt("vector", "c", "tensor", t_tp[(k - 2, nbs[k - 2] - 1)])
            for j16 in range(16):
                sc_ap = lutb[:, j16:j16 + 1]
                if j16 == 0:
                    ins("vector", lambda E, b=b, w=w, jj=j16, s=sc_ap: E.tensor_scalar(
                        out=accb[:, b * 2048:b * 2048 + w],
                        in0=bfb[:, b * 2048:b * 2048 + w],
                        scalar1=float(jj), scalar2=s,
                        op0=ALU.is_equal, op1=ALU.mult))
                else:
                    ins("vector", lambda E, b=b, w=w, jj=j16, s=sc_ap: E.tensor_scalar(
                        out=tmpb[:, 0:w],
                        in0=bfb[:, b * 2048:b * 2048 + w],
                        scalar1=float(jj), scalar2=s,
                        op0=ALU.is_equal, op1=ALU.mult))
                    ins("vector", lambda E, b=b, w=w: E.tensor_add(
                        accb[:, b * 2048:b * 2048 + w],
                        accb[:, b * 2048:b * 2048 + w],
                        tmpb[:, 0:w]))
            v_sel = ins("vector", lambda E: E.engine_nop(), inc=True)
            wt("tensor", "c", "vector", v_sel)
            if k >= 1:
                wt("tensor", "c", "scalar", a_ev[(k - 1, nbs[k - 1] - 1)])
            if k >= 2:
                wt_d("scalar", g_wr[k - 2])
            for j in range(nb):
                slot = j % 2
                if j >= 2:
                    wt("tensor", "c", "scalar", a_ev[(k, j - 2)])
                t_tp[(k, j)] = ins("tensor", lambda E, b=b, j=j, slot=slot: E.transpose(
                    pst[slot][:, 0:128],
                    accb[:, b * 2048 + j * 128:b * 2048 + (j + 1) * 128],
                    identb[:, :]), inc=True)
                wt("scalar", "c", "tensor", t_tp[(k, j)])
                a_ev[(k, j)] = ins("scalar", lambda E, b=b, j=j, slot=slot: E.copy(
                    out=wtb[:, b * 2048 + j * 128:b * 2048 + (j + 1) * 128],
                    in_=pst[slot][:, 0:128]), inc=True)
            wts("scalar", a_ev[(k, nb - 1)])
            L0 = c * 2048
            gg = min(L0 // 8192, 3)
            g_wr[k] = [dma("scalar", sc_part[gg][:, L0 - GOFFL[gg]:
                                                 L0 - GOFFL[gg] + w],
                           wtb[:, b * 2048:b * 2048 + w])]

        v_ccg = []
        for k in range(NMAT):
            emit_mat(k)
            c, _ = mat[k]
            if c in (3, 7, 11, 16):
                gg = len(v_ccg)
                wt_d("gpsimd", [t for kk in range(k + 1) for t in g_wr[kk]])
                v_ccg.append(ins("gpsimd", lambda E, gg=gg: E.collective_compute(
                    "AllGather", ALU.bypass, replica_groups=[list(range(8))],
                    ins=[sc_part[gg].opt()], outs=[sc_allg[gg].opt()]),
                    inc=True))
        for i in range(NP):
            if dpc["scalar"][i]:
                q["scalar"].append(lambda E, h=dpool["scalar"][i], v=dpc["scalar"][i]:
                                   E.wait_ge(h, v))
        cast_all_done = ins("scalar", lambda E: E.nop(), inc=True)

        PARTC = 33024
        GOFF = 0
        UOFF = IT * 2048
        DOFF = 2 * IT * 2048

        def load_image(dst_ap_fn, col0, w):
            toks = []
            off = 0
            while w > 0:
                g = min(col0 // 65536, 3)
                rem = col0 - GOFFG[g]
                r = rem // GW[g]
                lc = rem % GW[g]
                take = min(w, GW[g] - lc)
                toks.append(dma("sync", dst_ap_fn(off, take),
                                sc_allg[g][r, :, lc:lc + take]))
                col0 += take
                off += take
                w -= take
            return toks

        def wt_cc(e, col0, w):
            for g in range(4):
                if col0 < GOFFG[g] + 8 * GW[g] and col0 + w > GOFFG[g]:
                    wt(e, "c", "gpsimd", v_ccg[g])
        cast_all_done = ins("scalar", lambda E: E.nop(), inc=True)

        t_o = {}
        t_sup = {}
        a_oev = {}
        a_ktm = {}
        v_att = {}
        v_scast = {}
        v_dn = {}
        v_wo = {}
        t_gu = {}
        t_dn = {}
        ffn_i = [0]
        d_x = {}
        st_x = {}

        def ln_phase(marker):
            mm_first(4)
            vm = None
            for dc in range(KC):
                wt_m("tensor", marker(dc))
                last = dc == KC - 1
                vm = ins("tensor", lambda E, dc=dc, first=(dc == 0), last=last: E.matmul(
                    ps[4][0:1, 0:TS], onesk[:, :], x_sb[:, dc * TS:(dc + 1) * TS],
                    start=first, stop=last), inc=last)
            mm_last(4, vm)
            mm_first(5)
            a_sq = {}
            t_sqm = {}
            for dc in range(KC):
                wt_m("scalar", marker(dc))
                if dc >= 4:
                    wt("scalar", "c", "tensor", t_sqm[dc - 4])
                a_sq[dc] = ins("scalar", lambda E, dc=dc: E.activation(
                    out=sqb[:, (dc % 4) * TS:(dc % 4 + 1) * TS],
                    in_=x_sb[:, dc * TS:(dc + 1) * TS], func=AF.Square), inc=True)
                wt("tensor", "c", "scalar", a_sq[dc])
                last = dc == KC - 1
                t_sqm[dc] = ins("tensor", lambda E, dc=dc, first=(dc == 0), last=last: E.matmul(
                    ps[5][0:1, 0:TS], onesk[:, :],
                    sqb[:, (dc % 4) * TS:(dc % 4 + 1) * TS],
                    start=first, stop=last), inc=True)
            mm_last(5, t_sqm[KC - 1])
            prd(4, "vector", lambda E: E.tensor_copy(out=m_sb, in_=ps[4][0:1, 0:TS]))
            mm_first(4)
            t_bm = ins("tensor", lambda E: E.matmul(
                ps[4][:, 0:TS], ones_c[:, :], m_sb, start=True, stop=True), inc=True)
            mm_last(4, t_bm)
            prd(5, "vector", lambda E: E.tensor_copy(out=ms_sb, in_=ps[5][0:1, 0:TS]))
            ins("vector", lambda E: E.tensor_mul(m2_sb, m_sb, m_sb))
            ins("vector", lambda E: E.tensor_sub(var_sb, ms_sb, m2_sb))
            v_var = ins("vector", lambda E: E.tensor_scalar_add(var_sb, var_sb, EPS), inc=True)
            wt("scalar", "c", "vector", v_var)
            a_ln = ins("scalar", lambda E: E.activation(out=ln_sb, in_=var_sb, func=AF.Ln), inc=True)
            wt("vector", "c", "scalar", a_ln)
            v_l2 = ins("vector", lambda E: E.tensor_scalar_mul(ln_sb, ln_sb, -0.5), inc=True)
            wt("scalar", "c", "vector", v_l2)
            a_rstd = ins("scalar", lambda E: E.activation(
                out=rstd_sb, in_=ln_sb, func=AF.Exp), inc=True)
            wt("tensor", "c", "scalar", a_rstd)
            mm_first(5)
            t_br = ins("tensor", lambda E: E.matmul(
                ps[5][:, 0:TS], ones_c[:, :], rstd_sb, start=True, stop=True), inc=True)
            mm_last(5, t_br)
            wt("vector", "c", "tensor", t_br)
            v_app = []
            for dc in range(KC):
                wt_m("vector", marker(dc))
                ins("vector", lambda E, dc=dc: E.tensor_sub(
                    lntmp[:, :], x_sb[:, dc * TS:(dc + 1) * TS], ps[4][:, 0:TS]))
                v_app.append(ins("vector", lambda E, dc=dc: E.tensor_mul(
                    xln[:, dc * TS:(dc + 1) * TS], lntmp[:, :], ps[5][:, 0:TS]), inc=True))
            pbank[4]["r"]["vector"] = v_app[-1]
            pbank[5]["r"]["vector"] = v_app[-1]
            return v_app

        a_silu = {}
        v_h = {}

        def ffn_phase(v_xln):
            fi = ffn_i[0]
            ffn_i[0] += 1
            t_enter = cnt["tensor"]
            v_enter = cnt["vector"]
            d_g = {}
            d_u = {}

            def emit_gu_load(it):
                b = it % 3
                if it >= 3:
                    wt("sync", "c", "tensor", t_gu[(fi, it - 3)])
                else:
                    wt("sync", "c", "tensor", t_enter)
                wt_cc("sync", GOFF + it * 2048, 2048)
                wt_cc("sync", UOFF + it * 2048, 2048)
                d_g[it] = load_image(
                    lambda off, take, b=b: wg[b][:, off:off + take],
                    GOFF + it * 2048, 2048)
                d_u[it] = load_image(
                    lambda off, take, b=b: wu[b][:, off:off + take],
                    UOFF + it * 2048, 2048)

            d_d = {}

            def emit_d_load(dt):
                b = dt % 2
                if dt >= 2:
                    wt("sync", "c", "tensor", t_dn[(fi, dt - 2)])
                elif dt == 1:
                    wt("sync", "c", "tensor", t_gu[(fi, IT - 2)])
                else:
                    wt("sync", "c", "tensor", t_enter)
                wt_cc("sync", DOFF + dt * INTER, INTER)
                d_d[dt] = load_image(
                    lambda off, take, b=b: wd[b][:, off:off + take],
                    DOFF + dt * INTER, INTER)

            emit_gu_load(0)
            emit_gu_load(1)
            emit_gu_load(2)
            for it in range(IT):
                b = it % 3
                b2 = it % 2
                bg, bu = 2 * b, 2 * b + 1
                wt_d("tensor", d_g[it] + d_u[it])
                mm_first(bg)
                tg = None
                for kc in range(KC):
                    wt("tensor", "c", "vector", v_xln[kc])
                    last = kc == KC - 1
                    tg = ins("tensor", lambda E, kc=kc, b=b, bg=bg,
                             first=(kc == 0), last=last: E.matmul(
                        ps[bg][:, 0:TS], wg[b][:, kc * 128:(kc + 1) * 128],
                        xln[:, kc * TS:(kc + 1) * TS],
                        start=first, stop=last), inc=last)
                mm_last(bg, tg)
                mm_first(bu)
                tu = None
                for kc in range(KC):
                    last = kc == KC - 1
                    tu = ins("tensor", lambda E, kc=kc, b=b, bu=bu,
                             first=(kc == 0), last=last: E.matmul(
                        ps[bu][:, 0:TS], wu[b][:, kc * 128:(kc + 1) * 128],
                        xln[:, kc * TS:(kc + 1) * TS],
                        start=first, stop=last), inc=last)
                mm_last(bu, tu)
                t_gu[(fi, it)] = tu
                if it >= 2:
                    wt("scalar", "c", "vector", v_h[(fi, it - 2)])
                else:
                    wt("scalar", "c", "vector", v_enter)
                a_silu[(fi, it)] = prd(bg, "scalar", lambda E, b2=b2, bg=bg: E.activation(
                    out=sg[:, b2 * TS:(b2 + 1) * TS], in_=ps[bg][:, 0:TS], func=AF.Silu))
                wt("vector", "c", "scalar", a_silu[(fi, it)])
                if fi == 0:
                    wt("vector", "c", "scalar", cast_all_done)
                v_h[(fi, it)] = prd(bu, "vector", lambda E, it=it, b2=b2, bu=bu: E.tensor_mul(
                    h_sl[:, it * TS:(it + 1) * TS], sg[:, b2 * TS:(b2 + 1) * TS],
                    ps[bu][:, 0:TS]))
                if it + 3 < IT:
                    emit_gu_load(it + 3)
                if it == 6:
                    emit_d_load(0)
            emit_d_load(1)
            for dt in range(KC):
                b = dt % 2
                bank = dt % 2
                wt_d("tensor", d_d[dt])
                wt("tensor", "c", "vector", v_h[(fi, IT - 1)])
                mm_first(bank)
                td = None
                for ic in range(IT):
                    last = ic == IT - 1
                    td = ins("tensor", lambda E, ic=ic, b=b, bank=bank,
                             first=(ic == 0), last=last: E.matmul(
                        ps[bank][:, 0:TS], wd[b][:, ic * 128:(ic + 1) * 128],
                        h_sl[:, ic * TS:(ic + 1) * TS],
                        start=first, stop=last), inc=last)
                mm_last(bank, td)
                t_dn[(fi, dt)] = td
                v_dn[(fi, dt)] = prd(bank, "vector", lambda E, dt=dt, bank=bank:
                                     E.scalar_tensor_tensor(
                    out=x_sb[:, dt * TS:(dt + 1) * TS],
                    in0=ps[bank][:, 0:TS], scalar=0.5,
                    in1=x_sb[:, dt * TS:(dt + 1) * TS],
                    op0=ALU.mult, op1=ALU.add))
                if dt + 2 < KC:
                    emit_d_load(dt + 2)
            return fi

        def attn_phase(s, v_xln):
            t_enter = cnt["tensor"]
            v_enter = cnt["vector"]
            t_qk = {}
            a_qk = {}
            d_qk = {}
            qk_srcs = [(sc_q, q_fm, hh) for hh in range(NH)] + \
                      [(sc_k, k_fm, hh) for hh in range(NH)]

            def emit_qk_load(i):
                scd, dstv, hh = qk_srcs[i]
                b = i % 2
                if i >= 2:
                    wt("sync", "c", "tensor", t_qk[i - 2])
                else:
                    wt("sync", "c", "tensor", t_enter)
                wt_d("sync", cast_toks["q"] + cast_toks["k"])
                d_qk[i] = dma("sync", qkw[b][:, :], scd[hh, :, :])

            v_u_last = 0
            t_vb_last = {}
            a_sig_first = [True]
            d_vb = {}
            vb4 = [vbw[0][:, 0:4096], vbw[0][:, 4096:8192],
                   vbw[1][:, 0:4096], vbw[1][:, 4096:8192]]

            def emit_vb_load(sl8):
                if sl8 >= 2:
                    wt("sync", "c", "tensor", t_vb_last[sl8 - 2])
                    wt("gpsimd", "c", "tensor", t_vb_last[sl8 - 2])
                else:
                    wt("sync", "c", "tensor", t_enter)
                    wt("gpsimd", "c", "tensor", t_enter)
                    wt_d("sync", cast_toks["v"])
                    wt_d("gpsimd", cast_toks["b"])
                bq = (sl8 % 2) * 2
                d_vb[sl8] = [dma("sync", vb4[bq], sc_v[sl8]),
                             dma("gpsimd", vb4[bq + 1], sc_b[sl8])]

            emit_qk_load(0)
            emit_qk_load(1)
            emit_vb_load(0)
            emit_vb_load(1)
            for i in range(2 * NH):
                scd, dstv, hh = qk_srcs[i]
                b = i % 2
                bank = i % 2
                wt_d("tensor", d_qk[i])
                mm_first(bank)
                tl = None
                for kc in range(KC):
                    wt("tensor", "c", "vector", v_xln[kc])
                    last = kc == KC - 1
                    tl = ins("tensor", lambda E, kc=kc, b=b, bank=bank,
                             first=(kc == 0), last=last: E.matmul(
                        ps[bank][:, 0:TS], qkw[b][:, kc * 128:(kc + 1) * 128],
                        xln[:, kc * TS:(kc + 1) * TS],
                        start=first, stop=last), inc=last)
                mm_last(bank, tl)
                t_qk[i] = tl
                a_qk[i] = prd(bank, "scalar", lambda E, hh=hh, bank=bank, dstv=dstv:
                              E.copy(out=dstv[:, hh * TS:(hh + 1) * TS],
                                     in_=ps[bank][:, 0:TS]))
                if i + 2 < 2 * NH:
                    emit_qk_load(i + 2)
            for sl8 in range(8):
                bq = (sl8 % 2) * 2
                for tt in range(NCH):
                    bv, bb = 2 * (tt % 2), 2 * (tt % 2) + 1
                    wt_d("tensor", d_vb[sl8])
                    mm_first(bv)
                    tv = None
                    for kc in range(KC):
                        last = kc == KC - 1
                        tv = ins("tensor", lambda E, kc=kc, tt=tt, bv=bv, bq=bq,
                                 first=(kc == 0), last=last: E.matmul(
                            ps[bv][:, 0:256],
                            xln[:, kc * TS + tt * 128:kc * TS + (tt + 1) * 128],
                            vb4[bq][:, kc * 256:(kc + 1) * 256],
                            start=first, stop=last), inc=last)
                    mm_last(bv, tv)
                    mm_first(bb)
                    tb = None
                    for kc in range(KC):
                        last = kc == KC - 1
                        tb = ins("tensor", lambda E, kc=kc, tt=tt, bb=bb, bq=bq,
                                 first=(kc == 0), last=last: E.matmul(
                            ps[bb][:, 0:256],
                            xln[:, kc * TS + tt * 128:kc * TS + (tt + 1) * 128],
                            vb4[bq + 1][:, kc * 256:(kc + 1) * 256],
                            start=first, stop=last), inc=last)
                    mm_last(bb, tb)
                    t_vb_last[sl8] = tb
                    if a_sig_first[0]:
                        wt("scalar", "c", "vector", v_enter)
                        a_sig_first[0] = False
                    else:
                        wt("scalar", "c", "vector", v_u_last)
                    a_sig = prd(bb, "scalar", lambda E, bb=bb: E.activation(
                        out=sg[:, 0:256], in_=ps[bb][:, 0:256], func=AF.Sigmoid))
                    wt("vector", "c", "scalar", a_sig)
                    v_u_last = prd(bv, "vector", lambda E, sl8=sl8, tt=tt, bv=bv:
                                   E.tensor_mul(
                        u_tm[:, tt * D + sl8 * 256:tt * D + (sl8 + 1) * 256],
                        sg[:, 0:256], ps[bv][:, 0:256]))
                if sl8 + 2 < 8:
                    emit_vb_load(sl8 + 2)
            for bk in (0, 1, 2, 3, 4, 5, "t0", "t1"):
                for oe, v in pbank[bk]["r"].items():
                    wt("tensor", "c", oe, v)
                pbank[bk]["r"] = {}

            def head_tail(gc, c, hh):
                bo = 2 if hh % 2 == 0 else 5
                wt("tensor", "c", "vector", v_att[(gc, hh)])
                if (gc, hh - 2) in a_oev:
                    wt("tensor", "c", "scalar", a_oev[(gc, hh - 2)])
                elif (gc - 1, NH - 2 + (hh % 2)) in a_oev:
                    wt("tensor", "c", "scalar", a_oev[(gc - 1, NH - 2 + (hh % 2))])
                if (gc - 1, hh) in v_scast:
                    wt("tensor", "c", "vector", v_scast[(gc - 1, hh)])
                ins("tensor", lambda E, c=c, hh=hh, bo=bo: E.matmul(
                    ps[bo][:, 0:128],
                    u_tm[:, c * D + hh * HD:c * D + (hh + 1) * HD],
                    attm[:, (hh % 2) * 128:(hh % 2 + 1) * 128],
                    start=True, stop=False))
                t_o[(gc, hh)] = ins("tensor", lambda E, c=c, hh=hh, bo=bo: E.matmul(
                    ps[bo][:, 0:128],
                    S_bf[:, hh * HD:(hh + 1) * HD],
                    q_fm[:, hh * TS + c * 128:hh * TS + (c + 1) * 128],
                    start=False, stop=True), inc=True)
                bs = 3 + (hh % 2)
                wt("tensor", "c", "scalar", a_ktm[(gc, hh)])
                if (gc, hh - 2) in v_scast:
                    wt("tensor", "c", "vector", v_scast[(gc, hh - 2)])
                elif c > 0:
                    wt("tensor", "c", "vector", v_scast[(gc - 1, NH - 2 + (hh % 2))])
                t_sup[(gc, hh)] = ins("tensor", lambda E, c=c, hh=hh, bs=bs: E.matmul(
                    ps[bs][:, 0:128],
                    ktm[:, (hh % 2) * 128:(hh % 2 + 1) * 128],
                    u_tm[:, c * D + hh * HD:c * D + (hh + 1) * HD],
                    start=True, stop=True), inc=True)
                wt("scalar", "c", "tensor", t_o[(gc, hh)])
                a_oev[(gc, hh)] = ins("scalar", lambda E, c=c, hh=hh, bo=bo: E.copy(
                    out=o_fm[:, hh * TS + c * 128:hh * TS + (c + 1) * 128],
                    in_=ps[bo][:, 0:128]), inc=True)
                wt("vector", "c", "tensor", t_sup[(gc, hh)])
                ins("vector", lambda E, hh=hh, bs=bs: E.tensor_add(
                    S_f32[:, hh * HD:(hh + 1) * HD],
                    S_f32[:, hh * HD:(hh + 1) * HD],
                    ps[bs][:, 0:128]))
                v_scast[(gc, hh)] = ins("vector", lambda E, hh=hh: E.tensor_copy(
                    out=S_bf[:, hh * HD:(hh + 1) * HD],
                    in_=S_f32[:, hh * HD:(hh + 1) * HD]), inc=True)

            for c in range(NCH):
                gc = s * NCH + c
                for hh in range(NH):
                    slot = hh % 2
                    ba = hh % 2
                    if (gc, hh - 2) in a_ktm:
                        wt("tensor", "c", "scalar", a_ktm[(gc, hh - 2)])
                    elif (gc - 1, NH - 2 + slot) in a_ktm:
                        wt("tensor", "c", "scalar", a_ktm[(gc - 1, NH - 2 + slot)])
                    wt("tensor", "c", "scalar", a_qk[NH + hh])
                    wt("tensor", "c", "vector", v_u_last)
                    t_ktp = ins("tensor", lambda E, c=c, hh=hh, slot=slot: E.transpose(
                        pst[slot][:, 0:128],
                        k_fm[:, hh * TS + c * 128:hh * TS + (c + 1) * 128],
                        identb[:, :]), inc=True)
                    if (gc, hh - 2) in v_att:
                        wt("tensor", "c", "vector", v_att[(gc, hh - 2)])
                    elif c > 0:
                        wt("tensor", "c", "vector", v_att[(gc - 1, NH - 2 + ba)])
                    wt("tensor", "c", "scalar", a_qk[hh])
                    t_att = ins("tensor", lambda E, c=c, hh=hh, ba=ba: E.matmul(
                        ps[ba][:, 0:128],
                        k_fm[:, hh * TS + c * 128:hh * TS + (c + 1) * 128],
                        q_fm[:, hh * TS + c * 128:hh * TS + (c + 1) * 128],
                        start=True, stop=True), inc=True)
                    wt("scalar", "c", "tensor", t_ktp)
                    if (gc, hh - 2) in t_sup:
                        wt("scalar", "c", "tensor", t_sup[(gc, hh - 2)])
                    elif (gc - 1, NH - 2 + slot) in t_sup:
                        wt("scalar", "c", "tensor", t_sup[(gc - 1, NH - 2 + slot)])
                    a_ktm[(gc, hh)] = ins("scalar", lambda E, slot=slot: E.copy(
                        out=ktm[:, slot * 128:(slot + 1) * 128],
                        in_=pst[slot][:, 0:128]), inc=True)
                    wt("vector", "c", "tensor", t_att)
                    if (gc, hh - 2) in t_o:
                        wt("vector", "c", "tensor", t_o[(gc, hh - 2)])
                    elif (gc - 1, NH - 2 + ba) in t_o:
                        wt("vector", "c", "tensor", t_o[(gc - 1, NH - 2 + ba)])
                    v_att[(gc, hh)] = ins("vector", lambda E, ba=ba, slot=slot: E.tensor_mul(
                        attm[:, slot * 128:(slot + 1) * 128],
                        ps[ba][:, 0:128], maskc[:, :]), inc=True)
                    if hh > 0:
                        head_tail(gc, c, hh - 1)
                head_tail(gc, c, NH - 1)
            for bk in (0, 1, 2, 3, 4, 5):
                pbank[bk]["w"] = ("tensor", cnt["tensor"])
            pbank[0]["r"]["vector"] = cnt["vector"]
            pbank[1]["r"]["vector"] = cnt["vector"]
            pbank[2]["r"]["scalar"] = cnt["scalar"]
            pbank[5]["r"]["scalar"] = cnt["scalar"]
            pbank[3]["r"]["vector"] = cnt["vector"]
            pbank[4]["r"]["vector"] = cnt["vector"]
            pbank["t0"]["r"]["scalar"] = cnt["scalar"]
            pbank["t1"]["r"]["scalar"] = cnt["scalar"]

            gcl = s * NCH + NCH - 1
            t_wo = {}
            d_wo = {}

            def emit_wo_load(dt):
                b = dt % 2
                if dt >= 2:
                    wt("sync", "c", "tensor", t_wo[dt - 2])
                else:
                    wt("sync", "c", "tensor", t_qk[2 * NH - 2 + b])
                    wt_d("sync", cast_toks["o"])
                d_wo[dt] = dma("sync", qkw[b][:, :], sc_o[dt, :, :])

            emit_wo_load(0)
            emit_wo_load(1)
            for dt in range(KC):
                b = dt % 2
                bank = dt % 2
                wt_d("tensor", d_wo[dt])
                wt("tensor", "c", "scalar", a_oev[(gcl, NH - 1)])
                mm_first(bank)
                tw = None
                for kc in range(KC):
                    last = kc == KC - 1
                    tw = ins("tensor", lambda E, kc=kc, b=b, bank=bank,
                             first=(kc == 0), last=last: E.matmul(
                        ps[bank][:, 0:TS], qkw[b][:, kc * 128:(kc + 1) * 128],
                        o_fm[:, kc * TS:(kc + 1) * TS],
                        start=first, stop=last), inc=last)
                mm_last(bank, tw)
                t_wo[dt] = tw
                v_wo[(s, dt)] = prd(bank, "vector", lambda E, dt=dt, bank=bank:
                                    E.tensor_add(
                    x_sb[:, dt * TS:(dt + 1) * TS],
                    x_sb[:, dt * TS:(dt + 1) * TS],
                    ps[bank][:, 0:TS]))
                if dt + 2 < KC:
                    emit_wo_load(dt + 2)

        for s in range(NSLAB):
            for dc in range(KC):
                if s > 0:
                    wt_d("sync", st_x[(s - 1, dc)])
                d_x[(s, dc)] = dma("sync", x_sb[:, dc * TS:(dc + 1) * TS],
                                   x_in[dc * 128:(dc + 1) * 128, s * TS:(s + 1) * TS])

            v_xln1 = ln_phase(lambda dc, s=s: ("d", d_x[(s, dc)]))
            fi1 = ffn_phase(v_xln1)
            v_xln2 = ln_phase(lambda dc, fi=fi1: ("c", "vector", v_dn[(fi, dc)]))
            attn_phase(s, v_xln2)
            v_xln3 = ln_phase(lambda dc, s=s: ("c", "vector", v_wo[(s, dc)]))
            fi2 = ffn_phase(v_xln3)

            for dc in range(KC):
                wt("gpsimd", "c", "vector", v_dn[(fi2, dc)])
                st_x[(s, dc)] = dma("gpsimd",
                                    y_out[dc * 128:(dc + 1) * 128, s * TS:(s + 1) * TS],
                                    x_sb[:, dc * TS:(dc + 1) * TS])

        for e in DQUEUES:
            for i in range(NP):
                if dpc[e][i]:
                    q[e].append(lambda E, h=dpool[e][i], v=dpc[e][i]: E.wait_ge(h, v))

        block = es.enter_context(nc.Block())

        @block.sync
        def _(E):
            for f in q["sync"]:
                f(E)

        @block.gpsimd
        def _(E):
            for f in q["gpsimd"]:
                f(E)

        @block.vector
        def _(E):
            for f in q["vector"]:
                f(E)

        @block.scalar
        def _(E):
            for f in q["scalar"]:
                f(E)

        @block.tensor
        def _(E):
            for f in q["tensor"]:
                f(E)

    return nc


_CACHE = {}


def kernel(L0_attn_hidden_states, L0_attn_Wq, L0_attn_Wk, L0_attn_Wv,
           L0_attn_Wb, L0_attn_Wo, L0_ffn_lut, L0_ffn_gate_idx,
           L0_ffn_up_idx, L0_ffn_down_idx):
    x = np.asarray(L0_attn_hidden_states, dtype=np.float32)
    gidx = np.asarray(L0_ffn_gate_idx, dtype=np.int32)
    uidx = np.asarray(L0_ffn_up_idx, dtype=np.int32)
    didx = np.asarray(L0_ffn_down_idx, dtype=np.int32)
    gb = gidx.reshape(IT, 128, KC, 128).transpose(0, 2, 1, 3).reshape(-1, 128, 128)
    ub = uidx.reshape(IT, 128, KC, 128).transpose(0, 2, 1, 3).reshape(-1, 128, 128)
    db = didx.reshape(KC, 128, IT, 128).transpose(0, 2, 1, 3).reshape(-1, 128, 128)
    blocks = np.concatenate([gb, ub, db], axis=0)
    base = {
        "wq": np.ascontiguousarray(np.asarray(L0_attn_Wq, dtype=np.float32)),
        "wk": np.ascontiguousarray(np.asarray(L0_attn_Wk, dtype=np.float32)),
        "wv": np.ascontiguousarray(np.asarray(L0_attn_Wv, dtype=np.float32)),
        "wb": np.ascontiguousarray(np.asarray(L0_attn_Wb, dtype=np.float32)),
        "wo": np.ascontiguousarray(np.asarray(L0_attn_Wo, dtype=np.float32)),
        "lut": np.ascontiguousarray(np.asarray(L0_ffn_lut, dtype=np.float32).reshape(1, 16)),
    }
    if "nc" not in _CACHE:
        _CACHE["nc"] = build()
    nc = _CACHE["nc"]
    in_maps = []
    for c in range(8):
        m = dict(base)
        m["x_in"] = np.ascontiguousarray(x[c].T)
        own = np.concatenate([blocks[512 * 0 + c * 64:512 * 0 + (c + 1) * 64],
                              blocks[512 * 1 + c * 64:512 * 1 + (c + 1) * 64],
                              blocks[512 * 2 + c * 64:512 * 2 + (c + 1) * 64],
                              blocks[1536 + c * 66:1536 + (c + 1) * 66]], axis=0)
        m["myidx"] = np.ascontiguousarray(
            own.transpose(1, 0, 2).reshape(128, 33024))
        in_maps.append(m)
    res = run_bass_kernel_spmd(nc, in_maps, core_ids=list(range(8)))
    out = np.stack([np.asarray(res.results[c]["y"]).T for c in range(8)], axis=0)
    return np.ascontiguousarray(out.astype(np.float32))


# version 28
# speedup vs baseline: 1.1142x; 1.0014x over previous
import numpy as np
import contextlib
import concourse.bass as bass
import concourse.mybir as mybir
from concourse.bass_utils import run_bass_kernel_spmd

F32 = mybir.dt.float32
BF16 = mybir.dt.bfloat16
I32 = mybir.dt.int32
AF = mybir.ActivationFunctionType
ALU = mybir.AluOpType

D = 2048
SEQ = 2048
NH = 16
HD = 128
INTER = 5504
KC = D // 128
IT = INTER // 128
TS = 512
NSLAB = SEQ // TS
CH = 128
NCH = TS // CH
EPS = 1e-5

ENGS = ("sync", "gpsimd", "vector", "scalar", "tensor")
DQUEUES = ("sync", "scalar", "gpsimd")

DEBUG_EDGES = False


def build():
    nc = bass.Bass("TRN2", target_bir_lowering=False,
                   detect_race_conditions=DEBUG_EDGES)

    x_in = nc.dram_tensor("x_in", [D, SEQ], F32, kind="ExternalInput").ap()
    wq_d = nc.dram_tensor("wq", [D, D], F32, kind="ExternalInput").ap()
    wk_d = nc.dram_tensor("wk", [D, D], F32, kind="ExternalInput").ap()
    wv_d = nc.dram_tensor("wv", [D, D], F32, kind="ExternalInput").ap()
    wb_d = nc.dram_tensor("wb", [D, D], F32, kind="ExternalInput").ap()
    wo_d = nc.dram_tensor("wo", [D, D], F32, kind="ExternalInput").ap()
    lut_d = nc.dram_tensor("lut", [1, 16], F32, kind="ExternalInput").ap()
    y_out = nc.dram_tensor("y", [D, SEQ], F32, kind="ExternalOutput").ap()

    myidx_d = nc.dram_tensor("myidx", [128, 33024], I32, kind="ExternalInput").ap()
    GSZ = (64, 64, 64, 66)
    GW = [s * 128 for s in GSZ]
    GOFFL = [0, 8192, 16384, 24576]
    GOFFG = [0, 65536, 131072, 196608]
    sc_part = [nc.dram_tensor(f"sc_part{g}", [128, GW[g]], BF16,
                              kind="Internal").ap() for g in range(4)]
    sc_allg = [nc.dram_tensor(f"sc_all{g}", [8, 128, GW[g]], BF16,
                              kind="Internal", addr_space="Shared").ap()
               for g in range(4)]
    sc_q = nc.dram_tensor("sc_q", [NH, 128, D], BF16, kind="Internal").ap()
    sc_k = nc.dram_tensor("sc_k", [NH, 128, D], BF16, kind="Internal").ap()
    sc_o = nc.dram_tensor("sc_o", [KC, 128, D], BF16, kind="Internal").ap()
    sc_v = nc.dram_tensor("sc_v", [8, 128, KC * 256], BF16, kind="Internal").ap()
    sc_b = nc.dram_tensor("sc_b", [8, 128, KC * 256], BF16, kind="Internal").ap()

    es = contextlib.ExitStack()
    with es:
        sbuf = lambda name, shp, dt: es.enter_context(nc.sbuf_tensor(name, shp, dt))
        psal = lambda name, shp, dt: es.enter_context(nc.psum_tensor(name, shp, dt))

        x_sb = sbuf("x_sb", [128, KC * TS], F32)
        xln = sbuf("xln", [128, KC * TS], BF16)
        pool = sbuf("pool", [128, 24576], BF16)
        h_sl = pool[:, 0:IT * TS]
        q_fm = pool[:, 0:8192]
        k_fm = pool[:, 8192:16384]
        u_tm = pool[:, 16384:24576]
        o_fm = sbuf("o_fm", [128, 8192], BF16)
        S_f32 = sbuf("S_f32", [128, NH * HD], F32)
        S_bf = sbuf("S_bf", [128, NH * HD], BF16)
        wpool = sbuf("wpool", [128, 20480], BF16)
        wg = [wpool[:, 0:2048], wpool[:, 2048:4096], wpool[:, 16384:18432]]
        wu = [wpool[:, 4096:6144], wpool[:, 6144:8192], wpool[:, 18432:20480]]
        wd = [wpool[:, 8192:8192 + INTER], wpool[:, 8192 + INTER:8192 + 2 * INTER]]
        qkw = [wpool[:, 0:2048], wpool[:, 2048:4096]]
        vbw = [wpool[:, 4096:12288], wpool[:, 12288:20480]]
        sg = sbuf("sg", [128, 2 * TS], BF16)
        sqb = sbuf("sqb", [128, 4 * TS], F32)
        lntmp = sbuf("lntmp", [128, TS], F32)
        ktm = sbuf("ktm", [128, 256], BF16)
        attm = sbuf("attm", [128, 256], BF16)
        stat = sbuf("stat", [1, 6 * TS], F32)
        m_sb = stat[0:1, 0:TS]
        ms_sb = stat[0:1, TS:2 * TS]
        m2_sb = stat[0:1, 2 * TS:3 * TS]
        var_sb = stat[0:1, 3 * TS:4 * TS]
        ln_sb = stat[0:1, 4 * TS:5 * TS]
        rstd_sb = stat[0:1, 5 * TS:6 * TS]
        lut_sb = sbuf("lut_sb", [1, 16], F32)
        lutb = sbuf("lutb", [128, 16], F32)
        ones_c = sbuf("ones_c", [1, 128], F32)
        onesk = sbuf("onesk", [128, 1], F32)
        maskc = sbuf("maskc", [128, 128], BF16)
        identb = sbuf("identb", [128, 128], BF16)
        rawi = sbuf("rawi", [128, 4096], I32)
        bfb = pool[:, 0:4096]
        accb = pool[:, 4096:8192]
        tmpb = pool[:, 8192:10240]
        wtb = pool[:, 10240:14336]
        cwt = pool[:, 14336:18432]
        rawf = x_sb[:, 0:4096]

        ps = [psal(f"ps{i}", [128, 512], F32) for i in range(6)]
        pst = [psal("pst0", [128, 1024], BF16), psal("pst1", [128, 1024], BF16)]

        csem = {e: es.enter_context(nc.semaphore("c_" + e)) for e in ENGS}
        NP = 1 if DEBUG_EDGES else 8
        dpool = {e: [es.enter_context(nc.semaphore(f"d_{e}{i}")) for i in range(NP)]
                 for e in DQUEUES}
        dpc = {e: [0] * NP for e in DQUEUES}
        dseq = {e: 0 for e in DQUEUES}
        cnt = {e: 0 for e in ENGS}
        q = {e: [] for e in ENGS}
        lastw = {e: {} for e in ENGS}

        def ins(e, f, inc=False):
            if DEBUG_EDGES:
                prev = cnt[e]
                h = csem[e]
                if prev:
                    q[e].append(lambda E, h=h, v=prev: E.wait_ge(h, v))
                cnt[e] += 1
                v = cnt[e]
                q[e].append(lambda E, f=f, h=h: f(E).then_inc(h, 1))
                return v
            if inc:
                cnt[e] += 1
                v = cnt[e]
                h = csem[e]
                q[e].append(lambda E, f=f, h=h: f(E).then_inc(h, 1))
                return v
            q[e].append(f)
            return None

        def dma(e, out_ap, in_ap):
            i = dseq[e] % NP
            dseq[e] += 1
            h = dpool[e][i]
            if dpc[e][i]:
                q[e].append(lambda E, h=h, v=dpc[e][i]: E.wait_ge(h, v))
            dpc[e][i] += 16
            v = dpc[e][i]
            q[e].append(lambda E, o=out_ap, i_=in_ap, h=h:
                        E.dma_start(out=o, in_=i_).then_inc(h, 16))
            return (e, i, v)

        def wt(e, kind, oe, v):
            if not v or v <= 0:
                return
            if kind == "c" and oe == e:
                return
            key = (kind, oe)
            if lastw[e].get(key, 0) >= v:
                return
            lastw[e][key] = v
            h = csem[oe]
            q[e].append(lambda E, h=h, v=v: E.wait_ge(h, v))

        def wts(e, v):
            if not v:
                return
            key = ("self",)
            if lastw[e].get(key, 0) >= v:
                return
            lastw[e][key] = v
            q[e].append(lambda E, h=csem[e], v=v: E.wait_ge(h, v))

        def wt_d(e, tokens):
            if tokens is None:
                return
            if isinstance(tokens, tuple):
                tokens = [tokens]
            for (oe, i, v) in tokens:
                key = ("d", oe, i)
                if lastw[e].get(key, 0) >= v:
                    continue
                lastw[e][key] = v
                h = dpool[oe][i]
                q[e].append(lambda E, h=h, v=v: E.wait_ge(h, v))

        def wt_m(e, m):
            if m[0] == "c":
                wt(e, "c", m[1], m[2])
            else:
                wt_d(e, m[1])

        pbank = {i: {"w": ("tensor", 0), "r": {}}
                 for i in list(range(6)) + ["t0", "t1"]}

        def mm_first(bank):
            for oe, v in pbank[bank]["r"].items():
                wt("tensor", "c", oe, v)
            pbank[bank]["r"] = {}

        def mm_last(bank, v):
            pbank[bank]["w"] = ("tensor", v)

        def prd(bank, e, f):
            we, wv = pbank[bank]["w"]
            wt(e, "c", we, wv)
            v = ins(e, f, inc=True)
            pbank[bank]["r"][e] = max(pbank[bank]["r"].get(e, 0), v)
            return v

        ins("gpsimd", lambda E: E.memset(ones_c[:, :], 1.0))
        ins("gpsimd", lambda E: E.memset(onesk[:, :], 1.0 / D))
        g1 = ins("gpsimd", lambda E: E.memset(identb[:, :], 0.0), inc=True)
        q["gpsimd"].append(lambda E, h=csem["gpsimd"], v=g1: E.wait_ge(h, v))
        ins("gpsimd", lambda E: E.affine_select(
            out=identb[:, :], in_=identb[:, :], compare_op=ALU.not_equal,
            fill=1.0, base=0, pattern=[[-1, 128]], channel_multiplier=1))
        g2 = ins("gpsimd", lambda E: E.memset(maskc[:, :], 1.0), inc=True)
        q["gpsimd"].append(lambda E, h=csem["gpsimd"], v=g2: E.wait_ge(h, v))
        ins("gpsimd", lambda E: E.affine_select(
            out=maskc[:, :], in_=maskc[:, :], compare_op=ALU.is_ge,
            fill=0.0, base=0, pattern=[[1, 128]], channel_multiplier=-1))
        ins("gpsimd", lambda E: E.memset(S_f32[:, :], 0.0))
        v_init = ins("gpsimd", lambda E: E.memset(S_bf[:, :], 0.0), inc=True)

        d_lut = dma("sync", lut_sb[:, :], lut_d)
        wt_d("tensor", d_lut)
        wt("tensor", "c", "gpsimd", v_init)
        mm_first(5)
        t_lut = ins("tensor", lambda E: E.matmul(
            ps[5][:, 0:16], ones_c[:, :], lut_sb[:, :], start=True, stop=True), inc=True)
        mm_last(5, t_lut)
        v_lut = prd(5, "vector", lambda E: E.tensor_copy(out=lutb[:, :], in_=ps[5][:, 0:16]))
        q["vector"].append(lambda E, h=csem["vector"], v=v_lut: E.wait_ge(h, v))
        wt("vector", "c", "gpsimd", v_init)

        mat = [(c, 2048) for c in range(16)] + [(16, 256)]
        NMAT = len(mat)

        cast_toks = {wn: [] for wn in ("q", "k", "v", "b", "o")}
        for wn, wdram in (("q", wq_d), ("k", wk_d), ("v", wv_d), ("b", wb_d),
                          ("o", wo_d)):
            for kc in range(KC):
                if wn in ("q", "k", "o"):
                    scd = {"q": sc_q, "k": sc_k, "o": sc_o}[wn]
                    tok = dma(
                        "gpsimd",
                        scd[:, :, kc * 128:(kc + 1) * 128].rearrange("h p c -> p h c"),
                        wdram[kc * 128:(kc + 1) * 128, :].rearrange(
                            "p (h c) -> p h c", c=128))
                else:
                    scd = sc_v if wn == "v" else sc_b
                    tok = dma(
                        "gpsimd",
                        scd[:, :, kc * 256:(kc + 1) * 256].rearrange("s p c -> p s c"),
                        wdram[kc * 128:(kc + 1) * 128, :].rearrange(
                            "p (s c) -> p s c", c=256))
                cast_toks[wn].append(tok)

        v_cast = {}
        t_tp = {}
        a_ev = {}
        g_wr = {}
        nbs = [w // 128 for (_, w) in mat]

        idx_ld = {}

        def emit_idx_load(k):
            c, w = mat[k]
            b = k % 2
            if k >= 2:
                wt("sync", "c", "vector", v_cast[k - 2])
            idx_ld[k] = dma("sync", rawi[:, b * 2048:b * 2048 + w],
                            myidx_d[:, c * 2048:c * 2048 + w])

        emit_idx_load(0)
        emit_idx_load(1)

        def emit_mat(k):
            c, w = mat[k]
            b = k % 2
            nb = nbs[k]
            ld = idx_ld[k]
            wt_d("vector", ld)
            v_cast[k] = ins("vector", lambda E, b=b, w=w: E.tensor_copy(
                out=bfb[:, b * 2048:b * 2048 + w],
                in_=rawi[:, b * 2048:b * 2048 + w]), inc=True)
            if k + 2 < NMAT:
                emit_idx_load(k + 2)
            if k >= 2:
                wt("vector", "c", "tensor", t_tp[(k - 2, nbs[k - 2] - 1)])
            for j16 in range(16):
                sc_ap = lutb[:, j16:j16 + 1]
                if j16 == 0:
                    ins("vector", lambda E, b=b, w=w, jj=j16, s=sc_ap: E.tensor_scalar(
                        out=accb[:, b * 2048:b * 2048 + w],
                        in0=bfb[:, b * 2048:b * 2048 + w],
                        scalar1=float(jj), scalar2=s,
                        op0=ALU.is_equal, op1=ALU.mult))
                else:
                    ins("vector", lambda E, b=b, w=w, jj=j16, s=sc_ap: E.tensor_scalar(
                        out=tmpb[:, 0:w],
                        in0=bfb[:, b * 2048:b * 2048 + w],
                        scalar1=float(jj), scalar2=s,
                        op0=ALU.is_equal, op1=ALU.mult))
                    ins("vector", lambda E, b=b, w=w: E.tensor_add(
                        accb[:, b * 2048:b * 2048 + w],
                        accb[:, b * 2048:b * 2048 + w],
                        tmpb[:, 0:w]))
            v_sel = ins("vector", lambda E: E.engine_nop(), inc=True)
            wt("tensor", "c", "vector", v_sel)
            if k >= 1:
                wt("tensor", "c", "scalar", a_ev[(k - 1, nbs[k - 1] - 1)])
            if k >= 2:
                wt_d("scalar", g_wr[k - 2])
            for j in range(nb):
                slot = j % 2
                if j >= 2:
                    wt("tensor", "c", "scalar", a_ev[(k, j - 2)])
                t_tp[(k, j)] = ins("tensor", lambda E, b=b, j=j, slot=slot: E.transpose(
                    pst[slot][:, 0:128],
                    accb[:, b * 2048 + j * 128:b * 2048 + (j + 1) * 128],
                    identb[:, :]), inc=True)
                wt("scalar", "c", "tensor", t_tp[(k, j)])
                a_ev[(k, j)] = ins("scalar", lambda E, b=b, j=j, slot=slot: E.copy(
                    out=wtb[:, b * 2048 + j * 128:b * 2048 + (j + 1) * 128],
                    in_=pst[slot][:, 0:128]), inc=True)
            wts("scalar", a_ev[(k, nb - 1)])
            L0 = c * 2048
            gg = min(L0 // 8192, 3)
            g_wr[k] = [dma("scalar", sc_part[gg][:, L0 - GOFFL[gg]:
                                                 L0 - GOFFL[gg] + w],
                           wtb[:, b * 2048:b * 2048 + w])]

        v_ccg = []
        for k in range(NMAT):
            emit_mat(k)
            c, _ = mat[k]
            if c in (3, 7, 11, 16):
                gg = len(v_ccg)
                wt_d("gpsimd", [t for kk in range(k + 1) for t in g_wr[kk]])
                v_ccg.append(ins("gpsimd", lambda E, gg=gg: E.collective_compute(
                    "AllGather", ALU.bypass, replica_groups=[list(range(8))],
                    ins=[sc_part[gg].opt()], outs=[sc_allg[gg].opt()]),
                    inc=True))
        for i in range(NP):
            if dpc["scalar"][i]:
                q["scalar"].append(lambda E, h=dpool["scalar"][i], v=dpc["scalar"][i]:
                                   E.wait_ge(h, v))
        cast_all_done = ins("scalar", lambda E: E.nop(), inc=True)

        PARTC = 33024
        GOFF = 0
        UOFF = IT * 2048
        DOFF = 2 * IT * 2048

        def load_image(dst_ap_fn, col0, w):
            toks = []
            off = 0
            while w > 0:
                g = min(col0 // 65536, 3)
                rem = col0 - GOFFG[g]
                r = rem // GW[g]
                lc = rem % GW[g]
                take = min(w, GW[g] - lc)
                toks.append(dma("sync", dst_ap_fn(off, take),
                                sc_allg[g][r, :, lc:lc + take]))
                col0 += take
                off += take
                w -= take
            return toks

        def wt_cc(e, col0, w):
            for g in range(4):
                if col0 < GOFFG[g] + 8 * GW[g] and col0 + w > GOFFG[g]:
                    wt(e, "c", "gpsimd", v_ccg[g])
        cast_all_done = ins("scalar", lambda E: E.nop(), inc=True)

        t_o = {}
        t_sup = {}
        a_oev = {}
        a_ktm = {}
        v_att = {}
        v_scast = {}
        v_dn = {}
        v_wo = {}
        t_gu = {}
        t_dn = {}
        ffn_i = [0]
        d_x = {}
        st_x = {}

        def ln_phase(marker):
            mm_first(4)
            vm = None
            for dc in range(KC):
                wt_m("tensor", marker(dc))
                last = dc == KC - 1
                vm = ins("tensor", lambda E, dc=dc, first=(dc == 0), last=last: E.matmul(
                    ps[4][0:1, 0:TS], onesk[:, :], x_sb[:, dc * TS:(dc + 1) * TS],
                    start=first, stop=last), inc=last)
            mm_last(4, vm)
            mm_first(5)
            a_sq = {}
            t_sqm = {}
            for dc in range(KC):
                wt_m("scalar", marker(dc))
                if dc >= 4:
                    wt("scalar", "c", "tensor", t_sqm[dc - 4])
                a_sq[dc] = ins("scalar", lambda E, dc=dc: E.activation(
                    out=sqb[:, (dc % 4) * TS:(dc % 4 + 1) * TS],
                    in_=x_sb[:, dc * TS:(dc + 1) * TS], func=AF.Square), inc=True)
                wt("tensor", "c", "scalar", a_sq[dc])
                last = dc == KC - 1
                t_sqm[dc] = ins("tensor", lambda E, dc=dc, first=(dc == 0), last=last: E.matmul(
                    ps[5][0:1, 0:TS], onesk[:, :],
                    sqb[:, (dc % 4) * TS:(dc % 4 + 1) * TS],
                    start=first, stop=last), inc=True)
            mm_last(5, t_sqm[KC - 1])
            prd(4, "vector", lambda E: E.tensor_copy(out=m_sb, in_=ps[4][0:1, 0:TS]))
            mm_first(4)
            t_bm = ins("tensor", lambda E: E.matmul(
                ps[4][:, 0:TS], ones_c[:, :], m_sb, start=True, stop=True), inc=True)
            mm_last(4, t_bm)
            prd(5, "vector", lambda E: E.tensor_copy(out=ms_sb, in_=ps[5][0:1, 0:TS]))
            ins("vector", lambda E: E.tensor_mul(m2_sb, m_sb, m_sb))
            ins("vector", lambda E: E.tensor_sub(var_sb, ms_sb, m2_sb))
            v_var = ins("vector", lambda E: E.tensor_scalar_add(var_sb, var_sb, EPS), inc=True)
            wt("scalar", "c", "vector", v_var)
            a_ln = ins("scalar", lambda E: E.activation(out=ln_sb, in_=var_sb, func=AF.Ln), inc=True)
            wt("vector", "c", "scalar", a_ln)
            v_l2 = ins("vector", lambda E: E.tensor_scalar_mul(ln_sb, ln_sb, -0.5), inc=True)
            wt("scalar", "c", "vector", v_l2)
            a_rstd = ins("scalar", lambda E: E.activation(
                out=rstd_sb, in_=ln_sb, func=AF.Exp), inc=True)
            wt("tensor", "c", "scalar", a_rstd)
            mm_first(5)
            t_br = ins("tensor", lambda E: E.matmul(
                ps[5][:, 0:TS], ones_c[:, :], rstd_sb, start=True, stop=True), inc=True)
            mm_last(5, t_br)
            wt("vector", "c", "tensor", t_br)
            v_app = []
            for dc in range(KC):
                wt_m("vector", marker(dc))
                ins("vector", lambda E, dc=dc: E.tensor_sub(
                    lntmp[:, :], x_sb[:, dc * TS:(dc + 1) * TS], ps[4][:, 0:TS]))
                v_app.append(ins("vector", lambda E, dc=dc: E.tensor_mul(
                    xln[:, dc * TS:(dc + 1) * TS], lntmp[:, :], ps[5][:, 0:TS]), inc=True))
            pbank[4]["r"]["vector"] = v_app[-1]
            pbank[5]["r"]["vector"] = v_app[-1]
            return v_app

        a_silu = {}
        v_h = {}

        def ffn_phase(v_xln, t_prev=None):
            fi = ffn_i[0]
            ffn_i[0] += 1
            t_enter = cnt["tensor"]
            t_load = t_enter if t_prev is None else t_prev
            v_enter = cnt["vector"]
            d_g = {}
            d_u = {}

            def emit_gu_load(it):
                b = it % 3
                if it >= 3:
                    wt("sync", "c", "tensor", t_gu[(fi, it - 3)])
                else:
                    wt("sync", "c", "tensor", t_load)
                wt_cc("sync", GOFF + it * 2048, 2048)
                wt_cc("sync", UOFF + it * 2048, 2048)
                d_g[it] = load_image(
                    lambda off, take, b=b: wg[b][:, off:off + take],
                    GOFF + it * 2048, 2048)
                d_u[it] = load_image(
                    lambda off, take, b=b: wu[b][:, off:off + take],
                    UOFF + it * 2048, 2048)

            d_d = {}

            def emit_d_load(dt):
                b = dt % 2
                if dt >= 2:
                    wt("sync", "c", "tensor", t_dn[(fi, dt - 2)])
                elif dt == 1:
                    wt("sync", "c", "tensor", t_gu[(fi, IT - 2)])
                else:
                    wt("sync", "c", "tensor", t_load)
                wt_cc("sync", DOFF + dt * INTER, INTER)
                d_d[dt] = load_image(
                    lambda off, take, b=b: wd[b][:, off:off + take],
                    DOFF + dt * INTER, INTER)

            emit_gu_load(0)
            emit_gu_load(1)
            emit_gu_load(2)
            for it in range(IT):
                b = it % 3
                b2 = it % 2
                bg, bu = 2 * b2, 2 * b2 + 1
                wt_d("tensor", d_g[it] + d_u[it])
                mm_first(bg)
                tg = None
                for kc in range(KC):
                    wt("tensor", "c", "vector", v_xln[kc])
                    last = kc == KC - 1
                    tg = ins("tensor", lambda E, kc=kc, b=b, bg=bg,
                             first=(kc == 0), last=last: E.matmul(
                        ps[bg][:, 0:TS], wg[b][:, kc * 128:(kc + 1) * 128],
                        xln[:, kc * TS:(kc + 1) * TS],
                        start=first, stop=last), inc=last)
                mm_last(bg, tg)
                mm_first(bu)
                tu = None
                for kc in range(KC):
                    last = kc == KC - 1
                    tu = ins("tensor", lambda E, kc=kc, b=b, bu=bu,
                             first=(kc == 0), last=last: E.matmul(
                        ps[bu][:, 0:TS], wu[b][:, kc * 128:(kc + 1) * 128],
                        xln[:, kc * TS:(kc + 1) * TS],
                        start=first, stop=last), inc=last)
                mm_last(bu, tu)
                t_gu[(fi, it)] = tu
                if it >= 2:
                    wt("scalar", "c", "vector", v_h[(fi, it - 2)])
                else:
                    wt("scalar", "c", "vector", v_enter)
                a_silu[(fi, it)] = prd(bg, "scalar", lambda E, b2=b2, bg=bg: E.activation(
                    out=sg[:, b2 * TS:(b2 + 1) * TS], in_=ps[bg][:, 0:TS], func=AF.Silu))
                wt("vector", "c", "scalar", a_silu[(fi, it)])
                if fi == 0:
                    wt("vector", "c", "scalar", cast_all_done)
                v_h[(fi, it)] = prd(bu, "vector", lambda E, it=it, b2=b2, bu=bu: E.tensor_mul(
                    h_sl[:, it * TS:(it + 1) * TS], sg[:, b2 * TS:(b2 + 1) * TS],
                    ps[bu][:, 0:TS]))
                if it + 3 < IT:
                    emit_gu_load(it + 3)
                if it == 6:
                    emit_d_load(0)
            emit_d_load(1)
            for dt in range(KC):
                b = dt % 2
                bank = dt % 2
                wt_d("tensor", d_d[dt])
                wt("tensor", "c", "vector", v_h[(fi, IT - 1)])
                mm_first(bank)
                td = None
                for ic in range(IT):
                    last = ic == IT - 1
                    td = ins("tensor", lambda E, ic=ic, b=b, bank=bank,
                             first=(ic == 0), last=last: E.matmul(
                        ps[bank][:, 0:TS], wd[b][:, ic * 128:(ic + 1) * 128],
                        h_sl[:, ic * TS:(ic + 1) * TS],
                        start=first, stop=last), inc=last)
                mm_last(bank, td)
                t_dn[(fi, dt)] = td
                v_dn[(fi, dt)] = prd(bank, "vector", lambda E, dt=dt, bank=bank:
                                     E.scalar_tensor_tensor(
                    out=x_sb[:, dt * TS:(dt + 1) * TS],
                    in0=ps[bank][:, 0:TS], scalar=0.5,
                    in1=x_sb[:, dt * TS:(dt + 1) * TS],
                    op0=ALU.mult, op1=ALU.add))
                if dt + 2 < KC:
                    emit_d_load(dt + 2)
            return fi

        def attn_phase(s, v_xln, t_prev=None):
            t_enter = cnt["tensor"]
            t_load = t_enter if t_prev is None else t_prev
            v_enter = cnt["vector"]
            t_qk = {}
            a_qk = {}
            d_qk = {}
            qk_srcs = [(sc_q, q_fm, hh) for hh in range(NH)] + \
                      [(sc_k, k_fm, hh) for hh in range(NH)]

            def emit_qk_load(i):
                scd, dstv, hh = qk_srcs[i]
                b = i % 2
                if i >= 2:
                    wt("sync", "c", "tensor", t_qk[i - 2])
                else:
                    wt("sync", "c", "tensor", t_load)
                wt_d("sync", cast_toks["q"] + cast_toks["k"])
                d_qk[i] = dma("sync", qkw[b][:, :], scd[hh, :, :])

            v_u_last = 0
            t_vb_last = {}
            a_sig_first = [True]
            d_vb = {}
            vb4 = [vbw[0][:, 0:4096], vbw[0][:, 4096:8192],
                   vbw[1][:, 0:4096], vbw[1][:, 4096:8192]]

            def emit_vb_load(sl8):
                if sl8 >= 2:
                    wt("sync", "c", "tensor", t_vb_last[sl8 - 2])
                    wt("gpsimd", "c", "tensor", t_vb_last[sl8 - 2])
                else:
                    wt("sync", "c", "tensor", t_load)
                    wt("gpsimd", "c", "tensor", t_load)
                    wt_d("sync", cast_toks["v"])
                    wt_d("gpsimd", cast_toks["b"])
                bq = (sl8 % 2) * 2
                d_vb[sl8] = [dma("sync", vb4[bq], sc_v[sl8]),
                             dma("gpsimd", vb4[bq + 1], sc_b[sl8])]

            emit_qk_load(0)
            emit_qk_load(1)
            emit_vb_load(0)
            emit_vb_load(1)
            for i in range(2 * NH):
                scd, dstv, hh = qk_srcs[i]
                b = i % 2
                bank = i % 2
                wt_d("tensor", d_qk[i])
                mm_first(bank)
                tl = None
                for kc in range(KC):
                    wt("tensor", "c", "vector", v_xln[kc])
                    last = kc == KC - 1
                    tl = ins("tensor", lambda E, kc=kc, b=b, bank=bank,
                             first=(kc == 0), last=last: E.matmul(
                        ps[bank][:, 0:TS], qkw[b][:, kc * 128:(kc + 1) * 128],
                        xln[:, kc * TS:(kc + 1) * TS],
                        start=first, stop=last), inc=last)
                mm_last(bank, tl)
                t_qk[i] = tl
                a_qk[i] = prd(bank, "scalar", lambda E, hh=hh, bank=bank, dstv=dstv:
                              E.copy(out=dstv[:, hh * TS:(hh + 1) * TS],
                                     in_=ps[bank][:, 0:TS]))
                if i + 2 < 2 * NH:
                    emit_qk_load(i + 2)
            for sl8 in range(8):
                bq = (sl8 % 2) * 2
                for tt in range(NCH):
                    bv, bb = 2 * (tt % 2), 2 * (tt % 2) + 1
                    wt_d("tensor", d_vb[sl8])
                    mm_first(bv)
                    tv = None
                    for kc in range(KC):
                        last = kc == KC - 1
                        tv = ins("tensor", lambda E, kc=kc, tt=tt, bv=bv, bq=bq,
                                 first=(kc == 0), last=last: E.matmul(
                            ps[bv][:, 0:256],
                            xln[:, kc * TS + tt * 128:kc * TS + (tt + 1) * 128],
                            vb4[bq][:, kc * 256:(kc + 1) * 256],
                            start=first, stop=last), inc=last)
                    mm_last(bv, tv)
                    mm_first(bb)
                    tb = None
                    for kc in range(KC):
                        last = kc == KC - 1
                        tb = ins("tensor", lambda E, kc=kc, tt=tt, bb=bb, bq=bq,
                                 first=(kc == 0), last=last: E.matmul(
                            ps[bb][:, 0:256],
                            xln[:, kc * TS + tt * 128:kc * TS + (tt + 1) * 128],
                            vb4[bq + 1][:, kc * 256:(kc + 1) * 256],
                            start=first, stop=last), inc=last)
                    mm_last(bb, tb)
                    t_vb_last[sl8] = tb
                    if a_sig_first[0]:
                        wt("scalar", "c", "vector", v_enter)
                        a_sig_first[0] = False
                    else:
                        wt("scalar", "c", "vector", v_u_last)
                    a_sig = prd(bb, "scalar", lambda E, bb=bb: E.activation(
                        out=sg[:, 0:256], in_=ps[bb][:, 0:256], func=AF.Sigmoid))
                    wt("vector", "c", "scalar", a_sig)
                    v_u_last = prd(bv, "vector", lambda E, sl8=sl8, tt=tt, bv=bv:
                                   E.tensor_mul(
                        u_tm[:, tt * D + sl8 * 256:tt * D + (sl8 + 1) * 256],
                        sg[:, 0:256], ps[bv][:, 0:256]))
                if sl8 + 2 < 8:
                    emit_vb_load(sl8 + 2)
            for bk in (0, 1, 2, 3, 4, 5, "t0", "t1"):
                for oe, v in pbank[bk]["r"].items():
                    wt("tensor", "c", oe, v)
                pbank[bk]["r"] = {}

            def head_tail(gc, c, hh):
                bo = 2 if hh % 2 == 0 else 5
                wt("tensor", "c", "vector", v_att[(gc, hh)])
                if (gc, hh - 2) in a_oev:
                    wt("tensor", "c", "scalar", a_oev[(gc, hh - 2)])
                elif (gc - 1, NH - 2 + (hh % 2)) in a_oev:
                    wt("tensor", "c", "scalar", a_oev[(gc - 1, NH - 2 + (hh % 2))])
                if (gc - 1, hh) in v_scast:
                    wt("tensor", "c", "vector", v_scast[(gc - 1, hh)])
                ins("tensor", lambda E, c=c, hh=hh, bo=bo: E.matmul(
                    ps[bo][:, 0:128],
                    u_tm[:, c * D + hh * HD:c * D + (hh + 1) * HD],
                    attm[:, (hh % 2) * 128:(hh % 2 + 1) * 128],
                    start=True, stop=False))
                t_o[(gc, hh)] = ins("tensor", lambda E, c=c, hh=hh, bo=bo: E.matmul(
                    ps[bo][:, 0:128],
                    S_bf[:, hh * HD:(hh + 1) * HD],
                    q_fm[:, hh * TS + c * 128:hh * TS + (c + 1) * 128],
                    start=False, stop=True), inc=True)
                bs = 3 + (hh % 2)
                wt("tensor", "c", "scalar", a_ktm[(gc, hh)])
                if (gc, hh - 2) in v_scast:
                    wt("tensor", "c", "vector", v_scast[(gc, hh - 2)])
                elif c > 0:
                    wt("tensor", "c", "vector", v_scast[(gc - 1, NH - 2 + (hh % 2))])
                t_sup[(gc, hh)] = ins("tensor", lambda E, c=c, hh=hh, bs=bs: E.matmul(
                    ps[bs][:, 0:128],
                    ktm[:, (hh % 2) * 128:(hh % 2 + 1) * 128],
                    u_tm[:, c * D + hh * HD:c * D + (hh + 1) * HD],
                    start=True, stop=True), inc=True)
                wt("scalar", "c", "tensor", t_o[(gc, hh)])
                a_oev[(gc, hh)] = ins("scalar", lambda E, c=c, hh=hh, bo=bo: E.copy(
                    out=o_fm[:, hh * TS + c * 128:hh * TS + (c + 1) * 128],
                    in_=ps[bo][:, 0:128]), inc=True)
                wt("vector", "c", "tensor", t_sup[(gc, hh)])
                ins("vector", lambda E, hh=hh, bs=bs: E.tensor_add(
                    S_f32[:, hh * HD:(hh + 1) * HD],
                    S_f32[:, hh * HD:(hh + 1) * HD],
                    ps[bs][:, 0:128]))
                v_scast[(gc, hh)] = ins("vector", lambda E, hh=hh: E.tensor_copy(
                    out=S_bf[:, hh * HD:(hh + 1) * HD],
                    in_=S_f32[:, hh * HD:(hh + 1) * HD]), inc=True)

            for c in range(NCH):
                gc = s * NCH + c
                for hh in range(NH):
                    slot = hh % 2
                    ba = hh % 2
                    if (gc, hh - 2) in a_ktm:
                        wt("tensor", "c", "scalar", a_ktm[(gc, hh - 2)])
                    elif (gc - 1, NH - 2 + slot) in a_ktm:
                        wt("tensor", "c", "scalar", a_ktm[(gc - 1, NH - 2 + slot)])
                    wt("tensor", "c", "scalar", a_qk[NH + hh])
                    wt("tensor", "c", "vector", v_u_last)
                    t_ktp = ins("tensor", lambda E, c=c, hh=hh, slot=slot: E.transpose(
                        pst[slot][:, 0:128],
                        k_fm[:, hh * TS + c * 128:hh * TS + (c + 1) * 128],
                        identb[:, :]), inc=True)
                    if (gc, hh - 2) in v_att:
                        wt("tensor", "c", "vector", v_att[(gc, hh - 2)])
                    elif c > 0:
                        wt("tensor", "c", "vector", v_att[(gc - 1, NH - 2 + ba)])
                    wt("tensor", "c", "scalar", a_qk[hh])
                    t_att = ins("tensor", lambda E, c=c, hh=hh, ba=ba: E.matmul(
                        ps[ba][:, 0:128],
                        k_fm[:, hh * TS + c * 128:hh * TS + (c + 1) * 128],
                        q_fm[:, hh * TS + c * 128:hh * TS + (c + 1) * 128],
                        start=True, stop=True), inc=True)
                    wt("scalar", "c", "tensor", t_ktp)
                    if (gc, hh - 2) in t_sup:
                        wt("scalar", "c", "tensor", t_sup[(gc, hh - 2)])
                    elif (gc - 1, NH - 2 + slot) in t_sup:
                        wt("scalar", "c", "tensor", t_sup[(gc - 1, NH - 2 + slot)])
                    a_ktm[(gc, hh)] = ins("scalar", lambda E, slot=slot: E.copy(
                        out=ktm[:, slot * 128:(slot + 1) * 128],
                        in_=pst[slot][:, 0:128]), inc=True)
                    wt("vector", "c", "tensor", t_att)
                    if (gc, hh - 2) in t_o:
                        wt("vector", "c", "tensor", t_o[(gc, hh - 2)])
                    elif (gc - 1, NH - 2 + ba) in t_o:
                        wt("vector", "c", "tensor", t_o[(gc - 1, NH - 2 + ba)])
                    v_att[(gc, hh)] = ins("vector", lambda E, ba=ba, slot=slot: E.tensor_mul(
                        attm[:, slot * 128:(slot + 1) * 128],
                        ps[ba][:, 0:128], maskc[:, :]), inc=True)
                    if hh > 0:
                        head_tail(gc, c, hh - 1)
                head_tail(gc, c, NH - 1)
            for bk in (0, 1, 2, 3, 4, 5):
                pbank[bk]["w"] = ("tensor", cnt["tensor"])
            pbank[0]["r"]["vector"] = cnt["vector"]
            pbank[1]["r"]["vector"] = cnt["vector"]
            pbank[2]["r"]["scalar"] = cnt["scalar"]
            pbank[5]["r"]["scalar"] = cnt["scalar"]
            pbank[3]["r"]["vector"] = cnt["vector"]
            pbank[4]["r"]["vector"] = cnt["vector"]
            pbank["t0"]["r"]["scalar"] = cnt["scalar"]
            pbank["t1"]["r"]["scalar"] = cnt["scalar"]

            gcl = s * NCH + NCH - 1
            t_wo = {}
            d_wo = {}

            def emit_wo_load(dt):
                b = dt % 2
                if dt >= 2:
                    wt("sync", "c", "tensor", t_wo[dt - 2])
                else:
                    wt("sync", "c", "tensor", t_qk[2 * NH - 2 + b])
                    wt_d("sync", cast_toks["o"])
                d_wo[dt] = dma("sync", qkw[b][:, :], sc_o[dt, :, :])

            emit_wo_load(0)
            emit_wo_load(1)
            for dt in range(KC):
                b = dt % 2
                bank = dt % 2
                wt_d("tensor", d_wo[dt])
                wt("tensor", "c", "scalar", a_oev[(gcl, NH - 1)])
                mm_first(bank)
                tw = None
                for kc in range(KC):
                    last = kc == KC - 1
                    tw = ins("tensor", lambda E, kc=kc, b=b, bank=bank,
                             first=(kc == 0), last=last: E.matmul(
                        ps[bank][:, 0:TS], qkw[b][:, kc * 128:(kc + 1) * 128],
                        o_fm[:, kc * TS:(kc + 1) * TS],
                        start=first, stop=last), inc=last)
                mm_last(bank, tw)
                t_wo[dt] = tw
                v_wo[(s, dt)] = prd(bank, "vector", lambda E, dt=dt, bank=bank:
                                    E.tensor_add(
                    x_sb[:, dt * TS:(dt + 1) * TS],
                    x_sb[:, dt * TS:(dt + 1) * TS],
                    ps[bank][:, 0:TS]))
                if dt + 2 < KC:
                    emit_wo_load(dt + 2)

        for s in range(NSLAB):
            for dc in range(KC):
                if s > 0:
                    wt_d("sync", st_x[(s - 1, dc)])
                d_x[(s, dc)] = dma("sync", x_sb[:, dc * TS:(dc + 1) * TS],
                                   x_in[dc * 128:(dc + 1) * 128, s * TS:(s + 1) * TS])

            tp1 = cnt["tensor"]
            v_xln1 = ln_phase(lambda dc, s=s: ("d", d_x[(s, dc)]))
            fi1 = ffn_phase(v_xln1, tp1)
            tp2 = cnt["tensor"]
            v_xln2 = ln_phase(lambda dc, fi=fi1: ("c", "vector", v_dn[(fi, dc)]))
            attn_phase(s, v_xln2, tp2)
            tp3 = cnt["tensor"]
            v_xln3 = ln_phase(lambda dc, s=s: ("c", "vector", v_wo[(s, dc)]))
            fi2 = ffn_phase(v_xln3, tp3)

            for dc in range(KC):
                wt("gpsimd", "c", "vector", v_dn[(fi2, dc)])
                st_x[(s, dc)] = dma("gpsimd",
                                    y_out[dc * 128:(dc + 1) * 128, s * TS:(s + 1) * TS],
                                    x_sb[:, dc * TS:(dc + 1) * TS])

        for e in DQUEUES:
            for i in range(NP):
                if dpc[e][i]:
                    q[e].append(lambda E, h=dpool[e][i], v=dpc[e][i]: E.wait_ge(h, v))

        block = es.enter_context(nc.Block())

        @block.sync
        def _(E):
            for f in q["sync"]:
                f(E)

        @block.gpsimd
        def _(E):
            for f in q["gpsimd"]:
                f(E)

        @block.vector
        def _(E):
            for f in q["vector"]:
                f(E)

        @block.scalar
        def _(E):
            for f in q["scalar"]:
                f(E)

        @block.tensor
        def _(E):
            for f in q["tensor"]:
                f(E)

    return nc


_CACHE = {}


def kernel(L0_attn_hidden_states, L0_attn_Wq, L0_attn_Wk, L0_attn_Wv,
           L0_attn_Wb, L0_attn_Wo, L0_ffn_lut, L0_ffn_gate_idx,
           L0_ffn_up_idx, L0_ffn_down_idx):
    x = np.asarray(L0_attn_hidden_states, dtype=np.float32)
    gidx = np.asarray(L0_ffn_gate_idx, dtype=np.int32)
    uidx = np.asarray(L0_ffn_up_idx, dtype=np.int32)
    didx = np.asarray(L0_ffn_down_idx, dtype=np.int32)
    gb = gidx.reshape(IT, 128, KC, 128).transpose(0, 2, 1, 3).reshape(-1, 128, 128)
    ub = uidx.reshape(IT, 128, KC, 128).transpose(0, 2, 1, 3).reshape(-1, 128, 128)
    db = didx.reshape(KC, 128, IT, 128).transpose(0, 2, 1, 3).reshape(-1, 128, 128)
    blocks = np.concatenate([gb, ub, db], axis=0)
    base = {
        "wq": np.ascontiguousarray(np.asarray(L0_attn_Wq, dtype=np.float32)),
        "wk": np.ascontiguousarray(np.asarray(L0_attn_Wk, dtype=np.float32)),
        "wv": np.ascontiguousarray(np.asarray(L0_attn_Wv, dtype=np.float32)),
        "wb": np.ascontiguousarray(np.asarray(L0_attn_Wb, dtype=np.float32)),
        "wo": np.ascontiguousarray(np.asarray(L0_attn_Wo, dtype=np.float32)),
        "lut": np.ascontiguousarray(np.asarray(L0_ffn_lut, dtype=np.float32).reshape(1, 16)),
    }
    if "nc" not in _CACHE:
        _CACHE["nc"] = build()
    nc = _CACHE["nc"]
    in_maps = []
    for c in range(8):
        m = dict(base)
        m["x_in"] = np.ascontiguousarray(x[c].T)
        own = np.concatenate([blocks[512 * 0 + c * 64:512 * 0 + (c + 1) * 64],
                              blocks[512 * 1 + c * 64:512 * 1 + (c + 1) * 64],
                              blocks[512 * 2 + c * 64:512 * 2 + (c + 1) * 64],
                              blocks[1536 + c * 66:1536 + (c + 1) * 66]], axis=0)
        m["myidx"] = np.ascontiguousarray(
            own.transpose(1, 0, 2).reshape(128, 33024))
        in_maps.append(m)
    res = run_bass_kernel_spmd(nc, in_maps, core_ids=list(range(8)))
    out = np.stack([np.asarray(res.results[c]["y"]).T for c in range(8)], axis=0)
    return np.ascontiguousarray(out.astype(np.float32))
